# version 05
# baseline (speedup 1.0000x reference)
import contextlib
import numpy as np
import ml_dtypes
import concourse.bass as bass
import concourse.mybir as mybir
from concourse.bass_utils import run_bass_kernel_spmd

F32 = mybir.dt.float32
BF16 = mybir.dt.bfloat16
BF = ml_dtypes.bfloat16
AF = mybir.ActivationFunctionType
ALU = mybir.AluOpType
AX = mybir.AxisListType

NCORE = 8
TOK = 2048
DM = 2048
HID = 8192
NS = 16
NHG = 16
NK = 16
NJC = 64
NQ = 4
INV_D = 1.0 / DM
EPS = 1e-5
LADK = (128.0, 64.0, 32.0, 16.0, 8.0, 4.0, 2.0, 1.0)

O_COLS = 0
O_DP1 = 128
O_UB = 2176
O_UT = 4224
O_RES = 6272
O_X1 = 8320
O_LC = 10368
O_LY = 12416
O_GS = 14464
O_WL = 22656
O_PQ = 26752
O_BR = 28800
A_W = 29824
B_XI = 0
B_GWA = 32768
B_SG = 40960
B_OQ = 43008
B_W = 45056

(C_EPS, C_HALF, C_T0, C_T1, C_T2, C_T3, C_SCG, C_SCV, C_SCO,
 C_ISG, C_ISV, C_RA0, C_RA1, C_RA2, C_SA0, C_SA1, C_SA2, C_SB0, C_SB1,
 C_SB2, C_K0) = range(21)
C_MAC = 32
C_SXS = 48
C_SGC = 64
C_SVC = 80
C_S2 = 96
C_GXI = 112
NCOL = 128


class Prog:
    ENGS = ("sync", "tensor", "vector", "scalar", "gpsimd")

    def __init__(self):
        self.ops = {e: [] for e in self.ENGS}
        self.cnt = {}
        self.sems = {}

    def w(self, eng, sname, val):
        if val <= 0:
            return
        self.ops[eng].append(lambda e, s=self.sems[sname], v=val: e.wait_ge(s, v))

    def do(self, eng, fn):
        self.ops[eng].append(fn)

    def inc(self, eng, sname, fn, amt=1):
        self.cnt[sname] = self.cnt.get(sname, 0) + amt
        self.ops[eng].append(
            lambda e, f=fn, s=self.sems[sname], a=amt: f(e).then_inc(s, a))
        return self.cnt[sname]

    def val(self, sname):
        return self.cnt.get(sname, 0)


def build_program(debug=False):
    nc = bass.Bass("TRN2", target_bir_lowering=False)

    u_h = nc.dram_tensor("u", [TOK, DM], F32, kind="ExternalInput").ap()
    ut_h = nc.dram_tensor("ut", [DM, TOK], F32, kind="ExternalInput").ap()
    gwt_h = nc.dram_tensor("gwt", [DM, HID], F32, kind="ExternalInput").ap()
    vwt_h = nc.dram_tensor("vwt", [DM, HID], F32, kind="ExternalInput").ap()
    owt_h = nc.dram_tensor("owt", [HID, DM], F32, kind="ExternalInput").ap()
    gbr_h = nc.dram_tensor("gbr", [1, HID], F32, kind="ExternalInput").ap()
    vbr_h = nc.dram_tensor("vbr", [1, HID], F32, kind="ExternalInput").ap()
    obr_h = nc.dram_tensor("obr", [1, DM], F32, kind="ExternalInput").ap()
    wbr_h = nc.dram_tensor("wbr", [DM, 64], F32, kind="ExternalInput").ap()
    wbs_h = nc.dram_tensor("wbs", [64, 256], F32, kind="ExternalInput").ap()
    wcr_h = nc.dram_tensor("wcr", [256, 64], F32, kind="ExternalInput").ap()
    wcm_h = nc.dram_tensor("wcm", [64, DM], F32, kind="ExternalInput").ap()
    a_h = nc.dram_tensor("afac", [1, 32], F32, kind="ExternalInput").ap()
    d_h = nc.dram_tensor("dvec", [1, DM], F32, kind="ExternalInput").ap()
    idf_h = nc.dram_tensor("idf", [128, 128], F32, kind="ExternalInput").ap()
    idb_h = nc.dram_tensor("idb", [128, 128], BF16, kind="ExternalInput").ap()
    out_h = nc.dram_tensor("out", [TOK, DM], F32, kind="ExternalOutput").ap()
    scr_kind = "ExternalOutput" if debug else "Internal"
    x1_scr = nc.dram_tensor("x1_scr", [NS, 128, DM], F32, kind=scr_kind).ap()
    h_scr = nc.dram_tensor("h_scr", [NS, 128, HID], F32, kind=scr_kind).ap()
    oq_scr = nc.dram_tensor("oq_scr", [NJC, 128, DM], BF16, kind=scr_kind).ap()

    ctx = contextlib.ExitStack()
    A = ctx.enter_context(nc.sbuf_tensor("arena_f32", [128, A_W], F32))
    B = ctx.enter_context(nc.sbuf_tensor("arena_bf16", [128, B_W], BF16))

    def af(off, w, p0=0, p1=128):
        return A[p0:p1, off:off + w]

    def ab(off, w, p0=0, p1=128):
        return B[p0:p1, off:off + w]

    idf = ctx.enter_context(nc.sbuf_tensor("idf_s", [128, 128], F32))
    idb = ctx.enter_context(nc.sbuf_tensor("idb_s", [128, 128], BF16))
    ones1 = ctx.enter_context(nc.sbuf_tensor("ones1", [1, 128], F32))
    onesc = ctx.enter_context(nc.sbuf_tensor("onesc", [128, 1], F32))
    gxs = ctx.enter_context(nc.sbuf_tensor("gxs", [128, NS], F32))
    g11 = ctx.enter_context(nc.sbuf_tensor("g11", [1, 8], F32))

    p_a = ctx.enter_context(nc.psum_tensor("p_a", [128, 2048], F32))
    p_b = ctx.enter_context(nc.psum_tensor("p_b", [128, 1024], F32))
    p_tr = ctx.enter_context(nc.psum_tensor("p_tr", [128, 2048], BF16))

    P = Prog()
    semnames = ["PE", "DV", "AC", "GP", "L0",
                "LU0", "LU1", "LT0", "LT1", "LA0", "LA1", "LW0", "LW1",
                "LQ0", "LQ1", "LH0", "LH1", "LB0", "LB1", "LX0", "LX1",
                "LR0", "LR1",
                "SX0", "SX1", "SH0", "SH1", "SQ0", "SQ1", "SO0", "SO1"]
    for nm in semnames:
        P.sems[nm] = ctx.enter_context(nc.semaphore("s" + nm))

    cols = af(O_COLS, NCOL)

    def col(i):
        return A[:, O_COLS + i:O_COLS + i + 1]

    def ccol(base, i):
        return A[:, O_COLS + base + i:O_COLS + base + i + 1]

    M_ = O_WL
    wbrT = af(M_, 1024)
    wcm_f = af(M_ + 1024, DM, 0, 64)
    G_f = af(M_ + 3072, 64, 0, 64)
    t1b = af(M_ + 3136, 128, 0, 64)
    t2b = af(M_ + 3264, 128, 0, 64)
    wbs_t = af(O_PQ, 256, 0, 64)
    bb0 = af(O_PQ + 256, 256, 0, 64)
    bb1 = af(O_PQ + 512, 256, 0, 64)
    tmp256 = af(O_PQ + 768, 256, 0, 64)
    a_bc = af(O_PQ + 1024, 32, 0, 64)
    a_row = af(O_PQ + 1056, 32, 0, 1)
    btT = af(O_PQ + 1088, 128)
    wcr0 = af(O_PQ + 1216, 64)
    wcr1 = af(O_PQ + 1280, 64)
    d_row = af(O_RES, DM, 0, 1)
    dp1 = af(O_DP1, DM)

    ld0 = [(idf[:, :], idf_h[:, :]), (idb[:, :], idb_h[:, :]),
           (d_row, d_h[:, :]), (wbs_t, wbs_h[:, :]), (a_row, a_h[:, :]),
           (wcm_f, wcm_h[:, :]),
           (wbrT.rearrange("p (k r) -> p k r", r=64),
            wbr_h.rearrange("(k p) r -> p k r", p=128)),
           (wcr0, wcr_h[0:128, :]), (wcr1, wcr_h[128:256, :])]
    for dst, src in ld0:
        P.inc("sync", "L0", lambda e, d=dst, s=src: e.dma_start(d, s), 16)
    ld0_done = P.val("L0")

    P.do("vector", lambda e: e.memset(col(C_EPS), float(EPS)))
    P.do("vector", lambda e: e.memset(col(C_HALF), 0.5))
    for j, kk in enumerate(LADK):
        P.do("vector", lambda e, j=j, kk=kk: e.memset(col(C_K0 + j), -kk / 2.0))
    P.do("vector", lambda e: e.memset(ones1[:1, :], 1.0))
    P.do("vector", lambda e: e.memset(onesc[:, :], 1.0))
    P.do("vector", lambda e: e.memset(cols[:, C_RA0:C_RA0 + 3], 0.0))
    P.w("vector", "L0", ld0_done)
    dv_c0 = P.inc("vector", "DV", lambda e: (e.drain(), e.engine_nop())[1])

    P.w("tensor", "DV", dv_c0)
    pe_abc = P.inc("tensor", "PE", lambda e: e.matmul(
        p_b[0:128, 0:32], ones1[:1, 0:128], a_row, start=True, stop=True))
    P.w("vector", "PE", pe_abc)
    P.do("vector", lambda e: e.tensor_copy(a_bc, p_b[0:64, 0:32]))
    P.do("vector", lambda e: e.drain())
    P.do("vector", lambda e: e.tensor_copy(bb0, wbs_t))
    cur, nxt = bb0, bb1
    for lvl in range(8):
        wwid = 2 ** (7 - lvl)
        vi = cur.rearrange("p (b t w) -> p b t w", t=2, w=wwid)
        vo = nxt.rearrange("p (b t w) -> p b t w", t=2, w=wwid)
        vt = tmp256.rearrange("p (b t w) -> p b t w", t=2, w=wwid)
        for j in range(2):
            a0 = a_bc[:, lvl * 4 + j: lvl * 4 + j + 1]
            a1 = a_bc[:, lvl * 4 + 2 + j: lvl * 4 + 3 + j]
            P.do("vector", lambda e: e.drain())
            P.do("vector", lambda e, o=vt[:, :, j, :], i=vi[:, :, 0, :], s=a0:
                 e.tensor_single_scalar(o, i, s, ALU.mult))
            P.do("vector", lambda e: e.drain())
            P.do("vector", lambda e, o=vo[:, :, j, :], i=vi[:, :, 1, :], s=a1,
                 t=vt[:, :, j, :]:
                 e.scalar_tensor_tensor(o, i, s, t, ALU.mult, ALU.add))
        cur, nxt = nxt, cur
    dv_bfly = P.inc("vector", "DV", lambda e: (e.drain(), e.engine_nop())[1])

    P.w("tensor", "DV", dv_bfly)
    P.do("tensor", lambda e, i=cur: e.transpose(p_b[:, 0:64], i[:, 0:128],
                                                idf[0:64, 0:64]))
    pe_btt = P.inc("tensor", "PE", lambda e, i=cur: e.transpose(
        p_b[:, 64:128], i[:, 128:256], idf[0:64, 0:64]))
    P.w("vector", "PE", pe_btt)
    dv_btt = P.inc("vector", "DV", lambda e: e.tensor_copy(btT, p_b[:, 0:128]))
    P.w("tensor", "DV", dv_btt)
    P.do("tensor", lambda e: e.matmul(p_b[0:64, 256:320], btT[0:128, 0:64],
                                      wcr0, start=True, stop=False))
    pe_g = P.inc("tensor", "PE", lambda e: e.matmul(
        p_b[0:64, 256:320], btT[0:128, 64:128], wcr1, start=False, stop=True))
    P.w("vector", "PE", pe_g)
    dv_g = P.inc("vector", "DV", lambda e: e.tensor_copy(G_f, p_b[0:64, 256:320]))

    P.w("tensor", "DV", dv_g)
    dv_dpe = {}
    for i in range(4):
        if i >= 2:
            P.w("tensor", "DV", dv_dpe[i - 2])
        pe_d = P.inc("tensor", "PE", lambda e, i=i: e.matmul(
            p_a[:, (i % 2) * 512:(i % 2) * 512 + 512], ones1[:1, :],
            d_row[:, i * 512:(i + 1) * 512], start=True, stop=True))
        P.w("vector", "PE", pe_d)
        dv_dpe[i] = P.inc("vector", "DV", lambda e, i=i: e.tensor_scalar(
            dp1[:, i * 512:(i + 1) * 512],
            p_a[:, (i % 2) * 512:(i % 2) * 512 + 512],
            1.0, 1.0, ALU.mult, ALU.add))
    dv_dp1 = P.inc("vector", "DV", lambda e: (e.drain(), e.engine_nop())[1])

    ubuf = af(O_UB, DM)
    utb = af(O_UT, DM)
    res = af(O_RES, DM)
    x1b = af(O_X1, DM)
    lC = af(O_LC, DM)
    lY = af(O_LY, DM)
    sgt = ab(B_SG, DM)
    xt_bf = ab(B_OQ, DM)
    xi_all = B[:, B_XI:B_XI + NS * DM].rearrange("p (k t) -> p k t", t=TOK)
    pA_w = [af(O_GS, DM), af(O_GS + 2048, DM)]

    pa_chunks = []
    for mi, wh in ((0, gwt_h), (1, vwt_h)):
        fl = wh.rearrange("(c p) x -> c p x", p=128)
        for c in range(16):
            for s4 in range(4):
                pa_chunks.append((fl[c, :, s4 * 2048:(s4 + 1) * 2048], mi))
    flo = owt_h.rearrange("(c p) x -> c p x", p=128)
    for c in range(64):
        pa_chunks.append((flo[c, :, :], 2))
    NPA = len(pa_chunks)
    per = (NPA + NS - 1) // NS
    pa_sched = [list(range(s * per, min((s + 1) * per, NPA))) for s in range(NS)]

    la_ac = {}
    gp_add = {}
    abs_scr = [ab(B_GWA, DM), ab(B_GWA + 2048, DM)]

    def emit_passA(idxs):
        for j in idxs:
            src, mi = pa_chunks[j]
            bp = j % 2
            if j >= 2:
                P.w("sync", "AC", la_ac[j - 2])
            la = P.inc("sync", "LA%d" % bp,
                       lambda e, d=pA_w[bp], s=src: e.dma_start(d, s), 16)
            P.w("scalar", "LA%d" % bp, la)
            if j >= 2:
                P.w("scalar", "GP", gp_add[j - 2])
            acc = ccol(0, (C_SA0 if j % 2 == 0 else C_SB0) + mi)
            la_ac[j] = P.inc("scalar", "AC", lambda e, bp=bp, acc=acc, j=j:
                             e.activation(abs_scr[j % 2], pA_w[bp], AF.Abs,
                                          accum_out=acc))
            P.w("gpsimd", "AC", la_ac[j])
            gp_add[j] = P.inc("gpsimd", "GP", lambda e, mi=mi, acc=acc:
                              e.tensor_tensor(ccol(0, C_RA0 + mi),
                                              ccol(0, C_RA0 + mi), acc,
                                              ALU.add))
            P.do("gpsimd", lambda e: e.drain())

    st_x1 = {}
    dv_xi_done = {}
    dv_ures_s = {}
    pe_t1_s = {}
    dv_resadd = {}
    for s in range(NS):
        bp = s % 2
        if s >= 1:
            P.w("sync", "DV", dv_ures_s[s - 1])
        lu = P.inc("sync", "LU%d" % bp, lambda e, s=s: e.dma_start(
            ubuf, u_h[s * 128:(s + 1) * 128, :]), 16)
        if s >= 1:
            P.w("sync", "PE", pe_t1_s[s - 1])
        lt = P.inc("sync", "LT%d" % bp, lambda e, s=s: e.dma_start(
            utb.rearrange("p (k t) -> p k t", t=128),
            ut_h.rearrange("(k p) t -> p k t", p=128)
            [:, :, s * 128:(s + 1) * 128]), 16)
        P.w("tensor", "LT%d" % bp, lt)
        if s == 0:
            P.w("tensor", "DV", dv_dp1)
        for k in range(NK):
            fn = lambda e, k=k: e.matmul(
                p_b[0:64, 0:128], wbrT[:, k * 64:(k + 1) * 64],
                utb[:, k * 128:(k + 1) * 128], start=(k == 0), stop=(k == 15))
            if k == 15:
                pe_t1 = P.inc("tensor", "PE", fn)
            else:
                P.do("tensor", fn)
        pe_t1_s[s] = pe_t1
        P.w("vector", "PE", pe_t1)
        dv_t1 = P.inc("vector", "DV", lambda e: e.tensor_copy(
            t1b, p_b[0:64, 0:128]))
        P.w("tensor", "DV", dv_t1)
        pe_t2 = P.inc("tensor", "PE", lambda e: e.matmul(
            p_b[0:64, 512:640], G_f, t1b[0:64, :], start=True, stop=True))
        P.w("vector", "PE", pe_t2)
        dv_t2 = P.inc("vector", "DV", lambda e: e.tensor_copy(
            t2b, p_b[0:64, 512:640]))
        P.w("tensor", "DV", dv_t2)
        pe_y = {}
        for g in range(4):
            if s >= 1:
                P.w("tensor", "DV", dv_resadd[(s - 1, g)])
            pe_y[g] = P.inc("tensor", "PE", lambda e, g=g: e.matmul(
                p_a[:, g * 512:(g + 1) * 512], t2b[0:64, :],
                wcm_f[:, g * 512:(g + 1) * 512], start=True, stop=True))
        P.w("vector", "LU%d" % bp, lu)
        P.do("vector", lambda e: e.drain())
        P.do("vector", lambda e: e.tensor_tensor(res, ubuf, dp1, ALU.mult))
        dv_ures_s[s] = P.inc("vector", "DV",
                             lambda e: (e.drain(), e.engine_nop())[1])
        for g in range(4):
            P.w("vector", "PE", pe_y[g])
            P.do("vector", lambda e: e.drain())
            dv_resadd[(s, g)] = P.inc("vector", "DV", lambda e, g=g:
                                      e.tensor_tensor(
                                          res[:, g * 512:(g + 1) * 512],
                                          res[:, g * 512:(g + 1) * 512],
                                          p_a[:, g * 512:(g + 1) * 512],
                                          ALU.add))
        mcol, vcol, rcol = col(C_T0), col(C_T1), col(C_T3)
        P.do("vector", lambda e: e.drain())
        P.do("vector", lambda e: e.tensor_reduce(mcol, res, AX.X, ALU.add))
        P.do("vector", lambda e: e.drain())
        P.do("vector", lambda e: e.tensor_single_scalar(mcol, mcol, INV_D,
                                                        ALU.mult))
        P.do("vector", lambda e: e.drain())
        dv_ln = P.inc("vector", "DV", lambda e: e.tensor_single_scalar(
            res, res, mcol, ALU.subtract))
        P.w("scalar", "DV", dv_ln)
        P.do("scalar", lambda e: e.activation(sgt, res, AF.Square,
                                              accum_out=vcol))
        P.do("scalar", lambda e: e.drain())
        P.do("scalar", lambda e: e.activation(col(C_T2), vcol, AF.Ln,
                                              bias=col(C_EPS), scale=INV_D))
        P.do("scalar", lambda e: e.drain())
        ac_ln = P.inc("scalar", "AC", lambda e: e.activation(
            rcol, col(C_T2), AF.Exp, scale=-0.5))
        P.w("vector", "AC", ac_ln)
        if s >= 1:
            P.w("vector", "SX%d" % ((s - 1) % 2), st_x1[s - 1])
        P.do("vector", lambda e: e.drain())
        dv_x1 = P.inc("vector", "DV", lambda e: e.tensor_single_scalar(
            x1b, res, rcol, ALU.mult))
        P.w("sync", "DV", dv_x1)
        st_x1[s] = P.inc("sync", "SX%d" % bp, lambda e, s=s: e.dma_start(
            x1_scr[s, :, :], x1b), 16)
        gxc = gxs[:, s:s + 1]
        sxc = ccol(C_SXS, s)
        P.do("vector", lambda e: e.drain())
        P.do("vector", lambda e, g=gxc: e.tensor_reduce(
            g, x1b, AX.X, ALU.max, apply_absolute_value=True))
        P.do("vector", lambda e: e.drain())
        P.do("vector", lambda e, g=gxc: e.tensor_single_scalar(
            g, g, 1e-5, ALU.max))
        P.do("vector", lambda e: e.drain())
        P.do("vector", lambda e, g=gxc, s=s: e.tensor_single_scalar(
            ccol(C_GXI, s), g, 1.0 / 127.0, ALU.mult))
        P.do("vector", lambda e: e.drain())
        P.do("vector", lambda e, sc=sxc, s=s: e.reciprocal(
            sc, ccol(C_GXI, s)))
        P.do("vector", lambda e: e.drain())
        dv_prev = P.inc("vector", "DV", lambda e, sc=sxc:
                        e.tensor_single_scalar(lY, x1b, sc, ALU.mult))
        for j in range(8):
            src_c = lY if j == 0 else lC
            P.w("scalar", "DV", dv_prev)
            ac_j = P.inc("scalar", "AC", lambda e, src_c=src_c: e.activation(
                sgt, src_c, AF.Sign, bias=col(C_HALF)))
            P.w("vector", "AC", ac_j)
            dv_prev = P.inc("vector", "DV", lambda e, j=j, src_c=src_c:
                            e.scalar_tensor_tensor(lC, sgt, col(C_K0 + j),
                                                   src_c, ALU.mult, ALU.add))
        P.do("vector", lambda e: e.drain())
        P.do("vector", lambda e: e.tensor_tensor(res, lY, lC, ALU.subtract))
        P.do("vector", lambda e: e.drain())
        if s >= 1:
            P.w("vector", "PE", pe_xt_s)
        dv_xt = P.inc("vector", "DV", lambda e: e.tensor_scalar(
            xt_bf, res, 1.0, -0.5, ALU.mult, ALU.add))
        P.w("tensor", "DV", dv_xt)
        if s >= 1:
            P.w("tensor", "DV", dv_xi_done[s - 1])
        for k in range(16):
            fn = lambda e, k=k: e.transpose(
                p_tr[:, k * 128:(k + 1) * 128], xt_bf[:, k * 128:(k + 1) * 128],
                idb[:, :])
            if k == 15:
                pe_xt_s = P.inc("tensor", "PE", fn)
            else:
                P.do("tensor", fn)
        P.w("vector", "PE", pe_xt_s)
        ptr_view = p_tr[:, :].rearrange("p (k t) -> p k t", t=128)
        dv_xi_done[s] = P.inc(
            "vector", "DV",
            lambda e, s=s, o=xi_all[:, :, s * 128:(s + 1) * 128], i=ptr_view:
            e.tensor_copy(o, i))
        emit_passA(pa_sched[s])

    pe_after_p1 = P.val("PE")

    P.w("tensor", "GP", P.val("GP"))
    pe_g3 = {}
    for mi in range(3):
        pe_g3[mi] = P.inc("tensor", "PE", lambda e, mi=mi: e.matmul(
            p_b[0:1, 640 + mi:641 + mi], ccol(0, C_RA0 + mi), onesc[:, :],
            start=True, stop=True))
    P.w("vector", "PE", pe_g3[2])
    P.do("vector", lambda e: e.tensor_scalar(
        g11[:1, 0:3], p_b[0:1, 640:643], 1.0 / 16777216.0, 1e-5,
        ALU.mult, ALU.max))
    dv_g11 = P.inc("vector", "DV", lambda e: (e.drain(), e.engine_nop())[1])
    P.w("tensor", "DV", dv_g11)
    pe_gb = P.inc("tensor", "PE", lambda e: e.matmul(
        p_b[:, 644:647], ones1[:1, :], g11[:1, 0:3], start=True, stop=True))
    P.w("vector", "PE", pe_gb)
    for gi, (cth, csc, cis) in enumerate(
            [(C_SCG, C_SCG, C_ISG), (C_SCV, C_SCV, C_ISV), (C_SCO, C_SCO, -1)]):
        P.do("vector", lambda e, gi=gi, csc=csc: e.tensor_single_scalar(
            col(csc), p_b[:, 644 + gi:645 + gi], 0.5, ALU.mult))
        P.do("vector", lambda e: e.drain())
        if cis >= 0:
            P.do("vector", lambda e, csc=csc, cis=cis: e.reciprocal(
                col(cis), col(csc)))
            P.do("vector", lambda e: e.drain())
    TH = {}
    for mi, csc in ((0, C_SCG), (1, C_SCV), (2, C_SCO)):
        cp = C_SA0 + mi
        cn = C_SB0 + mi
        P.do("vector", lambda e, cp=cp, csc=csc: e.tensor_copy(
            col(cp), col(csc)))
        P.do("vector", lambda e: e.drain())
        P.do("vector", lambda e, cp=cp, cn=cn: e.tensor_single_scalar(
            col(cn), col(cp), -1.0, ALU.mult))
        P.do("vector", lambda e: e.drain())
        TH[mi] = (cp, cn)
    P.do("vector", lambda e: e.tensor_single_scalar(
        cols[:, C_SGC:C_SGC + 16], cols[:, C_GXI:C_GXI + 16], col(C_SCG),
        ALU.mult))
    P.do("vector", lambda e: e.drain())
    P.do("vector", lambda e: e.tensor_single_scalar(
        cols[:, C_SVC:C_SVC + 16], cols[:, C_GXI:C_GXI + 16], col(C_SCV),
        ALU.mult))
    P.do("vector", lambda e: e.drain())
    rows_g = lambda s: af(O_RES + s * 128, 128, 0, 1)
    rows_v = lambda s: af(O_X1 + s * 128, 128, 0, 1)
    P.w("vector", "SX0", st_x1[NS - 2])
    P.w("vector", "SX1", st_x1[NS - 1])
    for mi in range(2):
        base = C_SGC if mi == 0 else C_SVC
        rows = rows_g if mi == 0 else rows_v
        for s in range(NS):
            P.do("vector", lambda e, base=base, s=s: e.reciprocal(
                col(C_T0), ccol(base, s)))
            P.do("vector", lambda e: e.drain())
            dv_iv = P.inc("vector", "DV", lambda e: (e.drain(),
                                                     e.engine_nop())[1])
            P.w("tensor", "DV", dv_iv)
            pe_tv = P.inc("tensor", "PE", lambda e: e.transpose(
                p_b[0:1, 648:776], col(C_T0), idf[:, :]))
            P.w("vector", "PE", pe_tv)
            P.do("vector", lambda e: e.drain())
            P.inc("vector", "DV", lambda e, ro=rows(s): e.tensor_copy(
                ro, p_b[0:1, 648:776]))
    dv_rows = P.inc("vector", "DV", lambda e: (e.drain(), e.engine_nop())[1])

    wlb = [af(O_WL, DM), af(O_WL + 2048, DM)]
    pqb = af(O_PQ, DM)
    gs_slab = af(O_GS, 8192)
    sgh = [ab(B_SG, 1024), ab(B_SG + 1024, 1024)]
    oqb = ab(B_OQ, DM)
    gwq = B[:, B_GWA:B_GWA + 8192]
    hbuf2 = [af(O_UB, 512), af(O_UB + 512, 512)]
    brb = [af(O_BR, 512, 0, 1), af(O_BR + 512, 512, 0, 1)]
    P.do("vector", lambda e: e.memset(cols[:, C_MAC:C_MAC + 16], 0.0))
    dv_mz = P.inc("vector", "DV", lambda e: (e.drain(), e.engine_nop())[1])

    pq_ac = {}
    pq_st = {}
    chain = {"dv": 0}

    def emit_prequant(jcs):
        for jc in jcs:
            if jc >= NJC:
                continue
            if jc >= 1:
                P.w("sync", "AC", pq_ac[jc - 1])
            lq = P.inc("sync", "LQ%d" % (jc % 2), lambda e, jc=jc:
                       e.dma_start(pqb, flo[jc, :, :]), 16)
            P.w("scalar", "LQ%d" % (jc % 2), lq)
            for hh in range(2):
                sl = slice(hh * 1024, (hh + 1) * 1024)
                if chain["dv"]:
                    P.w("scalar", "DV", chain["dv"])
                P.do("scalar", lambda e, sl=sl: e.activation(
                    sgh[0], pqb[:, sl], AF.Sign, bias=col(C_SB0 + 2)))
                acq = P.inc("scalar", "AC", lambda e, sl=sl: e.activation(
                    sgh[1], pqb[:, sl], AF.Sign, bias=col(C_SA0 + 2)))
                P.w("vector", "AC", acq)
                if jc >= 1 and hh == 0:
                    P.w("vector", "SQ%d" % ((jc - 1) % 2), pq_st[jc - 1])
                dvq = P.inc("vector", "DV", lambda e, sl=sl: e.tensor_tensor(
                    oqb[:, sl], sgh[0], sgh[1], ALU.add))
                chain["dv"] = dvq
            pq_ac[jc] = acq
            P.w("sync", "DV", dvq)
            pq_st[jc] = P.inc("sync", "SQ%d" % (jc % 2),
                              lambda e, jc=jc: e.dma_start(oq_scr[jc, :, :],
                                                           oqb), 16)

    P.w("scalar", "DV", dv_rows)
    P.w("sync", "DV", dv_rows)
    P.w("sync", "PE", pe_after_p1)
    gwt_r = gwt_h.rearrange("(k p) x -> p k x", p=128)
    vwt_r = vwt_h.rearrange("(k p) x -> p k x", p=128)
    st_h = {}
    lw_ac = {}
    lw_dv = {}
    pe_step_done = {}
    dv_epi = {}
    ac_epi = {}
    lr_hist = {}
    pq_next = 0
    step = 0
    dvs = 0
    for hg in range(NHG):
        for mat in range(2):
            wr = gwt_r if mat == 0 else vwt_r
            cp, cn = TH[mat]
            gr = gbr_h if mat == 0 else vbr_h
            if step >= 2:
                P.w("sync", "PE", pe_step_done[step - 2])
            lr = P.inc("sync", "LR%d" % (step % 2), lambda e, gr=gr, hg=hg,
                       b=brb[step % 2]: e.dma_start(
                           b, gr[:, hg * 512:(hg + 1) * 512]), 16)
            lr_hist[step] = lr
            for c in range(4):
                j = step * 4 + c
                bp = j % 2
                if j >= 2:
                    P.w("sync", "AC", lw_ac[j - 2])
                lw = P.inc("sync", "LW%d" % bp, lambda e, wr=wr, c=c, hg=hg,
                           bp=bp: e.dma_start(
                               wlb[bp].rearrange("p (k x) -> p k x", x=512),
                               wr[:, c * 4:c * 4 + 4,
                                  hg * 512:(hg + 1) * 512]), 16)
                P.w("scalar", "LW%d" % bp, lw)
                if step >= 1 and c == 0:
                    P.w("vector", "PE", pe_step_done[step - 1])
                for hh in range(2):
                    sl = slice(hh * 1024, (hh + 1) * 1024)
                    if chain["dv"]:
                        P.w("scalar", "DV", chain["dv"])
                    P.do("scalar", lambda e, bp=bp, sl=sl, cn=cn:
                         e.activation(sgh[0], wlb[bp][:, sl], AF.Sign,
                                      bias=col(cn)))
                    acs = P.inc("scalar", "AC", lambda e, bp=bp, sl=sl, cp=cp:
                                e.activation(sgh[1], wlb[bp][:, sl], AF.Sign,
                                             bias=col(cp)))
                    P.w("vector", "AC", acs)
                    dvs2 = P.inc("vector", "DV", lambda e, c=c, sl=sl:
                                 e.tensor_tensor(
                                     gwq[:, c * 2048 + sl.start:
                                         c * 2048 + sl.stop],
                                     sgh[0], sgh[1], ALU.add))
                    chain["dv"] = dvs2
                lw_ac[j] = acs
                lw_dv[j] = dvs2
            dv_q_all = P.val("DV")
            P.w("tensor", "DV", dv_q_all)
            if step == 0:
                P.w("tensor", "DV", dv_xi_done[NS - 1])
                P.w("tensor", "DV", dv_mz)
            P.w("tensor", "LR%d" % (step % 2), lr_hist[step])
            for t in range(16):
                bank = t % 4
                if t >= 4:
                    P.w("tensor", "AC" if mat == 0 else "DV",
                        (ac_epi if mat == 0 else dv_epi)[(step, t - 4)])
                elif step >= 1:
                    prev = (step - 1, t + 12)
                    P.w("tensor", "AC" if (step - 1) % 2 == 0 else "DV",
                        (ac_epi if (step - 1) % 2 == 0 else dv_epi)[prev])
                ro = (rows_g if mat == 0 else rows_v)(t)
                P.do("tensor", lambda e, bank=bank, ro=ro, b=brb[step % 2]:
                     e.matmul(p_a[:, bank * 512:(bank + 1) * 512], ro, b,
                              start=True, stop=False))
                for k in range(NK):
                    fn = lambda e, k=k, t=t, bank=bank: e.matmul(
                        p_a[:, bank * 512:(bank + 1) * 512],
                        B[:, B_XI + k * 2048 + t * 128:
                           B_XI + k * 2048 + t * 128 + 128],
                        gwq[:, k * 512:(k + 1) * 512],
                        start=False, stop=(k == 15))
                    if k == 15:
                        pe_mm = P.inc("tensor", "PE", fn)
                    else:
                        P.do("tensor", fn)
                if mat == 0:
                    P.w("scalar", "PE", pe_mm)
                    if step >= 2:
                        P.w("scalar", "DV", dv_epi[(step - 1, t)])
                    ac_epi[(step, t)] = P.inc(
                        "scalar", "AC", lambda e, t=t, bank=bank:
                        e.activation(gs_slab[:, t * 512:(t + 1) * 512],
                                     p_a[:, bank * 512:(bank + 1) * 512],
                                     AF.Silu, scale=ccol(C_SGC, t)))
                else:
                    P.w("vector", "PE", pe_mm)
                    hp = t % 2
                    if t >= 2:
                        P.w("vector", "SH%d" % hp, st_h[(hg, t - 2)])
                    elif hg >= 1:
                        P.w("vector", "SH%d" % hp, st_h[(hg - 1, t + 14)])
                    if t == 0:
                        P.w("vector", "AC", ac_epi[(step - 1, 15)])
                    P.do("vector", lambda e: e.drain())
                    P.do("vector", lambda e, t=t, bank=bank, hp=hp:
                         e.scalar_tensor_tensor(
                             hbuf2[hp], p_a[:, bank * 512:(bank + 1) * 512],
                             ccol(C_SVC, t), gs_slab[:, t * 512:(t + 1) * 512],
                             ALU.mult, ALU.mult))
                    P.do("vector", lambda e: e.drain())
                    P.do("vector", lambda e, hp=hp: e.tensor_reduce(
                        col(C_T0), hbuf2[hp], AX.X, ALU.max,
                        apply_absolute_value=True))
                    P.do("vector", lambda e: e.drain())
                    dv_h = P.inc("vector", "DV", lambda e, t=t:
                                 e.tensor_tensor(ccol(C_MAC, t),
                                                 ccol(C_MAC, t), col(C_T0),
                                                 ALU.max))
                    dv_epi[(step, t)] = dv_h
                    P.w("sync", "DV", dv_h)
                    st_h[(hg, t)] = P.inc(
                        "sync", "SH%d" % hp,
                        lambda e, t=t, hg=hg, hp=hp: e.dma_start(
                            h_scr[t, :, hg * 512:(hg + 1) * 512], hbuf2[hp]),
                        16)
            pe_step_done[step] = P.val("PE")
            step += 1
        emit_prequant(list(range(pq_next, pq_next + 4)))
        pq_next += 4

    P.w("vector", "AC", pq_ac[NJC - 1])
    P.do("vector", lambda e: e.drain())
    P.do("vector", lambda e: e.tensor_scalar(
        cols[:, C_MAC:C_MAC + 16], cols[:, C_MAC:C_MAC + 16], 1.0, 1e-5,
        ALU.mult, ALU.max))
    P.do("vector", lambda e: e.drain())
    P.do("vector", lambda e: e.tensor_scalar(
        cols[:, C_S2:C_S2 + 16], cols[:, C_MAC:C_MAC + 16], 1.0 / 127.0, 0.0,
        ALU.mult, ALU.add))
    P.do("vector", lambda e: e.drain())
    P.do("vector", lambda e: e.reciprocal(
        cols[:, C_MAC:C_MAC + 16], cols[:, C_S2:C_S2 + 16]))
    P.do("vector", lambda e: e.drain())
    P.do("vector", lambda e: e.tensor_single_scalar(
        cols[:, C_S2:C_S2 + 16], cols[:, C_S2:C_S2 + 16], col(C_SCO),
        ALU.mult))
    P.do("vector", lambda e: e.drain())
    rows_o = lambda s: af(O_PQ + s * 128, 128, 0, 1)
    for s in range(NS):
        P.do("vector", lambda e, s=s: e.reciprocal(col(C_T0), ccol(C_S2, s)))
        P.do("vector", lambda e: e.drain())
        dv_iv = P.inc("vector", "DV", lambda e: (e.drain(),
                                                 e.engine_nop())[1])
        P.w("tensor", "DV", dv_iv)
        if s == 0:
            P.w("tensor", "PE", pe_step_done[31])
        pe_tv = P.inc("tensor", "PE", lambda e: e.transpose(
            p_b[0:1, 648:776], col(C_T0), idf[:, :]))
        P.w("vector", "PE", pe_tv)
        P.do("vector", lambda e: e.drain())
        P.inc("vector", "DV", lambda e, ro=rows_o(s): e.tensor_copy(
            ro, p_b[0:1, 648:776]))
    dv_orows = P.inc("vector", "DV", lambda e: (e.drain(), e.engine_nop())[1])

    NE = 8
    hbq = [af(O_UB, DM), af(O_UT, DM)]
    lCc = [af(O_LC, DM), af(O_WL, DM)]
    lYc = [af(O_LY, DM), af(O_WL + 2048, DM)]
    sgc = [ab(B_SG, DM), ab(B_OQ, DM)]
    xtmp = af(O_RES, DM)
    x1r2 = [af(O_GS, DM), af(O_GS + 2048, DM)]
    z2 = [af(O_DP1, DM), af(O_X1, DM)]
    bro = af(O_BR, 1024, 0, 1)
    owqb = [B[:, B_GWA + i * 4096:B_GWA + (i + 1) * 4096] for i in range(2)]
    oq_r = oq_scr.rearrange("j p d -> p j d")
    hitH = [B[:, B_XI + i * 16384:B_XI + (i + 1) * 16384] for i in range(2)]
    hitH_v = [hitH[i].rearrange("p (jc t) -> p jc t", t=256) for i in range(2)]
    sq2 = ab(B_SG, DM)

    pc = 0
    dv_pre = {}
    dv_lad = {}
    dv_evac = {}
    pe_tr_prev = {}
    pe_e_mm = {}
    dv_z_round = {}
    st_out = {}
    dv_ln2 = {}
    lb_hist = {}
    lx_hist = {}
    for e in range(NE):
        sA, sB = 2 * e, 2 * e + 1
        hh = e % 2
        for c in range(4):
            for ch in range(2):
                st = sA if ch == 0 else sB
                bp = pc % 2
                if pc >= 2:
                    P.w("sync", "DV", dv_pre[pc - 2])
                if e == 0 and c == 0 and ch == 0:
                    P.w("sync", "SH0", P.val("SH0"))
                    P.w("sync", "SH1", P.val("SH1"))
                lh = P.inc("sync", "LH%d" % bp, lambda e_, st=st, c=c, bp=bp:
                           e_.dma_start(hbq[bp],
                                        h_scr[st, :, c * 2048:(c + 1) * 2048]),
                           16)
                P.w("vector", "LH%d" % bp, lh)
                if e == 0 and c == 0 and ch == 0:
                    P.w("vector", "DV", dv_orows)
                    P.w("vector", "SQ0", P.val("SQ0"))
                    P.w("vector", "SQ1", P.val("SQ1"))
                r2c = ccol(C_MAC, st)
                lC3, lY3, sg3 = lCc[ch], lYc[ch], sgc[ch]
                P.do("vector", lambda e_: e_.drain())
                dv_pre[pc] = P.inc("vector", "DV",
                                   lambda e_, bp=bp, r2c=r2c, lY=lY3:
                                   e_.tensor_single_scalar(lY, hbq[bp], r2c,
                                                           ALU.mult))
                dv_prev = dv_pre[pc]
                for j in range(8):
                    src_c = lY3 if j == 0 else lC3
                    P.w("scalar", "DV", dv_prev)
                    if j == 0:
                        P.w("scalar", "PE", pe_tr_prev.get(ch, 0))
                    ac_j = P.inc("scalar", "AC",
                                 lambda e_, src_c=src_c, sg=sg3:
                                 e_.activation(sg, src_c, AF.Sign,
                                               bias=col(C_HALF)))
                    P.w("vector", "AC", ac_j)
                    dv_prev = P.inc("vector", "DV",
                                    lambda e_, j=j, src_c=src_c, sg=sg3,
                                    lC=lC3:
                                    e_.scalar_tensor_tensor(
                                        lC, sg, col(C_K0 + j), src_c,
                                        ALU.mult, ALU.add))
                P.do("vector", lambda e_: e_.drain())
                P.do("vector", lambda e_, lY=lY3, lC=lC3: e_.tensor_tensor(
                    xtmp, lY, lC, ALU.subtract))
                P.do("vector", lambda e_: e_.drain())
                dv_lad[pc] = P.inc("vector", "DV", lambda e_, sg=sg3:
                                   e_.tensor_scalar(sg, xtmp, 1.0, -0.5,
                                                    ALU.mult, ALU.add))
                P.w("tensor", "DV", dv_lad[pc])
                if pc >= 1:
                    P.w("tensor", "DV", dv_evac[pc - 1])
                for kk in range(16):
                    fn = lambda e_, kk=kk, sg=sg3: e_.transpose(
                        p_tr[:, kk * 128:(kk + 1) * 128],
                        sg[:, kk * 128:(kk + 1) * 128], idb[:, :])
                    if kk == 15:
                        pe_tr = P.inc("tensor", "PE", fn)
                    else:
                        P.do("tensor", fn)
                pe_tr_prev[ch] = pe_tr
                P.w("vector", "PE", pe_tr)
                if e >= 2 and c == 0 and ch == 0:
                    P.w("vector", "PE", pe_e_mm[e - 2])
                ptr_view = p_tr[:, :].rearrange("p (k t) -> p k t", t=128)
                dv_evac[pc] = P.inc(
                    "vector", "DV",
                    lambda e_, c=c, ch=ch, hh=hh, i=ptr_view:
                    e_.tensor_copy(hitH_v[hh][:, c * 16:(c + 1) * 16,
                                              ch * 128:(ch + 1) * 128], i))
                pc += 1
        dv_lad_all = P.val("DV")
        P.w("tensor", "DV", dv_lad_all)
        for t2 in range(2):
            st = 2 * e + t2
            if e > 0:
                P.w("sync", "DV", dv_ln2[(e - 1, t2)])
            elif t2 == 0:
                P.w("sync", "SX0", P.val("SX0"))
                P.w("sync", "SX1", P.val("SX1"))
            lx_hist[(e, t2)] = P.inc(
                "sync", "LX%d" % t2, lambda e_, st=st, t2=t2:
                e_.dma_start(x1r2[t2], x1_scr[st, :, :]), 16)
        for r in range(2):
            nd = e * 2 + r
            if nd >= 1:
                P.w("sync", "PE", lb_hist[((nd - 1) // 2, (nd - 1) % 2, 0)])
            else:
                P.w("sync", "PE", pe_step_done[2 * NHG - 1])
            lrq = P.inc("sync", "LR%d" % (nd % 2), lambda e_, r=r:
                        e_.dma_start(bro, obr_h[:, r * 1024:(r + 1) * 1024]),
                        16)
            for grp in range(16):
                gbp = grp % 2
                nload = nd * 16 + grp
                if nload >= 2:
                    pk = nload - 2
                    P.w("sync", "PE", lb_hist[(pk // 32, (pk // 16) % 2,
                                               pk % 16)])
                if e == 0 and r == 0 and grp == 0:
                    P.w("sync", "SQ0", P.val("SQ0"))
                    P.w("sync", "SQ1", P.val("SQ1"))
                lb = P.inc("sync", "LB%d" % gbp, lambda e_, r=r, grp=grp,
                           gbp=gbp: e_.dma_start(
                               owqb[gbp].rearrange("p (j d) -> p j d", d=1024),
                               oq_r[:, grp * 4:grp * 4 + 4,
                                    r * 1024:(r + 1) * 1024]), 16)
                P.w("tensor", "LB%d" % gbp, lb)
                if grp == 0:
                    P.w("tensor", "LR%d" % (nd % 2), lrq)
                    if r >= 1:
                        P.w("tensor", "DV", dv_z_round[(e, 0)])
                    elif e > 0:
                        P.w("tensor", "DV", dv_z_round[(e - 1, 1)])
                    for t2 in range(2):
                        st = 2 * e + t2
                        for di in range(2):
                            bank = t2 * 2 + di
                            P.do("tensor", lambda e_, bank=bank, st=st,
                                 di=di: e_.matmul(
                                     p_a[:, bank * 512:(bank + 1) * 512],
                                     rows_o(st), bro[:, di * 512:
                                                     (di + 1) * 512],
                                     start=True, stop=False))
                for jj in range(4):
                    jc = grp * 4 + jj
                    for t2 in range(2):
                        for di in range(2):
                            bank = t2 * 2 + di
                            fn = lambda e_, jc=jc, t2=t2, di=di, bank=bank,                                 gbp=gbp, jj=jj, hh=hh: e_.matmul(
                                    p_a[:, bank * 512:(bank + 1) * 512],
                                    hitH[hh][:, jc * 256 + t2 * 128:
                                             jc * 256 + t2 * 128 + 128],
                                    owqb[gbp][:, jj * 1024 + di * 512:
                                              jj * 1024 + di * 512 + 512],
                                    start=False, stop=(jc == NJC - 1))
                            if jj == 3 and t2 == 1 and di == 1:
                                pe_o = P.inc("tensor", "PE", fn)
                            else:
                                P.do("tensor", fn)
                lb_hist[(e, r, grp)] = pe_o
            for t2 in range(2):
                st = 2 * e + t2
                for di in range(2):
                    dmg = r * 2 + di
                    bank = t2 * 2 + di
                    P.w("vector", "PE", lb_hist[(e, r, 15)])
                    if r == 0 and di == 0:
                        P.w("vector", "LX%d" % t2, lx_hist[(e, t2)])
                        if e > 0:
                            P.w("vector", "SO%d" % ((2 * (e - 1) + t2) % 2),
                                st_out[2 * (e - 1) + t2])
                    s2c = ccol(C_S2, st)
                    P.do("vector", lambda e_: e_.drain())
                    dv_z = P.inc("vector", "DV",
                                 lambda e_, t2=t2, dmg=dmg, s2c=s2c,
                                 bank=bank:
                                 e_.scalar_tensor_tensor(
                                     z2[t2][:, dmg * 512:(dmg + 1) * 512],
                                     p_a[:, bank * 512:(bank + 1) * 512],
                                     s2c,
                                     x1r2[t2][:, dmg * 512:(dmg + 1) * 512],
                                     ALU.mult, ALU.add))
            dv_z_round[(e, r)] = dv_z
        for t2 in range(2):
            st = 2 * e + t2
            P.w("vector", "DV", dv_z_round[(e, 1)])
            zt = z2[t2]
            mcol, vcol, rcol = col(C_T0), col(C_T1), col(C_T3)
            P.do("vector", lambda e_: e_.drain())
            P.do("vector", lambda e_, zt=zt: e_.tensor_reduce(mcol, zt, AX.X,
                                                              ALU.add))
            P.do("vector", lambda e_: e_.drain())
            P.do("vector", lambda e_: e_.tensor_single_scalar(mcol, mcol,
                                                              INV_D,
                                                              ALU.mult))
            P.do("vector", lambda e_: e_.drain())
            dv_l2 = P.inc("vector", "DV", lambda e_, zt=zt:
                          e_.tensor_single_scalar(zt, zt, mcol,
                                                  ALU.subtract))
            P.w("scalar", "DV", dv_l2)
            P.do("scalar", lambda e_, zt=zt: e_.activation(sq2, zt, AF.Square,
                                                           accum_out=vcol))
            P.do("scalar", lambda e_: e_.drain())
            P.do("scalar", lambda e_: e_.activation(col(C_T2), vcol, AF.Ln,
                                                    bias=col(C_EPS),
                                                    scale=INV_D))
            P.do("scalar", lambda e_: e_.drain())
            ac_l2 = P.inc("scalar", "AC", lambda e_: e_.activation(
                rcol, col(C_T2), AF.Exp, scale=-0.5))
            P.w("vector", "AC", ac_l2)
            P.do("vector", lambda e_: e_.drain())
            dv_o = P.inc("vector", "DV", lambda e_, zt=zt:
                         e_.tensor_single_scalar(zt, zt, rcol, ALU.mult))
            dv_ln2[(e, t2)] = dv_o
            P.w("sync", "DV", dv_o)
            st_out[st] = P.inc("sync", "SO%d" % (st % 2),
                               lambda e_, st=st, zt=zt: e_.dma_start(
                                   out_h[st * 128:(st + 1) * 128, :], zt), 16)
        pe_e_mm[e] = P.val("PE")

    P.w("sync", "SO0", P.val("SO0"))
    P.w("sync", "SO1", P.val("SO1"))

    with nc.Block() as block:
        @block.sync
        def _(eng):
            for f in P.ops["sync"]:
                f(eng)

        @block.tensor
        def _(eng):
            for f in P.ops["tensor"]:
                f(eng)

        @block.vector
        def _(eng):
            for f in P.ops["vector"]:
                f(eng)

        @block.scalar
        def _(eng):
            for f in P.ops["scalar"]:
                f(eng)

        @block.gpsimd
        def _(eng):
            for f in P.ops["gpsimd"]:
                f(eng)

    ctx.close()
    return nc


_NC_CACHE = None


def kernel(L0_attn_u, L0_attn_A_factors, L0_attn_W_b_rank, L0_attn_W_b_state,
           L0_attn_W_c_rank, L0_attn_W_c_model, L0_attn_D_vec,
           L0_ffn_gate_w, L0_ffn_gate_b, L0_ffn_val_w, L0_ffn_val_b,
           L0_ffn_out_w, L0_ffn_out_b):
    global _NC_CACHE
    u = np.ascontiguousarray(np.asarray(L0_attn_u, dtype=np.float32).reshape(-1, DM))
    gw = np.asarray(L0_ffn_gate_w, dtype=np.float32)
    vw = np.asarray(L0_ffn_val_w, dtype=np.float32)
    ow = np.asarray(L0_ffn_out_w, dtype=np.float32)
    shared = dict(
        gwt=np.ascontiguousarray(gw.T),
        vwt=np.ascontiguousarray(vw.T),
        owt=np.ascontiguousarray(ow.T),
        gbr=np.ascontiguousarray(np.asarray(L0_ffn_gate_b,
                                            dtype=np.float32).reshape(1, HID)),
        vbr=np.ascontiguousarray(np.asarray(L0_ffn_val_b,
                                            dtype=np.float32).reshape(1, HID)),
        obr=np.ascontiguousarray(np.asarray(L0_ffn_out_b,
                                            dtype=np.float32).reshape(1, DM)),
        wbr=np.ascontiguousarray(np.asarray(L0_attn_W_b_rank, dtype=np.float32)),
        wbs=np.ascontiguousarray(np.asarray(L0_attn_W_b_state, dtype=np.float32)),
        wcr=np.ascontiguousarray(np.asarray(L0_attn_W_c_rank, dtype=np.float32)),
        wcm=np.ascontiguousarray(np.asarray(L0_attn_W_c_model, dtype=np.float32)),
        afac=np.ascontiguousarray(
            np.asarray(L0_attn_A_factors, dtype=np.float32).reshape(1, 32)),
        dvec=np.ascontiguousarray(
            np.asarray(L0_attn_D_vec, dtype=np.float32).reshape(1, DM)),
        idf=np.eye(128, dtype=np.float32),
        idb=np.eye(128, dtype=np.float32).astype(BF),
    )
    if _NC_CACHE is None:
        _NC_CACHE = build_program()
    in_maps = []
    for c in range(NCORE):
        uc = np.ascontiguousarray(u[c * TOK:(c + 1) * TOK, :])
        in_maps.append(dict(u=uc, ut=np.ascontiguousarray(uc.T), **shared))
    res = run_bass_kernel_spmd(_NC_CACHE, in_maps, core_ids=list(range(NCORE)))
    out = np.concatenate(
        [np.asarray(r["out"], dtype=np.float32) for r in res.results], axis=0)
    return out.reshape(4, 4096, DM)


# version 07
# speedup vs baseline: 1.3524x; 1.3524x over previous
import contextlib
import numpy as np
import ml_dtypes
import concourse.bass as bass
import concourse.mybir as mybir
from concourse.bass_utils import run_bass_kernel_spmd

F32 = mybir.dt.float32
BF16 = mybir.dt.bfloat16
BF = ml_dtypes.bfloat16
AF = mybir.ActivationFunctionType
ALU = mybir.AluOpType
AX = mybir.AxisListType

NCORE = 8
TOK = 2048
DM = 2048
HID = 8192
NS = 16
NHG = 16
NK = 16
NJC = 64
NQ = 4
INV_D = 1.0 / DM
EPS = 1e-5
LADK = (128.0, 64.0, 32.0, 16.0, 8.0, 4.0, 2.0, 1.0)

O_COLS = 0
O_DP1 = 128
O_UB = 2176
O_UT = 4224
O_RES = 6272
O_X1 = 8320
O_LC = 10368
O_LY = 12416
O_GS = 14464
O_WL = 22656
O_PQ = 26752
O_BR = 28800
A_W = 29824
B_XI = 0
B_GWA = 32768
B_SG = 40960
B_OQ = 43008
B_W = 45056

(C_EPS, C_HALF, C_T0, C_T1, C_T2, C_T3, C_SCG, C_SCV, C_SCO,
 C_ISG, C_ISV, C_RA0, C_RA1, C_RA2, C_SA0, C_SA1, C_SA2, C_SB0, C_SB1,
 C_SB2, C_K0) = range(21)
C_MAC = 32
C_SXS = 48
C_SGC = 64
C_SVC = 80
C_S2 = 96
C_GXI = 112
NCOL = 128


class Prog:
    ENGS = ("sync", "tensor", "vector", "scalar", "gpsimd")

    def __init__(self):
        self.ops = {e: [] for e in self.ENGS}
        self.cnt = {}
        self.sems = {}

    def w(self, eng, sname, val):
        if val <= 0:
            return
        self.ops[eng].append(lambda e, s=self.sems[sname], v=val: e.wait_ge(s, v))

    def do(self, eng, fn):
        self.ops[eng].append(fn)

    def inc(self, eng, sname, fn, amt=1):
        self.cnt[sname] = self.cnt.get(sname, 0) + amt
        self.ops[eng].append(
            lambda e, f=fn, s=self.sems[sname], a=amt: f(e).then_inc(s, a))
        return self.cnt[sname]

    def val(self, sname):
        return self.cnt.get(sname, 0)


def build_program(debug=False):
    nc = bass.Bass("TRN2", target_bir_lowering=False)

    u_h = nc.dram_tensor("u", [TOK, DM], F32, kind="ExternalInput").ap()
    ut_h = nc.dram_tensor("ut", [DM, TOK], F32, kind="ExternalInput").ap()
    gwt_h = nc.dram_tensor("gwt", [DM, HID], F32, kind="ExternalInput").ap()
    vwt_h = nc.dram_tensor("vwt", [DM, HID], F32, kind="ExternalInput").ap()
    owt_h = nc.dram_tensor("owt", [HID, DM], F32, kind="ExternalInput").ap()
    gbr_h = nc.dram_tensor("gbr", [1, HID], F32, kind="ExternalInput").ap()
    vbr_h = nc.dram_tensor("vbr", [1, HID], F32, kind="ExternalInput").ap()
    obr_h = nc.dram_tensor("obr", [1, DM], F32, kind="ExternalInput").ap()
    wbr_h = nc.dram_tensor("wbr", [DM, 64], F32, kind="ExternalInput").ap()
    wbs_h = nc.dram_tensor("wbs", [64, 256], F32, kind="ExternalInput").ap()
    wcr_h = nc.dram_tensor("wcr", [256, 64], F32, kind="ExternalInput").ap()
    wcm_h = nc.dram_tensor("wcm", [64, DM], F32, kind="ExternalInput").ap()
    a_h = nc.dram_tensor("afac", [1, 32], F32, kind="ExternalInput").ap()
    d_h = nc.dram_tensor("dvec", [1, DM], F32, kind="ExternalInput").ap()
    idf_h = nc.dram_tensor("idf", [128, 128], F32, kind="ExternalInput").ap()
    idb_h = nc.dram_tensor("idb", [128, 128], BF16, kind="ExternalInput").ap()
    out_h = nc.dram_tensor("out", [TOK, DM], F32, kind="ExternalOutput").ap()
    scr_kind = "ExternalOutput" if debug else "Internal"
    x1_scr = nc.dram_tensor("x1_scr", [NS, 128, DM], F32, kind=scr_kind).ap()
    h_scr = nc.dram_tensor("h_scr", [NS, 128, HID], F32, kind=scr_kind).ap()
    oq_scr = nc.dram_tensor("oq_scr", [NJC, 128, DM], BF16, kind=scr_kind).ap()

    ctx = contextlib.ExitStack()
    A = ctx.enter_context(nc.sbuf_tensor("arena_f32", [128, A_W], F32))
    B = ctx.enter_context(nc.sbuf_tensor("arena_bf16", [128, B_W], BF16))

    def af(off, w, p0=0, p1=128):
        return A[p0:p1, off:off + w]

    def ab(off, w, p0=0, p1=128):
        return B[p0:p1, off:off + w]

    idf = ctx.enter_context(nc.sbuf_tensor("idf_s", [128, 128], F32))
    idb = ctx.enter_context(nc.sbuf_tensor("idb_s", [128, 128], BF16))
    ones1 = ctx.enter_context(nc.sbuf_tensor("ones1", [1, 128], F32))
    onesc = ctx.enter_context(nc.sbuf_tensor("onesc", [128, 1], F32))
    gxs = ctx.enter_context(nc.sbuf_tensor("gxs", [128, NS], F32))
    g11 = ctx.enter_context(nc.sbuf_tensor("g11", [1, 8], F32))

    p_a = ctx.enter_context(nc.psum_tensor("p_a", [128, 2048], F32))
    p_b = ctx.enter_context(nc.psum_tensor("p_b", [128, 1024], F32))
    p_tr = ctx.enter_context(nc.psum_tensor("p_tr", [128, 2048], BF16))

    P = Prog()
    semnames = ["PE", "DV", "AC", "GP", "L0",
                "LU0", "LU1", "LT0", "LT1", "LA0", "LA1", "LW0", "LW1",
                "LQ0", "LQ1", "LH0", "LH1", "LB0", "LB1", "LX0", "LX1",
                "LR0", "LR1", "PT", "PG", "PL",
                "SX0", "SX1", "SH0", "SH1", "SQ0", "SQ1", "SO0", "SO1"]
    for nm in semnames:
        P.sems[nm] = ctx.enter_context(nc.semaphore("s" + nm))

    cols = af(O_COLS, NCOL)

    def col(i):
        return A[:, O_COLS + i:O_COLS + i + 1]

    def ccol(base, i):
        return A[:, O_COLS + base + i:O_COLS + base + i + 1]

    M_ = O_WL
    wbrT = af(M_, 1024)
    wcm_f = af(M_ + 1024, DM, 0, 64)
    G_f = af(M_ + 3072, 64, 0, 64)
    t1b = af(M_ + 3136, 128, 0, 64)
    t2b = af(M_ + 3264, 128, 0, 64)
    wbs_t = af(O_PQ, 256, 0, 64)
    bb0 = af(O_PQ + 256, 256, 0, 64)
    bb1 = af(O_PQ + 512, 256, 0, 64)
    tmp256 = af(O_PQ + 768, 256, 0, 64)
    a_bc = af(O_PQ + 1024, 32, 0, 64)
    a_row = af(O_PQ + 1056, 32, 0, 1)
    btT = af(O_PQ + 1088, 128)
    wcr0 = af(O_PQ + 1216, 64)
    wcr1 = af(O_PQ + 1280, 64)
    d_row = af(O_RES, DM, 0, 1)
    dp1 = af(O_DP1, DM)

    ld0 = [(idf[:, :], idf_h[:, :]), (idb[:, :], idb_h[:, :]),
           (d_row, d_h[:, :]), (wbs_t, wbs_h[:, :]), (a_row, a_h[:, :]),
           (wcm_f, wcm_h[:, :]),
           (wbrT.rearrange("p (k r) -> p k r", r=64),
            wbr_h.rearrange("(k p) r -> p k r", p=128)),
           (wcr0, wcr_h[0:128, :]), (wcr1, wcr_h[128:256, :])]
    for dst, src in ld0:
        P.inc("sync", "L0", lambda e, d=dst, s=src: e.dma_start(d, s), 16)
    ld0_done = P.val("L0")

    P.do("vector", lambda e: e.memset(col(C_EPS), float(EPS)))
    P.do("vector", lambda e: e.memset(col(C_HALF), 0.5))
    for j, kk in enumerate(LADK):
        P.do("vector", lambda e, j=j, kk=kk: e.memset(col(C_K0 + j), -kk / 2.0))
    P.do("vector", lambda e: e.memset(ones1[:1, :], 1.0))
    P.do("vector", lambda e: e.memset(onesc[:, :], 1.0))
    P.do("vector", lambda e: e.memset(cols[:, C_RA0:C_RA0 + 3], 0.0))
    P.w("vector", "L0", ld0_done)
    dv_c0 = P.inc("vector", "DV", lambda e: (e.drain(), e.engine_nop())[1])

    P.w("tensor", "DV", dv_c0)
    pe_abc = P.inc("tensor", "PE", lambda e: e.matmul(
        p_b[0:128, 0:32], ones1[:1, 0:128], a_row, start=True, stop=True))
    P.w("vector", "PE", pe_abc)
    P.do("vector", lambda e: e.tensor_copy(a_bc, p_b[0:64, 0:32]))
    P.do("vector", lambda e: e.drain())
    P.do("vector", lambda e: e.tensor_copy(bb0, wbs_t))
    cur, nxt = bb0, bb1
    for lvl in range(8):
        wwid = 2 ** (7 - lvl)
        vi = cur.rearrange("p (b t w) -> p b t w", t=2, w=wwid)
        vo = nxt.rearrange("p (b t w) -> p b t w", t=2, w=wwid)
        vt = tmp256.rearrange("p (b t w) -> p b t w", t=2, w=wwid)
        for j in range(2):
            a0 = a_bc[:, lvl * 4 + j: lvl * 4 + j + 1]
            a1 = a_bc[:, lvl * 4 + 2 + j: lvl * 4 + 3 + j]
            P.do("vector", lambda e: e.drain())
            P.do("vector", lambda e, o=vt[:, :, j, :], i=vi[:, :, 0, :], s=a0:
                 e.tensor_single_scalar(o, i, s, ALU.mult))
            P.do("vector", lambda e: e.drain())
            P.do("vector", lambda e, o=vo[:, :, j, :], i=vi[:, :, 1, :], s=a1,
                 t=vt[:, :, j, :]:
                 e.scalar_tensor_tensor(o, i, s, t, ALU.mult, ALU.add))
        cur, nxt = nxt, cur
    dv_bfly = P.inc("vector", "DV", lambda e: (e.drain(), e.engine_nop())[1])

    P.w("tensor", "DV", dv_bfly)
    P.do("tensor", lambda e, i=cur: e.transpose(p_b[:, 0:64], i[:, 0:128],
                                                idf[0:64, 0:64]))
    pe_btt = P.inc("tensor", "PE", lambda e, i=cur: e.transpose(
        p_b[:, 64:128], i[:, 128:256], idf[0:64, 0:64]))
    P.w("vector", "PE", pe_btt)
    dv_btt = P.inc("vector", "DV", lambda e: e.tensor_copy(btT, p_b[:, 0:128]))
    P.w("tensor", "DV", dv_btt)
    P.do("tensor", lambda e: e.matmul(p_b[0:64, 256:320], btT[0:128, 0:64],
                                      wcr0, start=True, stop=False))
    pe_g = P.inc("tensor", "PE", lambda e: e.matmul(
        p_b[0:64, 256:320], btT[0:128, 64:128], wcr1, start=False, stop=True))
    P.w("vector", "PE", pe_g)
    dv_g = P.inc("vector", "DV", lambda e: e.tensor_copy(G_f, p_b[0:64, 256:320]))

    P.w("tensor", "DV", dv_g)
    dv_dpe = {}
    for i in range(4):
        if i >= 2:
            P.w("tensor", "DV", dv_dpe[i - 2])
        pe_d = P.inc("tensor", "PE", lambda e, i=i: e.matmul(
            p_a[:, (i % 2) * 512:(i % 2) * 512 + 512], ones1[:1, :],
            d_row[:, i * 512:(i + 1) * 512], start=True, stop=True))
        P.w("vector", "PE", pe_d)
        dv_dpe[i] = P.inc("vector", "DV", lambda e, i=i: e.tensor_scalar(
            dp1[:, i * 512:(i + 1) * 512],
            p_a[:, (i % 2) * 512:(i % 2) * 512 + 512],
            1.0, 1.0, ALU.mult, ALU.add))
    dv_dp1 = P.inc("vector", "DV", lambda e: (e.drain(), e.engine_nop())[1])

    ubuf = af(O_UB, DM)
    utb = af(O_UT, DM)
    res = af(O_RES, DM)
    x1b = af(O_X1, DM)
    lC = af(O_LC, DM)
    lY = af(O_LY, DM)
    sgt = ab(B_SG, DM)
    xt_bf = ab(B_OQ, DM)
    xi_all = B[:, B_XI:B_XI + NS * DM].rearrange("p (k t) -> p k t", t=TOK)
    pA_w = [af(O_GS, DM), af(O_GS + 2048, DM)]

    pa_chunks = []
    for mi, wh in ((0, gwt_h), (1, vwt_h)):
        fl = wh.rearrange("(c p) x -> c p x", p=128)
        for c in range(16):
            for s4 in range(4):
                pa_chunks.append((fl[c, :, s4 * 2048:(s4 + 1) * 2048], mi))
    flo = owt_h.rearrange("(c p) x -> c p x", p=128)
    for c in range(64):
        pa_chunks.append((flo[c, :, :], 2))
    NPA = len(pa_chunks)
    per = (NPA + NS - 1) // NS
    pa_sched = [list(range(s * per, min((s + 1) * per, NPA))) for s in range(NS)]

    la_ac = {}
    gp_add = {}
    abs_scr = [ab(B_GWA, DM), ab(B_GWA + 2048, DM)]

    def emit_passA(idxs):
        for j in idxs:
            src, mi = pa_chunks[j]
            bp = j % 2
            if j >= 2:
                P.w("sync", "AC", la_ac[j - 2])
            la = P.inc("sync", "LA%d" % bp,
                       lambda e, d=pA_w[bp], s=src: e.dma_start(d, s), 16)
            P.w("scalar", "LA%d" % bp, la)
            if j >= 2:
                P.w("scalar", "GP", gp_add[j - 2])
            acc = ccol(0, (C_SA0 if j % 2 == 0 else C_SB0) + mi)
            la_ac[j] = P.inc("scalar", "AC", lambda e, bp=bp, acc=acc, j=j:
                             e.activation(abs_scr[j % 2], pA_w[bp], AF.Abs,
                                          accum_out=acc))
            P.w("gpsimd", "AC", la_ac[j])
            gp_add[j] = P.inc("gpsimd", "GP", lambda e, mi=mi, acc=acc:
                              e.tensor_tensor(ccol(0, C_RA0 + mi),
                                              ccol(0, C_RA0 + mi), acc,
                                              ALU.add))
            P.do("gpsimd", lambda e: e.drain())

    st_x1 = {}
    dv_xi_done = {}
    dv_ures_s = {}
    pe_t1_s = {}
    dv_resadd = {}
    for s in range(NS):
        bp = s % 2
        if s >= 1:
            P.w("sync", "DV", dv_ures_s[s - 1])
        lu = P.inc("sync", "LU%d" % bp, lambda e, s=s: e.dma_start(
            ubuf, u_h[s * 128:(s + 1) * 128, :]), 16)
        if s >= 1:
            P.w("sync", "PE", pe_t1_s[s - 1])
        lt = P.inc("sync", "LT%d" % bp, lambda e, s=s: e.dma_start(
            utb.rearrange("p (k t) -> p k t", t=128),
            ut_h.rearrange("(k p) t -> p k t", p=128)
            [:, :, s * 128:(s + 1) * 128]), 16)
        P.w("tensor", "LT%d" % bp, lt)
        if s == 0:
            P.w("tensor", "DV", dv_dp1)
        for k in range(NK):
            fn = lambda e, k=k: e.matmul(
                p_b[0:64, 0:128], wbrT[:, k * 64:(k + 1) * 64],
                utb[:, k * 128:(k + 1) * 128], start=(k == 0), stop=(k == 15))
            if k == 15:
                pe_t1 = P.inc("tensor", "PE", fn)
            else:
                P.do("tensor", fn)
        pe_t1_s[s] = pe_t1
        P.w("vector", "PE", pe_t1)
        dv_t1 = P.inc("vector", "DV", lambda e: e.tensor_copy(
            t1b, p_b[0:64, 0:128]))
        P.w("tensor", "DV", dv_t1)
        pe_t2 = P.inc("tensor", "PE", lambda e: e.matmul(
            p_b[0:64, 512:640], G_f, t1b[0:64, :], start=True, stop=True))
        P.w("vector", "PE", pe_t2)
        dv_t2 = P.inc("vector", "DV", lambda e: e.tensor_copy(
            t2b, p_b[0:64, 512:640]))
        P.w("tensor", "DV", dv_t2)
        pe_y = {}
        for g in range(4):
            if s >= 1:
                P.w("tensor", "DV", dv_resadd[(s - 1, g)])
            pe_y[g] = P.inc("tensor", "PE", lambda e, g=g: e.matmul(
                p_a[:, g * 512:(g + 1) * 512], t2b[0:64, :],
                wcm_f[:, g * 512:(g + 1) * 512], start=True, stop=True))
        P.w("vector", "LU%d" % bp, lu)
        P.do("vector", lambda e: e.drain())
        P.do("vector", lambda e: e.tensor_tensor(res, ubuf, dp1, ALU.mult))
        dv_ures_s[s] = P.inc("vector", "DV",
                             lambda e: (e.drain(), e.engine_nop())[1])
        for g in range(4):
            P.w("vector", "PE", pe_y[g])
            P.do("vector", lambda e: e.drain())
            dv_resadd[(s, g)] = P.inc("vector", "DV", lambda e, g=g:
                                      e.tensor_tensor(
                                          res[:, g * 512:(g + 1) * 512],
                                          res[:, g * 512:(g + 1) * 512],
                                          p_a[:, g * 512:(g + 1) * 512],
                                          ALU.add))
        mcol, vcol, rcol = col(C_T0), col(C_T1), col(C_T3)
        P.do("vector", lambda e: e.drain())
        P.do("vector", lambda e: e.tensor_reduce(mcol, res, AX.X, ALU.add))
        P.do("vector", lambda e: e.drain())
        P.do("vector", lambda e: e.tensor_single_scalar(mcol, mcol, INV_D,
                                                        ALU.mult))
        P.do("vector", lambda e: e.drain())
        dv_ln = P.inc("vector", "DV", lambda e: e.tensor_single_scalar(
            res, res, mcol, ALU.subtract))
        P.w("scalar", "DV", dv_ln)
        P.do("scalar", lambda e: e.activation(sgt, res, AF.Square,
                                              accum_out=vcol))
        P.do("scalar", lambda e: e.drain())
        P.do("scalar", lambda e: e.activation(col(C_T2), vcol, AF.Ln,
                                              bias=col(C_EPS), scale=INV_D))
        P.do("scalar", lambda e: e.drain())
        ac_ln = P.inc("scalar", "AC", lambda e: e.activation(
            rcol, col(C_T2), AF.Exp, scale=-0.5))
        P.w("vector", "AC", ac_ln)
        if s >= 1:
            P.w("vector", "SX%d" % ((s - 1) % 2), st_x1[s - 1])
        P.do("vector", lambda e: e.drain())
        dv_x1 = P.inc("vector", "DV", lambda e: e.tensor_single_scalar(
            x1b, res, rcol, ALU.mult))
        P.w("sync", "DV", dv_x1)
        st_x1[s] = P.inc("sync", "SX%d" % bp, lambda e, s=s: e.dma_start(
            x1_scr[s, :, :], x1b), 16)
        gxc = gxs[:, s:s + 1]
        sxc = ccol(C_SXS, s)
        P.do("vector", lambda e: e.drain())
        P.do("vector", lambda e, g=gxc: e.tensor_reduce(
            g, x1b, AX.X, ALU.max, apply_absolute_value=True))
        P.do("vector", lambda e: e.drain())
        P.do("vector", lambda e, g=gxc: e.tensor_single_scalar(
            g, g, 1e-5, ALU.max))
        P.do("vector", lambda e: e.drain())
        P.do("vector", lambda e, g=gxc, s=s: e.tensor_single_scalar(
            ccol(C_GXI, s), g, 1.0 / 127.0, ALU.mult))
        P.do("vector", lambda e: e.drain())
        P.do("vector", lambda e, sc=sxc, s=s: e.reciprocal(
            sc, ccol(C_GXI, s)))
        P.do("vector", lambda e: e.drain())
        dv_prev = P.inc("vector", "DV", lambda e, sc=sxc:
                        e.tensor_single_scalar(lY, x1b, sc, ALU.mult))
        for j in range(8):
            src_c = lY if j == 0 else lC
            P.w("scalar", "DV", dv_prev)
            ac_j = P.inc("scalar", "AC", lambda e, src_c=src_c: e.activation(
                sgt, src_c, AF.Sign, bias=col(C_HALF)))
            P.w("vector", "AC", ac_j)
            dv_prev = P.inc("vector", "DV", lambda e, j=j, src_c=src_c:
                            e.scalar_tensor_tensor(lC, sgt, col(C_K0 + j),
                                                   src_c, ALU.mult, ALU.add))
        P.do("vector", lambda e: e.drain())
        P.do("vector", lambda e: e.tensor_tensor(res, lY, lC, ALU.subtract))
        P.do("vector", lambda e: e.drain())
        if s >= 1:
            P.w("vector", "PE", pe_xt_s)
        dv_xt = P.inc("vector", "DV", lambda e: e.tensor_scalar(
            xt_bf, res, 1.0, -0.5, ALU.mult, ALU.add))
        P.w("tensor", "DV", dv_xt)
        if s >= 1:
            P.w("tensor", "DV", dv_xi_done[s - 1])
        for k in range(16):
            fn = lambda e, k=k: e.transpose(
                p_tr[:, k * 128:(k + 1) * 128], xt_bf[:, k * 128:(k + 1) * 128],
                idb[:, :])
            if k == 15:
                pe_xt_s = P.inc("tensor", "PE", fn)
            else:
                P.do("tensor", fn)
        P.w("vector", "PE", pe_xt_s)
        ptr_view = p_tr[:, :].rearrange("p (k t) -> p k t", t=128)
        dv_xi_done[s] = P.inc(
            "vector", "DV",
            lambda e, s=s, o=xi_all[:, :, s * 128:(s + 1) * 128], i=ptr_view:
            e.tensor_copy(o, i))
        emit_passA(pa_sched[s])

    pe_after_p1 = P.val("PE")

    P.w("tensor", "GP", P.val("GP"))
    pe_g3 = {}
    for mi in range(3):
        pe_g3[mi] = P.inc("tensor", "PE", lambda e, mi=mi: e.matmul(
            p_b[0:1, 640 + mi:641 + mi], ccol(0, C_RA0 + mi), onesc[:, :],
            start=True, stop=True))
    P.w("vector", "PE", pe_g3[2])
    P.do("vector", lambda e: e.tensor_scalar(
        g11[:1, 0:3], p_b[0:1, 640:643], 1.0 / 16777216.0, 1e-5,
        ALU.mult, ALU.max))
    dv_g11 = P.inc("vector", "DV", lambda e: (e.drain(), e.engine_nop())[1])
    P.w("tensor", "DV", dv_g11)
    pe_gb = P.inc("tensor", "PE", lambda e: e.matmul(
        p_b[:, 644:647], ones1[:1, :], g11[:1, 0:3], start=True, stop=True))
    P.w("vector", "PE", pe_gb)
    for gi, (cth, csc, cis) in enumerate(
            [(C_SCG, C_SCG, C_ISG), (C_SCV, C_SCV, C_ISV), (C_SCO, C_SCO, -1)]):
        P.do("vector", lambda e, gi=gi, csc=csc: e.tensor_single_scalar(
            col(csc), p_b[:, 644 + gi:645 + gi], 0.5, ALU.mult))
        P.do("vector", lambda e: e.drain())
        if cis >= 0:
            P.do("vector", lambda e, csc=csc, cis=cis: e.reciprocal(
                col(cis), col(csc)))
            P.do("vector", lambda e: e.drain())
    TH = {}
    for mi, csc in ((0, C_SCG), (1, C_SCV), (2, C_SCO)):
        cp = C_SA0 + mi
        cn = C_SB0 + mi
        P.do("vector", lambda e, cp=cp, csc=csc: e.tensor_copy(
            col(cp), col(csc)))
        P.do("vector", lambda e: e.drain())
        P.do("vector", lambda e, cp=cp, cn=cn: e.tensor_single_scalar(
            col(cn), col(cp), -1.0, ALU.mult))
        P.do("vector", lambda e: e.drain())
        TH[mi] = (cp, cn)
    P.do("vector", lambda e: e.tensor_single_scalar(
        cols[:, C_SGC:C_SGC + 16], cols[:, C_GXI:C_GXI + 16], col(C_SCG),
        ALU.mult))
    P.do("vector", lambda e: e.drain())
    P.do("vector", lambda e: e.tensor_single_scalar(
        cols[:, C_SVC:C_SVC + 16], cols[:, C_GXI:C_GXI + 16], col(C_SCV),
        ALU.mult))
    P.do("vector", lambda e: e.drain())
    rows_g = lambda s: af(O_RES + s * 128, 128, 0, 1)
    rows_v = lambda s: af(O_X1 + s * 128, 128, 0, 1)
    P.w("vector", "SX0", st_x1[NS - 2])
    P.w("vector", "SX1", st_x1[NS - 1])
    for mi in range(2):
        base = C_SGC if mi == 0 else C_SVC
        rows = rows_g if mi == 0 else rows_v
        for s in range(NS):
            P.do("vector", lambda e, base=base, s=s: e.reciprocal(
                col(C_T0), ccol(base, s)))
            P.do("vector", lambda e: e.drain())
            dv_iv = P.inc("vector", "DV", lambda e: (e.drain(),
                                                     e.engine_nop())[1])
            P.w("tensor", "DV", dv_iv)
            pe_tv = P.inc("tensor", "PE", lambda e: e.transpose(
                p_b[0:1, 648:776], col(C_T0), idf[:, :]))
            P.w("vector", "PE", pe_tv)
            P.do("vector", lambda e: e.drain())
            P.inc("vector", "DV", lambda e, ro=rows(s): e.tensor_copy(
                ro, p_b[0:1, 648:776]))
    dv_rows = P.inc("vector", "DV", lambda e: (e.drain(), e.engine_nop())[1])

    wlb = [af(O_WL, DM), af(O_WL + 2048, DM)]
    pqb = af(O_PQ, DM)
    gs_slab = af(O_GS, 8192)
    sgh = [ab(B_SG, 1024), ab(B_SG + 1024, 1024)]
    oqb = ab(B_OQ, DM)
    gwq = B[:, B_GWA:B_GWA + 8192]
    hbuf2 = [af(O_UB, 512), af(O_UB + 512, 512)]
    brb = [af(O_BR, 512, 0, 1), af(O_BR + 512, 512, 0, 1)]
    P.do("vector", lambda e: e.memset(cols[:, C_MAC:C_MAC + 16], 0.0))
    dv_mz = P.inc("vector", "DV", lambda e: (e.drain(), e.engine_nop())[1])

    pq_ac = {}
    pq_st = {}
    chain = {"dv": 0}

    def emit_prequant(jcs):
        for jc in jcs:
            if jc >= NJC:
                continue
            if jc >= 1:
                P.w("sync", "AC", pq_ac[jc - 1])
            lq = P.inc("sync", "LQ%d" % (jc % 2), lambda e, jc=jc:
                       e.dma_start(pqb, flo[jc, :, :]), 16)
            P.w("scalar", "LQ%d" % (jc % 2), lq)
            for hh in range(2):
                sl = slice(hh * 1024, (hh + 1) * 1024)
                if chain["dv"]:
                    P.w("scalar", "DV", chain["dv"])
                P.do("scalar", lambda e, sl=sl: e.activation(
                    sgh[0], pqb[:, sl], AF.Sign, bias=col(C_SB0 + 2)))
                acq = P.inc("scalar", "AC", lambda e, sl=sl: e.activation(
                    sgh[1], pqb[:, sl], AF.Sign, bias=col(C_SA0 + 2)))
                P.w("vector", "AC", acq)
                if jc >= 1 and hh == 0:
                    P.w("vector", "SQ%d" % ((jc - 1) % 2), pq_st[jc - 1])
                dvq = P.inc("vector", "DV", lambda e, sl=sl: e.tensor_tensor(
                    oqb[:, sl], sgh[0], sgh[1], ALU.add))
                chain["dv"] = dvq
            pq_ac[jc] = acq
            P.w("sync", "DV", dvq)
            pq_st[jc] = P.inc("sync", "SQ%d" % (jc % 2),
                              lambda e, jc=jc: e.dma_start(oq_scr[jc, :, :],
                                                           oqb), 16)

    P.w("scalar", "DV", dv_rows)
    P.w("sync", "DV", dv_rows)
    P.w("sync", "PE", pe_after_p1)
    gwt_r = gwt_h.rearrange("(k p) x -> p k x", p=128)
    vwt_r = vwt_h.rearrange("(k p) x -> p k x", p=128)
    st_h = {}
    lw_ac = {}
    lw_dv = {}
    pe_step_done = {}
    dv_epi = {}
    ac_epi = {}
    lr_hist = {}
    pq_next = 0
    step = 0
    dvs = 0
    for hg in range(NHG):
        for mat in range(2):
            wr = gwt_r if mat == 0 else vwt_r
            cp, cn = TH[mat]
            gr = gbr_h if mat == 0 else vbr_h
            if step >= 2:
                P.w("sync", "PE", pe_step_done[step - 2])
            lr = P.inc("sync", "LR%d" % (step % 2), lambda e, gr=gr, hg=hg,
                       b=brb[step % 2]: e.dma_start(
                           b, gr[:, hg * 512:(hg + 1) * 512]), 16)
            lr_hist[step] = lr
            for c in range(4):
                j = step * 4 + c
                bp = j % 2
                if j >= 2:
                    P.w("sync", "AC", lw_ac[j - 2])
                lw = P.inc("sync", "LW%d" % bp, lambda e, wr=wr, c=c, hg=hg,
                           bp=bp: e.dma_start(
                               wlb[bp].rearrange("p (k x) -> p k x", x=512),
                               wr[:, c * 4:c * 4 + 4,
                                  hg * 512:(hg + 1) * 512]), 16)
                P.w("scalar", "LW%d" % bp, lw)
                if step >= 1 and c == 0:
                    P.w("vector", "PE", pe_step_done[step - 1])
                for hh in range(2):
                    sl = slice(hh * 1024, (hh + 1) * 1024)
                    if chain["dv"]:
                        P.w("scalar", "DV", chain["dv"])
                    P.do("scalar", lambda e, bp=bp, sl=sl, cn=cn:
                         e.activation(sgh[0], wlb[bp][:, sl], AF.Sign,
                                      bias=col(cn)))
                    acs = P.inc("scalar", "AC", lambda e, bp=bp, sl=sl, cp=cp:
                                e.activation(sgh[1], wlb[bp][:, sl], AF.Sign,
                                             bias=col(cp)))
                    P.w("vector", "AC", acs)
                    dvs2 = P.inc("vector", "DV", lambda e, c=c, sl=sl:
                                 e.tensor_tensor(
                                     gwq[:, c * 2048 + sl.start:
                                         c * 2048 + sl.stop],
                                     sgh[0], sgh[1], ALU.add))
                    chain["dv"] = dvs2
                lw_ac[j] = acs
                lw_dv[j] = dvs2
            dv_q_all = P.val("DV")
            P.w("tensor", "DV", dv_q_all)
            if step == 0:
                P.w("tensor", "DV", dv_xi_done[NS - 1])
                P.w("tensor", "DV", dv_mz)
            P.w("tensor", "LR%d" % (step % 2), lr_hist[step])
            for t in range(16):
                bank = t % 6
                psl = (p_a[:, bank * 512:(bank + 1) * 512] if bank < 4
                       else p_b[:, (bank - 4) * 512:(bank - 3) * 512])
                if t >= 6:
                    P.w("tensor", "AC" if mat == 0 else "DV",
                        (ac_epi if mat == 0 else dv_epi)[(step, t - 6)])
                elif step >= 1:
                    prev = (step - 1, t + 10)
                    P.w("tensor", "AC" if (step - 1) % 2 == 0 else "DV",
                        (ac_epi if (step - 1) % 2 == 0 else dv_epi)[prev])
                ro = (rows_g if mat == 0 else rows_v)(t)
                P.do("tensor", lambda e, psl=psl, ro=ro, b=brb[step % 2]:
                     e.matmul(psl, ro, b, start=True, stop=False))
                for k in range(NK):
                    fn = lambda e, k=k, t=t, psl=psl: e.matmul(
                        psl,
                        B[:, B_XI + k * 2048 + t * 128:
                           B_XI + k * 2048 + t * 128 + 128],
                        gwq[:, k * 512:(k + 1) * 512],
                        start=False, stop=(k == 15))
                    if k == 15:
                        pe_mm = P.inc("tensor", "PE", fn)
                    else:
                        P.do("tensor", fn)
                if mat == 0:
                    P.w("scalar", "PE", pe_mm)
                    if step >= 2:
                        P.w("scalar", "DV", dv_epi[(step - 1, t)])
                    ac_epi[(step, t)] = P.inc(
                        "scalar", "AC", lambda e, t=t, psl=psl:
                        e.activation(gs_slab[:, t * 512:(t + 1) * 512],
                                     psl, AF.Silu, scale=ccol(C_SGC, t)))
                else:
                    P.w("vector", "PE", pe_mm)
                    hp = t % 2
                    if t >= 2:
                        P.w("vector", "SH%d" % hp, st_h[(hg, t - 2)])
                    elif hg >= 1:
                        P.w("vector", "SH%d" % hp, st_h[(hg - 1, t + 14)])
                    if t == 0:
                        P.w("vector", "AC", ac_epi[(step - 1, 15)])
                    P.do("vector", lambda e: e.drain())
                    P.do("vector", lambda e, t=t, psl=psl, hp=hp:
                         e.scalar_tensor_tensor(
                             hbuf2[hp], psl,
                             ccol(C_SVC, t), gs_slab[:, t * 512:(t + 1) * 512],
                             ALU.mult, ALU.mult))
                    P.do("vector", lambda e: e.drain())
                    P.do("vector", lambda e, hp=hp: e.tensor_reduce(
                        col(C_T0), hbuf2[hp], AX.X, ALU.max,
                        apply_absolute_value=True))
                    P.do("vector", lambda e: e.drain())
                    dv_h = P.inc("vector", "DV", lambda e, t=t:
                                 e.tensor_tensor(ccol(C_MAC, t),
                                                 ccol(C_MAC, t), col(C_T0),
                                                 ALU.max))
                    dv_epi[(step, t)] = dv_h
                    P.w("sync", "DV", dv_h)
                    st_h[(hg, t)] = P.inc(
                        "sync", "SH%d" % hp,
                        lambda e, t=t, hg=hg, hp=hp: e.dma_start(
                            h_scr[t, :, hg * 512:(hg + 1) * 512], hbuf2[hp]),
                        16)
            pe_step_done[step] = P.val("PE")
            step += 1
        emit_prequant(list(range(pq_next, pq_next + 4)))
        pq_next += 4

    P.w("vector", "AC", pq_ac[NJC - 1])
    P.do("vector", lambda e: e.drain())
    P.do("vector", lambda e: e.tensor_scalar(
        cols[:, C_MAC:C_MAC + 16], cols[:, C_MAC:C_MAC + 16], 1.0, 1e-5,
        ALU.mult, ALU.max))
    P.do("vector", lambda e: e.drain())
    P.do("vector", lambda e: e.tensor_scalar(
        cols[:, C_S2:C_S2 + 16], cols[:, C_MAC:C_MAC + 16], 1.0 / 127.0, 0.0,
        ALU.mult, ALU.add))
    P.do("vector", lambda e: e.drain())
    P.do("vector", lambda e: e.reciprocal(
        cols[:, C_MAC:C_MAC + 16], cols[:, C_S2:C_S2 + 16]))
    P.do("vector", lambda e: e.drain())
    P.do("vector", lambda e: e.tensor_single_scalar(
        cols[:, C_S2:C_S2 + 16], cols[:, C_S2:C_S2 + 16], col(C_SCO),
        ALU.mult))
    P.do("vector", lambda e: e.drain())
    rows_o = lambda s: af(O_PQ + s * 128, 128, 0, 1)
    for s in range(NS):
        P.do("vector", lambda e, s=s: e.reciprocal(col(C_T0), ccol(C_S2, s)))
        P.do("vector", lambda e: e.drain())
        dv_iv = P.inc("vector", "DV", lambda e: (e.drain(),
                                                 e.engine_nop())[1])
        P.w("tensor", "DV", dv_iv)
        if s == 0:
            P.w("tensor", "PE", pe_step_done[31])
        pe_tv = P.inc("tensor", "PE", lambda e: e.transpose(
            p_b[0:1, 648:776], col(C_T0), idf[:, :]))
        P.w("vector", "PE", pe_tv)
        P.do("vector", lambda e: e.drain())
        P.inc("vector", "DV", lambda e, ro=rows_o(s): e.tensor_copy(
            ro, p_b[0:1, 648:776]))
    dv_orows = P.inc("vector", "DV", lambda e: (e.drain(), e.engine_nop())[1])

    NE = 8
    hbq = [af(O_UB, DM), af(O_UT, DM)]
    lCc = [af(O_LC, DM), af(O_WL, DM)]
    lYc = [af(O_LY, DM), af(O_WL + 2048, DM)]
    sgc = [ab(B_SG, DM), ab(B_OQ, DM)]
    xtmp = af(O_RES, DM)
    x1r2 = [af(O_GS, DM), af(O_GS + 2048, DM)]
    z2 = [af(O_DP1, DM), af(O_X1, DM)]
    bro = af(O_BR, 1024, 0, 1)
    owqb = [B[:, B_GWA + i * 4096:B_GWA + (i + 1) * 4096] for i in range(2)]
    oq_r = oq_scr.rearrange("j p d -> p j d")
    hitH = [B[:, B_XI + i * 16384:B_XI + (i + 1) * 16384] for i in range(2)]
    hitH_v = [hitH[i].rearrange("p (jc t) -> p jc t", t=256) for i in range(2)]
    sq2 = ab(B_SG, DM)

    dv_lad = {}
    dv_evac = {}
    dv_evac_all = {}
    dv_z_round = {}
    st_out = {}
    dv_ln2 = {}
    lh_vals = {}
    lb_vals = {}
    lr_vals = {}
    lx_vals = {}
    tr_emitters = {}

    def emit_lh(e, p):
        pc = e * 8 + p
        c, ch = p // 2, p % 2
        st = 2 * e + ch
        bp = pc % 2
        if pc >= 2:
            P.w("sync", "PL", pc - 1)
        if pc == 0:
            P.w("sync", "SH0", P.val("SH0"))
            P.w("sync", "SH1", P.val("SH1"))
        lh_vals[pc] = P.inc("sync", "LH%d" % bp, lambda e_, st=st, c=c,
                            bp=bp: e_.dma_start(
                                hbq[bp],
                                h_scr[st, :, c * 2048:(c + 1) * 2048]), 16)

    def emit_p3_sync(e):
        em = e - 1
        if em >= 0:
            nd0 = em * 2
            if nd0 >= 1:
                P.w("sync", "PG", (nd0 - 1) * 16 + 1)
            lr_vals[nd0] = P.inc("sync", "LR%d" % (nd0 % 2), lambda e_:
                                 e_.dma_start(bro, obr_h[:, 0:1024]), 16)
            for t2 in range(2):
                st = 2 * em + t2
                if em > 0:
                    P.w("sync", "DV", dv_ln2[(em - 1, t2)])
                elif t2 == 0:
                    P.w("sync", "SX0", P.val("SX0"))
                    P.w("sync", "SX1", P.val("SX1"))
                lx_vals[(em, t2)] = P.inc(
                    "sync", "LX%d" % t2, lambda e_, st=st, t2=t2:
                    e_.dma_start(x1r2[t2], x1_scr[st, :, :]), 16)
            for g in range(16):
                nload = em * 32 + g
                if nload >= 2:
                    P.w("sync", "PG", nload - 1)
                if em == 0 and g == 0:
                    P.w("sync", "SQ0", P.val("SQ0"))
                    P.w("sync", "SQ1", P.val("SQ1"))
                lb_vals[(em, 0, g)] = P.inc(
                    "sync", "LB%d" % (g % 2), lambda e_, g=g: e_.dma_start(
                        owqb[g % 2].rearrange("p (j d) -> p j d", d=1024),
                        oq_r[:, g * 4:g * 4 + 4, 0:1024]), 16)
                if e < NE and 1 <= g <= 8:
                    emit_lh(e, g - 1)
            nd1 = em * 2 + 1
            P.w("sync", "PG", (nd1 - 1) * 16 + 1)
            lr_vals[nd1] = P.inc("sync", "LR%d" % (nd1 % 2), lambda e_:
                                 e_.dma_start(bro, obr_h[:, 1024:2048]), 16)
            for g in range(16):
                nload = em * 32 + 16 + g
                P.w("sync", "PG", nload - 1)
                lb_vals[(em, 1, g)] = P.inc(
                    "sync", "LB%d" % (g % 2), lambda e_, g=g: e_.dma_start(
                        owqb[g % 2].rearrange("p (j d) -> p j d", d=1024),
                        oq_r[:, g * 4:g * 4 + 4, 1024:2048]), 16)
        else:
            for p in range(8):
                emit_lh(e, p)

    def emit_ladder(e):
        hh = e % 2
        emitters = []
        for c in range(4):
            pcs = [e * 8 + c * 2, e * 8 + c * 2 + 1]
            for ch in range(2):
                pc = pcs[ch]
                st = 2 * e + ch
                bp = pc % 2
                P.w("vector", "LH%d" % bp, lh_vals[pc])
                if pc == 0:
                    P.w("vector", "DV", dv_orows)
                    P.w("vector", "SQ0", P.val("SQ0"))
                    P.w("vector", "SQ1", P.val("SQ1"))
                r2c = ccol(C_MAC, st)
                P.do("vector", lambda e_: e_.drain())
                P.inc("vector", "PL", lambda e_, bp=bp, r2c=r2c, lY=lYc[ch]:
                      e_.tensor_single_scalar(lY, hbq[bp], r2c, ALU.mult))
            dv_prev = [0, 0]
            for j in range(8):
                ac_j = [0, 0]
                for ch in range(2):
                    pc = pcs[ch]
                    src_c = lYc[ch] if j == 0 else lCc[ch]
                    if j == 0:
                        P.w("scalar", "PL", pc + 1)
                        if pc >= 2:
                            P.w("scalar", "PT", pc - 1)
                    else:
                        P.w("scalar", "DV", dv_prev[ch])
                    ac_j[ch] = P.inc("scalar", "AC",
                                     lambda e_, src_c=src_c, sg=sgc[ch]:
                                     e_.activation(sg, src_c, AF.Sign,
                                                   bias=col(C_HALF)))
                for ch in range(2):
                    src_c = lYc[ch] if j == 0 else lCc[ch]
                    P.w("vector", "AC", ac_j[ch])
                    dv_prev[ch] = P.inc(
                        "vector", "DV",
                        lambda e_, j=j, src_c=src_c, sg=sgc[ch], lC=lCc[ch]:
                        e_.scalar_tensor_tensor(lC, sg, col(C_K0 + j), src_c,
                                                ALU.mult, ALU.add))
            for ch in range(2):
                pc = pcs[ch]
                c_, ch_ = c, ch
                P.do("vector", lambda e_: e_.drain())
                P.do("vector", lambda e_, lY=lYc[ch], lC=lCc[ch]:
                     e_.tensor_tensor(xtmp, lY, lC, ALU.subtract))
                P.do("vector", lambda e_: e_.drain())
                dv_lad[pc] = P.inc("vector", "DV", lambda e_, sg=sgc[ch]:
                                   e_.tensor_scalar(sg, xtmp, 1.0, -0.5,
                                                    ALU.mult, ALU.add))

                def tr_emit(pc=pc, sg=sgc[ch]):
                    P.w("tensor", "DV", dv_lad[pc])
                    if pc >= 1:
                        P.w("tensor", "DV", dv_evac[pc - 1])
                    for kk in range(16):
                        fn = lambda e_, kk=kk, sg=sg: e_.transpose(
                            p_tr[:, kk * 128:(kk + 1) * 128],
                            sg[:, kk * 128:(kk + 1) * 128], idb[:, :])
                        if kk == 15:
                            P.inc("tensor", "PT", fn)
                        else:
                            P.do("tensor", fn)
                emitters.append(tr_emit)
                P.w("vector", "PT", pc + 1)
                if e >= 2 and pc == e * 8:
                    P.w("vector", "PG", (e - 1) * 32)
                ptr_view = p_tr[:, :].rearrange("p (k t) -> p k t", t=128)
                dv_evac[pc] = P.inc(
                    "vector", "DV",
                    lambda e_, c=c_, ch=ch_, hh=hh, i=ptr_view:
                    e_.tensor_copy(hitH_v[hh][:, c * 16:(c + 1) * 16,
                                              ch * 128:(ch + 1) * 128], i))
        dv_evac_all[e] = P.val("DV")
        tr_emitters[e] = emitters

    def emit_mm(em, emitters):
        hh = em % 2
        spl = 0
        for r in range(2):
            nd = em * 2 + r
            for grp in range(16):
                gbp = grp % 2
                if r == 0 and grp % 2 == 1 and spl < len(emitters):
                    emitters[spl]()
                    spl += 1
                P.w("tensor", "LB%d" % gbp, lb_vals[(em, r, grp)])
                if grp == 0:
                    P.w("tensor", "LR%d" % (nd % 2), lr_vals[nd])
                    if r == 0:
                        P.w("tensor", "DV", dv_evac_all[em])
                    if r >= 1:
                        P.w("tensor", "DV", dv_z_round[(em, 0)])
                    elif em > 0:
                        P.w("tensor", "DV", dv_z_round[(em - 1, 1)])
                    for t2 in range(2):
                        st = 2 * em + t2
                        for di in range(2):
                            bank = t2 * 2 + di
                            P.do("tensor", lambda e_, bank=bank, st=st,
                                 di=di: e_.matmul(
                                     p_a[:, bank * 512:(bank + 1) * 512],
                                     rows_o(st),
                                     bro[:, di * 512:(di + 1) * 512],
                                     start=True, stop=False))
                for jj in range(4):
                    jc = grp * 4 + jj
                    for t2 in range(2):
                        for di in range(2):
                            bank = t2 * 2 + di
                            fn = lambda e_, jc=jc, t2=t2, di=di, bank=bank, \
                                gbp=gbp, jj=jj, hh=hh: e_.matmul(
                                    p_a[:, bank * 512:(bank + 1) * 512],
                                    hitH[hh][:, jc * 256 + t2 * 128:
                                             jc * 256 + t2 * 128 + 128],
                                    owqb[gbp][:, jj * 1024 + di * 512:
                                              jj * 1024 + di * 512 + 512],
                                    start=False, stop=(jc == NJC - 1))
                            if jj == 3 and t2 == 1 and di == 1:
                                P.inc("tensor", "PG", fn)
                            else:
                                P.do("tensor", fn)
            for t2 in range(2):
                st = 2 * em + t2
                for di in range(2):
                    dmg = r * 2 + di
                    bank = t2 * 2 + di
                    P.w("vector", "PG", nd * 16 + 16)
                    if r == 0 and di == 0:
                        P.w("vector", "LX%d" % t2, lx_vals[(em, t2)])
                        if em > 0:
                            P.w("vector", "SO%d" % ((2 * (em - 1) + t2) % 2),
                                st_out[2 * (em - 1) + t2])
                    s2c = ccol(C_S2, st)
                    P.do("vector", lambda e_: e_.drain())
                    dv_z = P.inc("vector", "DV",
                                 lambda e_, t2=t2, dmg=dmg, s2c=s2c,
                                 bank=bank:
                                 e_.scalar_tensor_tensor(
                                     z2[t2][:, dmg * 512:(dmg + 1) * 512],
                                     p_a[:, bank * 512:(bank + 1) * 512],
                                     s2c,
                                     x1r2[t2][:, dmg * 512:(dmg + 1) * 512],
                                     ALU.mult, ALU.add))
            dv_z_round[(em, r)] = dv_z
        for t2 in range(2):
            st = 2 * em + t2
            P.w("vector", "DV", dv_z_round[(em, 1)])
            zt = z2[t2]
            mcol, vcol, rcol = col(C_T0), col(C_T1), col(C_T3)
            P.do("vector", lambda e_: e_.drain())
            P.do("vector", lambda e_, zt=zt: e_.tensor_reduce(mcol, zt, AX.X,
                                                              ALU.add))
            P.do("vector", lambda e_: e_.drain())
            P.do("vector", lambda e_: e_.tensor_single_scalar(mcol, mcol,
                                                              INV_D,
                                                              ALU.mult))
            P.do("vector", lambda e_: e_.drain())
            dv_l2 = P.inc("vector", "DV", lambda e_, zt=zt:
                          e_.tensor_single_scalar(zt, zt, mcol,
                                                  ALU.subtract))
            P.w("scalar", "DV", dv_l2)
            P.do("scalar", lambda e_, zt=zt: e_.activation(sq2, zt, AF.Square,
                                                           accum_out=vcol))
            P.do("scalar", lambda e_: e_.drain())
            P.do("scalar", lambda e_: e_.activation(col(C_T2), vcol, AF.Ln,
                                                    bias=col(C_EPS),
                                                    scale=INV_D))
            P.do("scalar", lambda e_: e_.drain())
            ac_l2 = P.inc("scalar", "AC", lambda e_: e_.activation(
                rcol, col(C_T2), AF.Exp, scale=-0.5))
            P.w("vector", "AC", ac_l2)
            P.do("vector", lambda e_: e_.drain())
            dv_o = P.inc("vector", "DV", lambda e_, zt=zt:
                         e_.tensor_single_scalar(zt, zt, rcol, ALU.mult))
            dv_ln2[(em, t2)] = dv_o
            P.w("sync", "DV", dv_o)
            st_out[st] = P.inc("sync", "SO%d" % (st % 2),
                               lambda e_, st=st, zt=zt: e_.dma_start(
                                   out_h[st * 128:(st + 1) * 128, :], zt), 16)

    P.w("sync", "PE", pe_step_done[2 * NHG - 1])
    for e in range(NE + 1):
        emit_p3_sync(e)
        if e < NE:
            emit_ladder(e)
        if e == 0:
            for em_ in tr_emitters[0]:
                em_()
            tr_emitters[0] = []
        if e >= 1:
            emit_mm(e - 1, tr_emitters[e] if e < NE else [])

    P.w("sync", "SO0", P.val("SO0"))
    P.w("sync", "SO1", P.val("SO1"))

    with nc.Block() as block:
        @block.sync
        def _(eng):
            for f in P.ops["sync"]:
                f(eng)

        @block.tensor
        def _(eng):
            for f in P.ops["tensor"]:
                f(eng)

        @block.vector
        def _(eng):
            for f in P.ops["vector"]:
                f(eng)

        @block.scalar
        def _(eng):
            for f in P.ops["scalar"]:
                f(eng)

        @block.gpsimd
        def _(eng):
            for f in P.ops["gpsimd"]:
                f(eng)

    ctx.close()
    return nc


_NC_CACHE = None


def kernel(L0_attn_u, L0_attn_A_factors, L0_attn_W_b_rank, L0_attn_W_b_state,
           L0_attn_W_c_rank, L0_attn_W_c_model, L0_attn_D_vec,
           L0_ffn_gate_w, L0_ffn_gate_b, L0_ffn_val_w, L0_ffn_val_b,
           L0_ffn_out_w, L0_ffn_out_b):
    global _NC_CACHE
    u = np.ascontiguousarray(np.asarray(L0_attn_u, dtype=np.float32).reshape(-1, DM))
    gw = np.asarray(L0_ffn_gate_w, dtype=np.float32)
    vw = np.asarray(L0_ffn_val_w, dtype=np.float32)
    ow = np.asarray(L0_ffn_out_w, dtype=np.float32)
    shared = dict(
        gwt=np.ascontiguousarray(gw.T),
        vwt=np.ascontiguousarray(vw.T),
        owt=np.ascontiguousarray(ow.T),
        gbr=np.ascontiguousarray(np.asarray(L0_ffn_gate_b,
                                            dtype=np.float32).reshape(1, HID)),
        vbr=np.ascontiguousarray(np.asarray(L0_ffn_val_b,
                                            dtype=np.float32).reshape(1, HID)),
        obr=np.ascontiguousarray(np.asarray(L0_ffn_out_b,
                                            dtype=np.float32).reshape(1, DM)),
        wbr=np.ascontiguousarray(np.asarray(L0_attn_W_b_rank, dtype=np.float32)),
        wbs=np.ascontiguousarray(np.asarray(L0_attn_W_b_state, dtype=np.float32)),
        wcr=np.ascontiguousarray(np.asarray(L0_attn_W_c_rank, dtype=np.float32)),
        wcm=np.ascontiguousarray(np.asarray(L0_attn_W_c_model, dtype=np.float32)),
        afac=np.ascontiguousarray(
            np.asarray(L0_attn_A_factors, dtype=np.float32).reshape(1, 32)),
        dvec=np.ascontiguousarray(
            np.asarray(L0_attn_D_vec, dtype=np.float32).reshape(1, DM)),
        idf=np.eye(128, dtype=np.float32),
        idb=np.eye(128, dtype=np.float32).astype(BF),
    )
    if _NC_CACHE is None:
        _NC_CACHE = build_program()
    in_maps = []
    for c in range(NCORE):
        uc = np.ascontiguousarray(u[c * TOK:(c + 1) * TOK, :])
        in_maps.append(dict(u=uc, ut=np.ascontiguousarray(uc.T), **shared))
    res = run_bass_kernel_spmd(_NC_CACHE, in_maps, core_ids=list(range(NCORE)))
    out = np.concatenate(
        [np.asarray(r["out"], dtype=np.float32) for r in res.results], axis=0)
    return out.reshape(4, 4096, DM)


# version 08
# speedup vs baseline: 1.3821x; 1.0220x over previous
import contextlib
import numpy as np
import ml_dtypes
import concourse.bass as bass
import concourse.mybir as mybir
from concourse.bass_utils import run_bass_kernel_spmd

F32 = mybir.dt.float32
BF16 = mybir.dt.bfloat16
BF = ml_dtypes.bfloat16
AF = mybir.ActivationFunctionType
ALU = mybir.AluOpType
AX = mybir.AxisListType

NCORE = 8
TOK = 2048
DM = 2048
HID = 8192
NS = 16
NHG = 16
NK = 16
NJC = 64
NQ = 4
INV_D = 1.0 / DM
EPS = 1e-5
LADK = (128.0, 64.0, 32.0, 16.0, 8.0, 4.0, 2.0, 1.0)

O_COLS = 0
O_DP1 = 128
O_UB = 2176
O_UT = 4224
O_RES = 6272
O_X1 = 8320
O_LC = 10368
O_LY = 12416
O_GS = 14464
O_WL = 18560
O_PQ = 22656
O_BR = 24704
A_W = 25728
B_XI = 0
B_GWA = 32768
B_GWB = 40960
B_SG = 49152
B_OQ = 51200
B_W = 53248

(C_EPS, C_HALF, C_T0, C_T1, C_T2, C_T3, C_SCG, C_SCV, C_SCO,
 C_ISG, C_ISV, C_RA0, C_RA1, C_RA2, C_SA0, C_SA1, C_SA2, C_SB0, C_SB1,
 C_SB2, C_K0) = range(21)
C_MAC = 32
C_SXS = 48
C_SGC = 64
C_SVC = 80
C_S2 = 96
C_GXI = 112
NCOL = 128


class Prog:
    ENGS = ("sync", "tensor", "vector", "scalar", "gpsimd")

    def __init__(self):
        self.ops = {e: [] for e in self.ENGS}
        self.cnt = {}
        self.sems = {}

    def w(self, eng, sname, val):
        if val <= 0:
            return
        self.ops[eng].append(lambda e, s=self.sems[sname], v=val: e.wait_ge(s, v))

    def do(self, eng, fn):
        self.ops[eng].append(fn)

    def inc(self, eng, sname, fn, amt=1):
        self.cnt[sname] = self.cnt.get(sname, 0) + amt
        self.ops[eng].append(
            lambda e, f=fn, s=self.sems[sname], a=amt: f(e).then_inc(s, a))
        return self.cnt[sname]

    def val(self, sname):
        return self.cnt.get(sname, 0)


def build_program(debug=False):
    nc = bass.Bass("TRN2", target_bir_lowering=False)

    u_h = nc.dram_tensor("u", [TOK, DM], F32, kind="ExternalInput").ap()
    ut_h = nc.dram_tensor("ut", [DM, TOK], F32, kind="ExternalInput").ap()
    gwt_h = nc.dram_tensor("gwt", [DM, HID], F32, kind="ExternalInput").ap()
    vwt_h = nc.dram_tensor("vwt", [DM, HID], F32, kind="ExternalInput").ap()
    owt_h = nc.dram_tensor("owt", [HID, DM], F32, kind="ExternalInput").ap()
    gbr_h = nc.dram_tensor("gbr", [1, HID], F32, kind="ExternalInput").ap()
    vbr_h = nc.dram_tensor("vbr", [1, HID], F32, kind="ExternalInput").ap()
    obr_h = nc.dram_tensor("obr", [1, DM], F32, kind="ExternalInput").ap()
    wbr_h = nc.dram_tensor("wbr", [DM, 64], F32, kind="ExternalInput").ap()
    wbs_h = nc.dram_tensor("wbs", [64, 256], F32, kind="ExternalInput").ap()
    wcr_h = nc.dram_tensor("wcr", [256, 64], F32, kind="ExternalInput").ap()
    wcm_h = nc.dram_tensor("wcm", [64, DM], F32, kind="ExternalInput").ap()
    a_h = nc.dram_tensor("afac", [1, 32], F32, kind="ExternalInput").ap()
    d_h = nc.dram_tensor("dvec", [1, DM], F32, kind="ExternalInput").ap()
    idf_h = nc.dram_tensor("idf", [128, 128], F32, kind="ExternalInput").ap()
    idb_h = nc.dram_tensor("idb", [128, 128], BF16, kind="ExternalInput").ap()
    out_h = nc.dram_tensor("out", [TOK, DM], F32, kind="ExternalOutput").ap()
    scr_kind = "ExternalOutput" if debug else "Internal"
    x1_scr = nc.dram_tensor("x1_scr", [NS, 128, DM], F32, kind=scr_kind).ap()
    h_scr = nc.dram_tensor("h_scr", [NS, 128, HID], F32, kind=scr_kind).ap()
    oq_scr = nc.dram_tensor("oq_scr", [NJC, 128, DM], BF16, kind=scr_kind).ap()

    ctx = contextlib.ExitStack()
    A = ctx.enter_context(nc.sbuf_tensor("arena_f32", [128, A_W], F32))
    B = ctx.enter_context(nc.sbuf_tensor("arena_bf16", [128, B_W], BF16))

    def af(off, w, p0=0, p1=128):
        return A[p0:p1, off:off + w]

    def ab(off, w, p0=0, p1=128):
        return B[p0:p1, off:off + w]

    idf = ctx.enter_context(nc.sbuf_tensor("idf_s", [128, 128], F32))
    idb = ctx.enter_context(nc.sbuf_tensor("idb_s", [128, 128], BF16))
    ones1 = ctx.enter_context(nc.sbuf_tensor("ones1", [1, 128], F32))
    onesc = ctx.enter_context(nc.sbuf_tensor("onesc", [128, 1], F32))
    gxs = ctx.enter_context(nc.sbuf_tensor("gxs", [128, NS], F32))
    g11 = ctx.enter_context(nc.sbuf_tensor("g11", [1, 8], F32))

    p_a = ctx.enter_context(nc.psum_tensor("p_a", [128, 2048], F32))
    p_b = ctx.enter_context(nc.psum_tensor("p_b", [128, 1024], F32))
    p_tr = ctx.enter_context(nc.psum_tensor("p_tr", [128, 2048], BF16))

    P = Prog()
    semnames = ["PE", "DV", "AC", "GP", "L0",
                "LU0", "LU1", "LT0", "LT1", "LA0", "LA1", "LW0", "LW1",
                "LQ0", "LQ1", "LH0", "LH1", "LB0", "LB1", "LX0", "LX1",
                "LR0", "LR1", "PT", "PG", "PL",
                "SX0", "SX1", "SH0", "SH1", "SQ0", "SQ1", "SO0", "SO1"]
    for nm in semnames:
        P.sems[nm] = ctx.enter_context(nc.semaphore("s" + nm))

    cols = af(O_COLS, NCOL)

    def col(i):
        return A[:, O_COLS + i:O_COLS + i + 1]

    def ccol(base, i):
        return A[:, O_COLS + base + i:O_COLS + base + i + 1]

    M_ = O_WL
    wbrT = af(M_, 1024)
    wcm_f = af(M_ + 1024, DM, 0, 64)
    G_f = af(M_ + 3072, 64, 0, 64)
    t1b = af(M_ + 3136, 128, 0, 64)
    t2b = af(M_ + 3264, 128, 0, 64)
    wbs_t = af(O_PQ, 256, 0, 64)
    bb0 = af(O_PQ + 256, 256, 0, 64)
    bb1 = af(O_PQ + 512, 256, 0, 64)
    tmp256 = af(O_PQ + 768, 256, 0, 64)
    a_bc = af(O_PQ + 1024, 32, 0, 64)
    a_row = af(O_PQ + 1056, 32, 0, 1)
    btT = af(O_PQ + 1088, 128)
    wcr0 = af(O_PQ + 1216, 64)
    wcr1 = af(O_PQ + 1280, 64)
    d_row = af(O_RES, DM, 0, 1)
    dp1 = af(O_DP1, DM)

    ld0 = [(idf[:, :], idf_h[:, :]), (idb[:, :], idb_h[:, :]),
           (d_row, d_h[:, :]), (wbs_t, wbs_h[:, :]), (a_row, a_h[:, :]),
           (wcm_f, wcm_h[:, :]),
           (wbrT.rearrange("p (k r) -> p k r", r=64),
            wbr_h.rearrange("(k p) r -> p k r", p=128)),
           (wcr0, wcr_h[0:128, :]), (wcr1, wcr_h[128:256, :])]
    for dst, src in ld0:
        P.inc("sync", "L0", lambda e, d=dst, s=src: e.dma_start(d, s), 16)
    ld0_done = P.val("L0")

    P.do("vector", lambda e: e.memset(col(C_EPS), float(EPS)))
    P.do("vector", lambda e: e.memset(col(C_HALF), 0.5))
    for j, kk in enumerate(LADK):
        P.do("vector", lambda e, j=j, kk=kk: e.memset(col(C_K0 + j), -kk / 2.0))
    P.do("vector", lambda e: e.memset(ones1[:1, :], 1.0))
    P.do("vector", lambda e: e.memset(onesc[:, :], 1.0))
    P.do("vector", lambda e: e.memset(cols[:, C_RA0:C_RA0 + 3], 0.0))
    P.w("vector", "L0", ld0_done)
    dv_c0 = P.inc("vector", "DV", lambda e: (e.drain(), e.engine_nop())[1])

    P.w("tensor", "DV", dv_c0)
    pe_abc = P.inc("tensor", "PE", lambda e: e.matmul(
        p_b[0:128, 0:32], ones1[:1, 0:128], a_row, start=True, stop=True))
    P.w("vector", "PE", pe_abc)
    P.do("vector", lambda e: e.tensor_copy(a_bc, p_b[0:64, 0:32]))
    P.do("vector", lambda e: e.drain())
    P.do("vector", lambda e: e.tensor_copy(bb0, wbs_t))
    cur, nxt = bb0, bb1
    for lvl in range(8):
        wwid = 2 ** (7 - lvl)
        vi = cur.rearrange("p (b t w) -> p b t w", t=2, w=wwid)
        vo = nxt.rearrange("p (b t w) -> p b t w", t=2, w=wwid)
        vt = tmp256.rearrange("p (b t w) -> p b t w", t=2, w=wwid)
        for j in range(2):
            a0 = a_bc[:, lvl * 4 + j: lvl * 4 + j + 1]
            a1 = a_bc[:, lvl * 4 + 2 + j: lvl * 4 + 3 + j]
            P.do("vector", lambda e: e.drain())
            P.do("vector", lambda e, o=vt[:, :, j, :], i=vi[:, :, 0, :], s=a0:
                 e.tensor_single_scalar(o, i, s, ALU.mult))
            P.do("vector", lambda e: e.drain())
            P.do("vector", lambda e, o=vo[:, :, j, :], i=vi[:, :, 1, :], s=a1,
                 t=vt[:, :, j, :]:
                 e.scalar_tensor_tensor(o, i, s, t, ALU.mult, ALU.add))
        cur, nxt = nxt, cur
    dv_bfly = P.inc("vector", "DV", lambda e: (e.drain(), e.engine_nop())[1])

    P.w("tensor", "DV", dv_bfly)
    P.do("tensor", lambda e, i=cur: e.transpose(p_b[:, 0:64], i[:, 0:128],
                                                idf[0:64, 0:64]))
    pe_btt = P.inc("tensor", "PE", lambda e, i=cur: e.transpose(
        p_b[:, 64:128], i[:, 128:256], idf[0:64, 0:64]))
    P.w("vector", "PE", pe_btt)
    dv_btt = P.inc("vector", "DV", lambda e: e.tensor_copy(btT, p_b[:, 0:128]))
    P.w("tensor", "DV", dv_btt)
    P.do("tensor", lambda e: e.matmul(p_b[0:64, 256:320], btT[0:128, 0:64],
                                      wcr0, start=True, stop=False))
    pe_g = P.inc("tensor", "PE", lambda e: e.matmul(
        p_b[0:64, 256:320], btT[0:128, 64:128], wcr1, start=False, stop=True))
    P.w("vector", "PE", pe_g)
    dv_g = P.inc("vector", "DV", lambda e: e.tensor_copy(G_f, p_b[0:64, 256:320]))

    P.w("tensor", "DV", dv_g)
    dv_dpe = {}
    for i in range(4):
        if i >= 2:
            P.w("tensor", "DV", dv_dpe[i - 2])
        pe_d = P.inc("tensor", "PE", lambda e, i=i: e.matmul(
            p_a[:, (i % 2) * 512:(i % 2) * 512 + 512], ones1[:1, :],
            d_row[:, i * 512:(i + 1) * 512], start=True, stop=True))
        P.w("vector", "PE", pe_d)
        dv_dpe[i] = P.inc("vector", "DV", lambda e, i=i: e.tensor_scalar(
            dp1[:, i * 512:(i + 1) * 512],
            p_a[:, (i % 2) * 512:(i % 2) * 512 + 512],
            1.0, 1.0, ALU.mult, ALU.add))
    dv_dp1 = P.inc("vector", "DV", lambda e: (e.drain(), e.engine_nop())[1])

    ubuf = af(O_UB, DM)
    utb = af(O_UT, DM)
    res = af(O_RES, DM)
    x1b = af(O_X1, DM)
    lC = af(O_LC, DM)
    lY = af(O_LY, DM)
    sgt = ab(B_SG, DM)
    xt_bf = ab(B_OQ, DM)
    xi_all = B[:, B_XI:B_XI + NS * DM].rearrange("p (k t) -> p k t", t=TOK)
    pA_w = [af(O_GS, DM), af(O_GS + 2048, DM)]

    pa_chunks = []
    for mi, wh in ((0, gwt_h), (1, vwt_h)):
        fl = wh.rearrange("(c p) x -> c p x", p=128)
        for c in range(16):
            for s4 in range(4):
                pa_chunks.append((fl[c, :, s4 * 2048:(s4 + 1) * 2048], mi))
    flo = owt_h.rearrange("(c p) x -> c p x", p=128)
    for c in range(64):
        pa_chunks.append((flo[c, :, :], 2))
    NPA = len(pa_chunks)
    per = (NPA + NS - 1) // NS
    pa_sched = [list(range(s * per, min((s + 1) * per, NPA))) for s in range(NS)]

    la_ac = {}
    gp_add = {}
    abs_scr = [ab(B_GWA, DM), ab(B_GWA + 2048, DM)]

    def emit_passA(idxs):
        for j in idxs:
            src, mi = pa_chunks[j]
            bp = j % 2
            if j >= 2:
                P.w("sync", "AC", la_ac[j - 2])
            la = P.inc("sync", "LA%d" % bp,
                       lambda e, d=pA_w[bp], s=src: e.dma_start(d, s), 16)
            P.w("scalar", "LA%d" % bp, la)
            if j >= 2:
                P.w("scalar", "GP", gp_add[j - 2])
            acc = ccol(0, (C_SA0 if j % 2 == 0 else C_SB0) + mi)
            la_ac[j] = P.inc("scalar", "AC", lambda e, bp=bp, acc=acc, j=j:
                             e.activation(abs_scr[j % 2], pA_w[bp], AF.Abs,
                                          accum_out=acc))
            P.w("gpsimd", "AC", la_ac[j])
            gp_add[j] = P.inc("gpsimd", "GP", lambda e, mi=mi, acc=acc:
                              e.tensor_tensor(ccol(0, C_RA0 + mi),
                                              ccol(0, C_RA0 + mi), acc,
                                              ALU.add))
            P.do("gpsimd", lambda e: e.drain())

    st_x1 = {}
    dv_xi_done = {}
    dv_ures_s = {}
    pe_t1_s = {}
    dv_resadd = {}
    for s in range(NS):
        bp = s % 2
        if s >= 1:
            P.w("sync", "DV", dv_ures_s[s - 1])
        lu = P.inc("sync", "LU%d" % bp, lambda e, s=s: e.dma_start(
            ubuf, u_h[s * 128:(s + 1) * 128, :]), 16)
        if s >= 1:
            P.w("sync", "PE", pe_t1_s[s - 1])
        lt = P.inc("sync", "LT%d" % bp, lambda e, s=s: e.dma_start(
            utb.rearrange("p (k t) -> p k t", t=128),
            ut_h.rearrange("(k p) t -> p k t", p=128)
            [:, :, s * 128:(s + 1) * 128]), 16)
        P.w("tensor", "LT%d" % bp, lt)
        if s == 0:
            P.w("tensor", "DV", dv_dp1)
        for k in range(NK):
            fn = lambda e, k=k: e.matmul(
                p_b[0:64, 0:128], wbrT[:, k * 64:(k + 1) * 64],
                utb[:, k * 128:(k + 1) * 128], start=(k == 0), stop=(k == 15))
            if k == 15:
                pe_t1 = P.inc("tensor", "PE", fn)
            else:
                P.do("tensor", fn)
        pe_t1_s[s] = pe_t1
        P.w("vector", "PE", pe_t1)
        dv_t1 = P.inc("vector", "DV", lambda e: e.tensor_copy(
            t1b, p_b[0:64, 0:128]))
        P.w("tensor", "DV", dv_t1)
        pe_t2 = P.inc("tensor", "PE", lambda e: e.matmul(
            p_b[0:64, 512:640], G_f, t1b[0:64, :], start=True, stop=True))
        P.w("vector", "PE", pe_t2)
        dv_t2 = P.inc("vector", "DV", lambda e: e.tensor_copy(
            t2b, p_b[0:64, 512:640]))
        P.w("tensor", "DV", dv_t2)
        pe_y = {}
        for g in range(4):
            if s >= 1:
                P.w("tensor", "DV", dv_resadd[(s - 1, g)])
            pe_y[g] = P.inc("tensor", "PE", lambda e, g=g: e.matmul(
                p_a[:, g * 512:(g + 1) * 512], t2b[0:64, :],
                wcm_f[:, g * 512:(g + 1) * 512], start=True, stop=True))
        P.w("vector", "LU%d" % bp, lu)
        P.do("vector", lambda e: e.drain())
        P.do("vector", lambda e: e.tensor_tensor(res, ubuf, dp1, ALU.mult))
        dv_ures_s[s] = P.inc("vector", "DV",
                             lambda e: (e.drain(), e.engine_nop())[1])
        for g in range(4):
            P.w("vector", "PE", pe_y[g])
            P.do("vector", lambda e: e.drain())
            dv_resadd[(s, g)] = P.inc("vector", "DV", lambda e, g=g:
                                      e.tensor_tensor(
                                          res[:, g * 512:(g + 1) * 512],
                                          res[:, g * 512:(g + 1) * 512],
                                          p_a[:, g * 512:(g + 1) * 512],
                                          ALU.add))
        mcol, vcol, rcol = col(C_T0), col(C_T1), col(C_T3)
        P.do("vector", lambda e: e.drain())
        P.do("vector", lambda e: e.tensor_reduce(mcol, res, AX.X, ALU.add))
        P.do("vector", lambda e: e.drain())
        P.do("vector", lambda e: e.tensor_single_scalar(mcol, mcol, INV_D,
                                                        ALU.mult))
        P.do("vector", lambda e: e.drain())
        dv_ln = P.inc("vector", "DV", lambda e: e.tensor_single_scalar(
            res, res, mcol, ALU.subtract))
        P.w("scalar", "DV", dv_ln)
        P.do("scalar", lambda e: e.activation(sgt, res, AF.Square,
                                              accum_out=vcol))
        P.do("scalar", lambda e: e.drain())
        P.do("scalar", lambda e: e.activation(col(C_T2), vcol, AF.Ln,
                                              bias=col(C_EPS), scale=INV_D))
        P.do("scalar", lambda e: e.drain())
        ac_ln = P.inc("scalar", "AC", lambda e: e.activation(
            rcol, col(C_T2), AF.Exp, scale=-0.5))
        P.w("vector", "AC", ac_ln)
        if s >= 1:
            P.w("vector", "SX%d" % ((s - 1) % 2), st_x1[s - 1])
        P.do("vector", lambda e: e.drain())
        dv_x1 = P.inc("vector", "DV", lambda e: e.tensor_single_scalar(
            x1b, res, rcol, ALU.mult))
        P.w("sync", "DV", dv_x1)
        st_x1[s] = P.inc("sync", "SX%d" % bp, lambda e, s=s: e.dma_start(
            x1_scr[s, :, :], x1b), 16)
        gxc = gxs[:, s:s + 1]
        sxc = ccol(C_SXS, s)
        P.do("vector", lambda e: e.drain())
        P.do("vector", lambda e, g=gxc: e.tensor_reduce(
            g, x1b, AX.X, ALU.max, apply_absolute_value=True))
        P.do("vector", lambda e: e.drain())
        P.do("vector", lambda e, g=gxc: e.tensor_single_scalar(
            g, g, 1e-5, ALU.max))
        P.do("vector", lambda e: e.drain())
        P.do("vector", lambda e, g=gxc, s=s: e.tensor_single_scalar(
            ccol(C_GXI, s), g, 1.0 / 127.0, ALU.mult))
        P.do("vector", lambda e: e.drain())
        P.do("vector", lambda e, sc=sxc, s=s: e.reciprocal(
            sc, ccol(C_GXI, s)))
        P.do("vector", lambda e: e.drain())
        dv_prev = P.inc("vector", "DV", lambda e, sc=sxc:
                        e.tensor_single_scalar(lY, x1b, sc, ALU.mult))
        for j in range(8):
            src_c = lY if j == 0 else lC
            P.w("scalar", "DV", dv_prev)
            ac_j = P.inc("scalar", "AC", lambda e, src_c=src_c: e.activation(
                sgt, src_c, AF.Sign, bias=col(C_HALF)))
            P.w("vector", "AC", ac_j)
            dv_prev = P.inc("vector", "DV", lambda e, j=j, src_c=src_c:
                            e.scalar_tensor_tensor(lC, sgt, col(C_K0 + j),
                                                   src_c, ALU.mult, ALU.add))
        P.do("vector", lambda e: e.drain())
        P.do("vector", lambda e: e.tensor_tensor(res, lY, lC, ALU.subtract))
        P.do("vector", lambda e: e.drain())
        if s >= 1:
            P.w("vector", "PE", pe_xt_s)
        dv_xt = P.inc("vector", "DV", lambda e: e.tensor_scalar(
            xt_bf, res, 1.0, -0.5, ALU.mult, ALU.add))
        P.w("tensor", "DV", dv_xt)
        if s >= 1:
            P.w("tensor", "DV", dv_xi_done[s - 1])
        for k in range(16):
            fn = lambda e, k=k: e.transpose(
                p_tr[:, k * 128:(k + 1) * 128], xt_bf[:, k * 128:(k + 1) * 128],
                idb[:, :])
            if k == 15:
                pe_xt_s = P.inc("tensor", "PE", fn)
            else:
                P.do("tensor", fn)
        P.w("vector", "PE", pe_xt_s)
        ptr_view = p_tr[:, :].rearrange("p (k t) -> p k t", t=128)
        dv_xi_done[s] = P.inc(
            "vector", "DV",
            lambda e, s=s, o=xi_all[:, :, s * 128:(s + 1) * 128], i=ptr_view:
            e.tensor_copy(o, i))
        emit_passA(pa_sched[s])

    pe_after_p1 = P.val("PE")

    P.w("tensor", "GP", P.val("GP"))
    pe_g3 = {}
    for mi in range(3):
        pe_g3[mi] = P.inc("tensor", "PE", lambda e, mi=mi: e.matmul(
            p_b[0:1, 640 + mi:641 + mi], ccol(0, C_RA0 + mi), onesc[:, :],
            start=True, stop=True))
    P.w("vector", "PE", pe_g3[2])
    P.do("vector", lambda e: e.tensor_scalar(
        g11[:1, 0:3], p_b[0:1, 640:643], 1.0 / 16777216.0, 1e-5,
        ALU.mult, ALU.max))
    dv_g11 = P.inc("vector", "DV", lambda e: (e.drain(), e.engine_nop())[1])
    P.w("tensor", "DV", dv_g11)
    pe_gb = P.inc("tensor", "PE", lambda e: e.matmul(
        p_b[:, 644:647], ones1[:1, :], g11[:1, 0:3], start=True, stop=True))
    P.w("vector", "PE", pe_gb)
    for gi, (cth, csc, cis) in enumerate(
            [(C_SCG, C_SCG, C_ISG), (C_SCV, C_SCV, C_ISV), (C_SCO, C_SCO, -1)]):
        P.do("vector", lambda e, gi=gi, csc=csc: e.tensor_single_scalar(
            col(csc), p_b[:, 644 + gi:645 + gi], 0.5, ALU.mult))
        P.do("vector", lambda e: e.drain())
        if cis >= 0:
            P.do("vector", lambda e, csc=csc, cis=cis: e.reciprocal(
                col(cis), col(csc)))
            P.do("vector", lambda e: e.drain())
    TH = {}
    for mi, csc in ((0, C_SCG), (1, C_SCV), (2, C_SCO)):
        cp = C_SA0 + mi
        cn = C_SB0 + mi
        P.do("vector", lambda e, cp=cp, csc=csc: e.tensor_copy(
            col(cp), col(csc)))
        P.do("vector", lambda e: e.drain())
        P.do("vector", lambda e, cp=cp, cn=cn: e.tensor_single_scalar(
            col(cn), col(cp), -1.0, ALU.mult))
        P.do("vector", lambda e: e.drain())
        TH[mi] = (cp, cn)
    P.do("vector", lambda e: e.tensor_single_scalar(
        cols[:, C_SGC:C_SGC + 16], cols[:, C_GXI:C_GXI + 16], col(C_SCG),
        ALU.mult))
    P.do("vector", lambda e: e.drain())
    P.do("vector", lambda e: e.tensor_single_scalar(
        cols[:, C_SVC:C_SVC + 16], cols[:, C_GXI:C_GXI + 16], col(C_SCV),
        ALU.mult))
    P.do("vector", lambda e: e.drain())
    rows_g = lambda s: af(O_RES + s * 128, 128, 0, 1)
    rows_v = lambda s: af(O_X1 + s * 128, 128, 0, 1)
    P.w("vector", "SX0", st_x1[NS - 2])
    P.w("vector", "SX1", st_x1[NS - 1])
    for mi in range(2):
        base = C_SGC if mi == 0 else C_SVC
        rows = rows_g if mi == 0 else rows_v
        for s in range(NS):
            P.do("vector", lambda e, base=base, s=s: e.reciprocal(
                col(C_T0), ccol(base, s)))
            P.do("vector", lambda e: e.drain())
            dv_iv = P.inc("vector", "DV", lambda e: (e.drain(),
                                                     e.engine_nop())[1])
            P.w("tensor", "DV", dv_iv)
            pe_tv = P.inc("tensor", "PE", lambda e: e.transpose(
                p_b[0:1, 648:776], col(C_T0), idf[:, :]))
            P.w("vector", "PE", pe_tv)
            P.do("vector", lambda e: e.drain())
            P.inc("vector", "DV", lambda e, ro=rows(s): e.tensor_copy(
                ro, p_b[0:1, 648:776]))
    dv_rows = P.inc("vector", "DV", lambda e: (e.drain(), e.engine_nop())[1])

    wlb = [af(O_WL, DM), af(O_WL + 2048, DM)]
    pqb = af(O_PQ, DM)
    gs_slab = af(O_GS, 4096)
    sgh = [ab(B_SG, 1024), ab(B_SG + 1024, 1024)]
    oqb = ab(B_OQ, DM)
    gwq2 = [B[:, B_GWA:B_GWA + 8192], B[:, B_GWB:B_GWB + 8192]]
    hbuf2 = [af(O_UB, 512), af(O_UB + 512, 512)]
    brb = [af(O_BR, 512, 0, 1), af(O_BR + 512, 512, 0, 1)]
    P.do("vector", lambda e: e.memset(cols[:, C_MAC:C_MAC + 16], 0.0))
    dv_mz = P.inc("vector", "DV", lambda e: (e.drain(), e.engine_nop())[1])

    pq_ac = {}
    pq_st = {}
    chain = {"dv": 0}

    def emit_prequant(jcs):
        for jc in jcs:
            if jc >= NJC:
                continue
            if jc >= 1:
                P.w("sync", "AC", pq_ac[jc - 1])
            lq = P.inc("sync", "LQ%d" % (jc % 2), lambda e, jc=jc:
                       e.dma_start(pqb, flo[jc, :, :]), 16)
            P.w("scalar", "LQ%d" % (jc % 2), lq)
            for hh in range(2):
                sl = slice(hh * 1024, (hh + 1) * 1024)
                if chain["dv"]:
                    P.w("scalar", "DV", chain["dv"])
                P.do("scalar", lambda e, sl=sl: e.activation(
                    sgh[0], pqb[:, sl], AF.Sign, bias=col(C_SB0 + 2)))
                acq = P.inc("scalar", "AC", lambda e, sl=sl: e.activation(
                    sgh[1], pqb[:, sl], AF.Sign, bias=col(C_SA0 + 2)))
                P.w("vector", "AC", acq)
                if jc >= 1 and hh == 0:
                    P.w("vector", "SQ%d" % ((jc - 1) % 2), pq_st[jc - 1])
                dvq = P.inc("vector", "DV", lambda e, sl=sl: e.tensor_tensor(
                    oqb[:, sl], sgh[0], sgh[1], ALU.add))
                chain["dv"] = dvq
            pq_ac[jc] = acq
            P.w("sync", "DV", dvq)
            pq_st[jc] = P.inc("sync", "SQ%d" % (jc % 2),
                              lambda e, jc=jc: e.dma_start(oq_scr[jc, :, :],
                                                           oqb), 16)

    P.w("scalar", "DV", dv_rows)
    P.w("sync", "DV", dv_rows)
    P.w("sync", "PE", pe_after_p1)
    gwt_r = gwt_h.rearrange("(k p) x -> p k x", p=128)
    vwt_r = vwt_h.rearrange("(k p) x -> p k x", p=128)
    st_h = {}
    lw_ac = {}
    pe_slab_free = {}
    pe_burst = {}
    dv_quant = {}
    epi_done = {}
    lr_hist = {}
    pq_next = 0
    jq = 0
    tc = 0
    for hg in range(NHG):
        for mat in range(2):
            wr = gwt_r if mat == 0 else vwt_r
            cp, cn = TH[mat]
            wsl = gwq2[mat]
            for c in range(4):
                bp = jq % 2
                if jq >= 2:
                    P.w("sync", "AC", lw_ac[jq - 2])
                lw = P.inc("sync", "LW%d" % bp, lambda e, wr=wr, c=c, hg=hg,
                           bp=bp: e.dma_start(
                               wlb[bp].rearrange("p (k x) -> p k x", x=512),
                               wr[:, c * 4:c * 4 + 4,
                                  hg * 512:(hg + 1) * 512]), 16)
                P.w("scalar", "LW%d" % bp, lw)
                if hg >= 1 and c == 0:
                    P.w("vector", "PE", pe_slab_free[(hg - 1, mat)])
                for hh in range(2):
                    sl = slice(hh * 1024, (hh + 1) * 1024)
                    if chain["dv"]:
                        P.w("scalar", "DV", chain["dv"])
                    P.do("scalar", lambda e, bp=bp, sl=sl, cn=cn:
                         e.activation(sgh[0], wlb[bp][:, sl], AF.Sign,
                                      bias=col(cn)))
                    acs = P.inc("scalar", "AC", lambda e, bp=bp, sl=sl, cp=cp:
                                e.activation(sgh[1], wlb[bp][:, sl], AF.Sign,
                                             bias=col(cp)))
                    P.w("vector", "AC", acs)
                    dvs2 = P.inc("vector", "DV", lambda e, c=c, sl=sl,
                                 wsl=wsl: e.tensor_tensor(
                                     wsl[:, c * 2048 + sl.start:
                                         c * 2048 + sl.stop],
                                     sgh[0], sgh[1], ALU.add))
                    chain["dv"] = dvs2
                lw_ac[jq] = acs
                jq += 1
            dv_quant[(hg, mat)] = chain["dv"]
        if hg >= 1:
            P.w("sync", "PE", pe_burst[(hg - 1, 1, 1)])
        lrg = P.inc("sync", "LR0", lambda e, hg=hg: e.dma_start(
            brb[0], gbr_h[:, hg * 512:(hg + 1) * 512]), 16)
        lrv = P.inc("sync", "LR1", lambda e, hg=hg: e.dma_start(
            brb[1], vbr_h[:, hg * 512:(hg + 1) * 512]), 16)
        lr_hist[hg] = (lrg, lrv)
        for half in range(2):
            for mat in range(2):
                wsl = gwq2[mat]
                P.w("tensor", "DV", dv_quant[(hg, mat)])
                if hg == 0 and half == 0 and mat == 0:
                    P.w("tensor", "DV", dv_xi_done[NS - 1])
                    P.w("tensor", "DV", dv_mz)
                P.w("tensor", "LR%d" % mat, lr_hist[hg][mat])
                for i8 in range(8):
                    t = half * 8 + i8
                    bank = tc % 6
                    psl = (p_a[:, bank * 512:(bank + 1) * 512] if bank < 4
                           else p_b[:, (bank - 4) * 512:(bank - 3) * 512])
                    if tc >= 6:
                        kind, val = epi_done[tc - 6]
                        P.w("tensor", kind, val)
                    ro = (rows_g if mat == 0 else rows_v)(t)
                    P.do("tensor", lambda e, psl=psl, ro=ro, b=brb[mat]:
                         e.matmul(psl, ro, b, start=True, stop=False))
                    for k in range(NK):
                        fn = lambda e, k=k, t=t, psl=psl, wsl=wsl: e.matmul(
                            psl,
                            B[:, B_XI + k * 2048 + t * 128:
                               B_XI + k * 2048 + t * 128 + 128],
                            wsl[:, k * 512:(k + 1) * 512],
                            start=False, stop=(k == 15))
                        if k == 15:
                            pe_mm = P.inc("tensor", "PE", fn)
                        else:
                            P.do("tensor", fn)
                    slot = t % 8
                    if mat == 0:
                        P.w("scalar", "PE", pe_mm)
                        if hg > 0 or half > 0:
                            back = 8 if half == 1 else 16
                            P.w("scalar", "DV", epi_done[tc - back][1])
                        ac_e = P.inc(
                            "scalar", "AC", lambda e, slot=slot, psl=psl,
                            t=t: e.activation(
                                gs_slab[:, slot * 512:(slot + 1) * 512],
                                psl, AF.Silu, scale=ccol(C_SGC, t)))
                        epi_done[tc] = ("AC", ac_e)
                    else:
                        P.w("vector", "PE", pe_mm)
                        hp = t % 2
                        nvt = hg * 16 + t
                        if nvt >= 2:
                            P.w("vector", "SH%d" % hp,
                                st_h[((nvt - 2) // 16, (nvt - 2) % 16)])
                        P.w("vector", "AC", epi_done[tc - 8][1])
                        P.do("vector", lambda e: e.drain())
                        P.do("vector", lambda e, slot=slot, psl=psl, hp=hp,
                             t=t: e.scalar_tensor_tensor(
                                 hbuf2[hp], psl, ccol(C_SVC, t),
                                 gs_slab[:, slot * 512:(slot + 1) * 512],
                                 ALU.mult, ALU.mult))
                        P.do("vector", lambda e: e.drain())
                        P.do("vector", lambda e, hp=hp: e.tensor_reduce(
                            col(C_T0), hbuf2[hp], AX.X, ALU.max,
                            apply_absolute_value=True))
                        P.do("vector", lambda e: e.drain())
                        dv_h = P.inc("vector", "DV", lambda e, t=t:
                                     e.tensor_tensor(ccol(C_MAC, t),
                                                     ccol(C_MAC, t),
                                                     col(C_T0), ALU.max))
                        epi_done[tc] = ("DV", dv_h)
                        P.w("sync", "DV", dv_h)
                        st_h[(hg, t)] = P.inc(
                            "sync", "SH%d" % hp,
                            lambda e, t=t, hg=hg, hp=hp: e.dma_start(
                                h_scr[t, :, hg * 512:(hg + 1) * 512],
                                hbuf2[hp]), 16)
                    tc += 1
                pe_burst[(hg, half, mat)] = P.val("PE")
                pe_slab_free[(hg, mat)] = P.val("PE")
        emit_prequant(list(range(pq_next, pq_next + 4)))
        pq_next += 4

    P.w("vector", "AC", pq_ac[NJC - 1])
    P.do("vector", lambda e: e.drain())
    P.do("vector", lambda e: e.tensor_scalar(
        cols[:, C_MAC:C_MAC + 16], cols[:, C_MAC:C_MAC + 16], 1.0, 1e-5,
        ALU.mult, ALU.max))
    P.do("vector", lambda e: e.drain())
    P.do("vector", lambda e: e.tensor_scalar(
        cols[:, C_S2:C_S2 + 16], cols[:, C_MAC:C_MAC + 16], 1.0 / 127.0, 0.0,
        ALU.mult, ALU.add))
    P.do("vector", lambda e: e.drain())
    P.do("vector", lambda e: e.reciprocal(
        cols[:, C_MAC:C_MAC + 16], cols[:, C_S2:C_S2 + 16]))
    P.do("vector", lambda e: e.drain())
    P.do("vector", lambda e: e.tensor_single_scalar(
        cols[:, C_S2:C_S2 + 16], cols[:, C_S2:C_S2 + 16], col(C_SCO),
        ALU.mult))
    P.do("vector", lambda e: e.drain())
    rows_o = lambda s: af(O_PQ + s * 128, 128, 0, 1)
    for s in range(NS):
        P.do("vector", lambda e, s=s: e.reciprocal(col(C_T0), ccol(C_S2, s)))
        P.do("vector", lambda e: e.drain())
        dv_iv = P.inc("vector", "DV", lambda e: (e.drain(),
                                                 e.engine_nop())[1])
        P.w("tensor", "DV", dv_iv)
        if s == 0:
            P.w("tensor", "PE", pe_burst[(NHG - 1, 1, 1)])
        pe_tv = P.inc("tensor", "PE", lambda e: e.transpose(
            p_b[0:1, 648:776], col(C_T0), idf[:, :]))
        P.w("vector", "PE", pe_tv)
        P.do("vector", lambda e: e.drain())
        P.inc("vector", "DV", lambda e, ro=rows_o(s): e.tensor_copy(
            ro, p_b[0:1, 648:776]))
    dv_orows = P.inc("vector", "DV", lambda e: (e.drain(), e.engine_nop())[1])

    NE = 8
    hbq = [af(O_UB, DM), af(O_UT, DM)]
    lCc = [af(O_LC, DM), af(O_WL, DM)]
    lYc = [af(O_LY, DM), af(O_WL + 2048, DM)]
    sgc = [ab(B_SG, DM), ab(B_OQ, DM)]
    xtmp = af(O_RES, DM)
    x1r2 = [af(O_GS, DM), af(O_GS + 2048, DM)]
    z2 = [af(O_DP1, DM), af(O_X1, DM)]
    bro = af(O_BR, 1024, 0, 1)
    owqb = [B[:, B_GWA + i * 4096:B_GWA + (i + 1) * 4096] for i in range(2)]
    oq_r = oq_scr.rearrange("j p d -> p j d")
    hitH = [B[:, B_XI + i * 16384:B_XI + (i + 1) * 16384] for i in range(2)]
    hitH_v = [hitH[i].rearrange("p (jc t) -> p jc t", t=256) for i in range(2)]
    sq2 = ab(B_SG, DM)

    dv_lad = {}
    dv_evac = {}
    dv_evac_all = {}
    dv_z_round = {}
    st_out = {}
    dv_ln2 = {}
    lh_vals = {}
    lb_vals = {}
    lr_vals = {}
    lx_vals = {}
    tr_emitters = {}

    def emit_lh(e, p):
        pc = e * 8 + p
        c, ch = p // 2, p % 2
        st = 2 * e + ch
        bp = pc % 2
        if pc >= 2:
            P.w("sync", "PL", pc - 1)
        if pc == 0:
            P.w("sync", "SH0", P.val("SH0"))
            P.w("sync", "SH1", P.val("SH1"))
        lh_vals[pc] = P.inc("sync", "LH%d" % bp, lambda e_, st=st, c=c,
                            bp=bp: e_.dma_start(
                                hbq[bp],
                                h_scr[st, :, c * 2048:(c + 1) * 2048]), 16)

    def emit_p3_sync(e):
        em = e - 1
        if em >= 0:
            nd0 = em * 2
            if nd0 >= 1:
                P.w("sync", "PG", (nd0 - 1) * 16 + 1)
            lr_vals[nd0] = P.inc("sync", "LR%d" % (nd0 % 2), lambda e_:
                                 e_.dma_start(bro, obr_h[:, 0:1024]), 16)
            for t2 in range(2):
                st = 2 * em + t2
                if em > 0:
                    P.w("sync", "DV", dv_ln2[(em - 1, t2)])
                elif t2 == 0:
                    P.w("sync", "SX0", P.val("SX0"))
                    P.w("sync", "SX1", P.val("SX1"))
                lx_vals[(em, t2)] = P.inc(
                    "sync", "LX%d" % t2, lambda e_, st=st, t2=t2:
                    e_.dma_start(x1r2[t2], x1_scr[st, :, :]), 16)
            for g in range(16):
                nload = em * 32 + g
                if nload >= 2:
                    P.w("sync", "PG", nload - 1)
                if em == 0 and g == 0:
                    P.w("sync", "SQ0", P.val("SQ0"))
                    P.w("sync", "SQ1", P.val("SQ1"))
                lb_vals[(em, 0, g)] = P.inc(
                    "sync", "LB%d" % (g % 2), lambda e_, g=g: e_.dma_start(
                        owqb[g % 2].rearrange("p (j d) -> p j d", d=1024),
                        oq_r[:, g * 4:g * 4 + 4, 0:1024]), 16)
                if e < NE and 1 <= g <= 8:
                    emit_lh(e, g - 1)
            nd1 = em * 2 + 1
            P.w("sync", "PG", (nd1 - 1) * 16 + 1)
            lr_vals[nd1] = P.inc("sync", "LR%d" % (nd1 % 2), lambda e_:
                                 e_.dma_start(bro, obr_h[:, 1024:2048]), 16)
            for g in range(16):
                nload = em * 32 + 16 + g
                P.w("sync", "PG", nload - 1)
                lb_vals[(em, 1, g)] = P.inc(
                    "sync", "LB%d" % (g % 2), lambda e_, g=g: e_.dma_start(
                        owqb[g % 2].rearrange("p (j d) -> p j d", d=1024),
                        oq_r[:, g * 4:g * 4 + 4, 1024:2048]), 16)
        else:
            for p in range(8):
                emit_lh(e, p)

    def emit_ladder(e):
        hh = e % 2
        emitters = []
        for c in range(4):
            pcs = [e * 8 + c * 2, e * 8 + c * 2 + 1]
            for ch in range(2):
                pc = pcs[ch]
                st = 2 * e + ch
                bp = pc % 2
                P.w("vector", "LH%d" % bp, lh_vals[pc])
                if pc == 0:
                    P.w("vector", "DV", dv_orows)
                    P.w("vector", "SQ0", P.val("SQ0"))
                    P.w("vector", "SQ1", P.val("SQ1"))
                r2c = ccol(C_MAC, st)
                P.do("vector", lambda e_: e_.drain())
                P.inc("vector", "PL", lambda e_, bp=bp, r2c=r2c, lY=lYc[ch]:
                      e_.tensor_single_scalar(lY, hbq[bp], r2c, ALU.mult))
            dv_prev = [0, 0]
            for j in range(8):
                ac_j = [0, 0]
                for ch in range(2):
                    pc = pcs[ch]
                    src_c = lYc[ch] if j == 0 else lCc[ch]
                    if j == 0:
                        P.w("scalar", "PL", pc + 1)
                        if pc >= 2:
                            P.w("scalar", "PT", pc - 1)
                    else:
                        P.w("scalar", "DV", dv_prev[ch])
                    ac_j[ch] = P.inc("scalar", "AC",
                                     lambda e_, src_c=src_c, sg=sgc[ch]:
                                     e_.activation(sg, src_c, AF.Sign,
                                                   bias=col(C_HALF)))
                for ch in range(2):
                    src_c = lYc[ch] if j == 0 else lCc[ch]
                    P.w("vector", "AC", ac_j[ch])
                    dv_prev[ch] = P.inc(
                        "vector", "DV",
                        lambda e_, j=j, src_c=src_c, sg=sgc[ch], lC=lCc[ch]:
                        e_.scalar_tensor_tensor(lC, sg, col(C_K0 + j), src_c,
                                                ALU.mult, ALU.add))
            for ch in range(2):
                pc = pcs[ch]
                c_, ch_ = c, ch
                P.do("vector", lambda e_: e_.drain())
                P.do("vector", lambda e_, lY=lYc[ch], lC=lCc[ch]:
                     e_.tensor_tensor(xtmp, lY, lC, ALU.subtract))
                P.do("vector", lambda e_: e_.drain())
                dv_lad[pc] = P.inc("vector", "DV", lambda e_, sg=sgc[ch]:
                                   e_.tensor_scalar(sg, xtmp, 1.0, -0.5,
                                                    ALU.mult, ALU.add))

                def tr_emit(pc=pc, sg=sgc[ch]):
                    P.w("tensor", "DV", dv_lad[pc])
                    if pc >= 1:
                        P.w("tensor", "DV", dv_evac[pc - 1])
                    for kk in range(16):
                        fn = lambda e_, kk=kk, sg=sg: e_.transpose(
                            p_tr[:, kk * 128:(kk + 1) * 128],
                            sg[:, kk * 128:(kk + 1) * 128], idb[:, :])
                        if kk == 15:
                            P.inc("tensor", "PT", fn)
                        else:
                            P.do("tensor", fn)
                emitters.append(tr_emit)
                P.w("vector", "PT", pc + 1)
                if e >= 2 and pc == e * 8:
                    P.w("vector", "PG", (e - 1) * 32)
                ptr_view = p_tr[:, :].rearrange("p (k t) -> p k t", t=128)
                dv_evac[pc] = P.inc(
                    "vector", "DV",
                    lambda e_, c=c_, ch=ch_, hh=hh, i=ptr_view:
                    e_.tensor_copy(hitH_v[hh][:, c * 16:(c + 1) * 16,
                                              ch * 128:(ch + 1) * 128], i))
        dv_evac_all[e] = P.val("DV")
        tr_emitters[e] = emitters

    def emit_mm(em, emitters):
        hh = em % 2
        spl = 0
        for r in range(2):
            nd = em * 2 + r
            for grp in range(16):
                gbp = grp % 2
                if r == 0 and grp % 2 == 1 and spl < len(emitters):
                    emitters[spl]()
                    spl += 1
                P.w("tensor", "LB%d" % gbp, lb_vals[(em, r, grp)])
                if grp == 0:
                    P.w("tensor", "LR%d" % (nd % 2), lr_vals[nd])
                    if r == 0:
                        P.w("tensor", "DV", dv_evac_all[em])
                    if r >= 1:
                        P.w("tensor", "DV", dv_z_round[(em, 0)])
                    elif em > 0:
                        P.w("tensor", "DV", dv_z_round[(em - 1, 1)])
                    for t2 in range(2):
                        st = 2 * em + t2
                        for di in range(2):
                            bank = t2 * 2 + di
                            P.do("tensor", lambda e_, bank=bank, st=st,
                                 di=di: e_.matmul(
                                     p_a[:, bank * 512:(bank + 1) * 512],
                                     rows_o(st),
                                     bro[:, di * 512:(di + 1) * 512],
                                     start=True, stop=False))
                for jj in range(4):
                    jc = grp * 4 + jj
                    for t2 in range(2):
                        for di in range(2):
                            bank = t2 * 2 + di
                            fn = lambda e_, jc=jc, t2=t2, di=di, bank=bank, \
                                gbp=gbp, jj=jj, hh=hh: e_.matmul(
                                    p_a[:, bank * 512:(bank + 1) * 512],
                                    hitH[hh][:, jc * 256 + t2 * 128:
                                             jc * 256 + t2 * 128 + 128],
                                    owqb[gbp][:, jj * 1024 + di * 512:
                                              jj * 1024 + di * 512 + 512],
                                    start=False, stop=(jc == NJC - 1))
                            if jj == 3 and t2 == 1 and di == 1:
                                P.inc("tensor", "PG", fn)
                            else:
                                P.do("tensor", fn)
            for t2 in range(2):
                st = 2 * em + t2
                for di in range(2):
                    dmg = r * 2 + di
                    bank = t2 * 2 + di
                    P.w("vector", "PG", nd * 16 + 16)
                    if r == 0 and di == 0:
                        P.w("vector", "LX%d" % t2, lx_vals[(em, t2)])
                        if em > 0:
                            P.w("vector", "SO%d" % ((2 * (em - 1) + t2) % 2),
                                st_out[2 * (em - 1) + t2])
                    s2c = ccol(C_S2, st)
                    P.do("vector", lambda e_: e_.drain())
                    dv_z = P.inc("vector", "DV",
                                 lambda e_, t2=t2, dmg=dmg, s2c=s2c,
                                 bank=bank:
                                 e_.scalar_tensor_tensor(
                                     z2[t2][:, dmg * 512:(dmg + 1) * 512],
                                     p_a[:, bank * 512:(bank + 1) * 512],
                                     s2c,
                                     x1r2[t2][:, dmg * 512:(dmg + 1) * 512],
                                     ALU.mult, ALU.add))
            dv_z_round[(em, r)] = dv_z
        for t2 in range(2):
            st = 2 * em + t2
            P.w("vector", "DV", dv_z_round[(em, 1)])
            zt = z2[t2]
            mcol, vcol, rcol = col(C_T0), col(C_T1), col(C_T3)
            P.do("vector", lambda e_: e_.drain())
            P.do("vector", lambda e_, zt=zt: e_.tensor_reduce(mcol, zt, AX.X,
                                                              ALU.add))
            P.do("vector", lambda e_: e_.drain())
            P.do("vector", lambda e_: e_.tensor_single_scalar(mcol, mcol,
                                                              INV_D,
                                                              ALU.mult))
            P.do("vector", lambda e_: e_.drain())
            dv_l2 = P.inc("vector", "DV", lambda e_, zt=zt:
                          e_.tensor_single_scalar(zt, zt, mcol,
                                                  ALU.subtract))
            P.w("scalar", "DV", dv_l2)
            P.do("scalar", lambda e_, zt=zt: e_.activation(sq2, zt, AF.Square,
                                                           accum_out=vcol))
            P.do("scalar", lambda e_: e_.drain())
            P.do("scalar", lambda e_: e_.activation(col(C_T2), vcol, AF.Ln,
                                                    bias=col(C_EPS),
                                                    scale=INV_D))
            P.do("scalar", lambda e_: e_.drain())
            ac_l2 = P.inc("scalar", "AC", lambda e_: e_.activation(
                rcol, col(C_T2), AF.Exp, scale=-0.5))
            P.w("vector", "AC", ac_l2)
            P.do("vector", lambda e_: e_.drain())
            dv_o = P.inc("vector", "DV", lambda e_, zt=zt:
                         e_.tensor_single_scalar(zt, zt, rcol, ALU.mult))
            dv_ln2[(em, t2)] = dv_o
            P.w("sync", "DV", dv_o)
            st_out[st] = P.inc("sync", "SO%d" % (st % 2),
                               lambda e_, st=st, zt=zt: e_.dma_start(
                                   out_h[st * 128:(st + 1) * 128, :], zt), 16)

    P.w("sync", "PE", pe_burst[(NHG - 1, 1, 1)])
    for e in range(NE + 1):
        emit_p3_sync(e)
        if e < NE:
            emit_ladder(e)
        if e == 0:
            for em_ in tr_emitters[0]:
                em_()
            tr_emitters[0] = []
        if e >= 1:
            emit_mm(e - 1, tr_emitters[e] if e < NE else [])

    P.w("sync", "SO0", P.val("SO0"))
    P.w("sync", "SO1", P.val("SO1"))

    with nc.Block() as block:
        @block.sync
        def _(eng):
            for f in P.ops["sync"]:
                f(eng)

        @block.tensor
        def _(eng):
            for f in P.ops["tensor"]:
                f(eng)

        @block.vector
        def _(eng):
            for f in P.ops["vector"]:
                f(eng)

        @block.scalar
        def _(eng):
            for f in P.ops["scalar"]:
                f(eng)

        @block.gpsimd
        def _(eng):
            for f in P.ops["gpsimd"]:
                f(eng)

    ctx.close()
    return nc


_NC_CACHE = None


def kernel(L0_attn_u, L0_attn_A_factors, L0_attn_W_b_rank, L0_attn_W_b_state,
           L0_attn_W_c_rank, L0_attn_W_c_model, L0_attn_D_vec,
           L0_ffn_gate_w, L0_ffn_gate_b, L0_ffn_val_w, L0_ffn_val_b,
           L0_ffn_out_w, L0_ffn_out_b):
    global _NC_CACHE
    u = np.ascontiguousarray(np.asarray(L0_attn_u, dtype=np.float32).reshape(-1, DM))
    gw = np.asarray(L0_ffn_gate_w, dtype=np.float32)
    vw = np.asarray(L0_ffn_val_w, dtype=np.float32)
    ow = np.asarray(L0_ffn_out_w, dtype=np.float32)
    shared = dict(
        gwt=np.ascontiguousarray(gw.T),
        vwt=np.ascontiguousarray(vw.T),
        owt=np.ascontiguousarray(ow.T),
        gbr=np.ascontiguousarray(np.asarray(L0_ffn_gate_b,
                                            dtype=np.float32).reshape(1, HID)),
        vbr=np.ascontiguousarray(np.asarray(L0_ffn_val_b,
                                            dtype=np.float32).reshape(1, HID)),
        obr=np.ascontiguousarray(np.asarray(L0_ffn_out_b,
                                            dtype=np.float32).reshape(1, DM)),
        wbr=np.ascontiguousarray(np.asarray(L0_attn_W_b_rank, dtype=np.float32)),
        wbs=np.ascontiguousarray(np.asarray(L0_attn_W_b_state, dtype=np.float32)),
        wcr=np.ascontiguousarray(np.asarray(L0_attn_W_c_rank, dtype=np.float32)),
        wcm=np.ascontiguousarray(np.asarray(L0_attn_W_c_model, dtype=np.float32)),
        afac=np.ascontiguousarray(
            np.asarray(L0_attn_A_factors, dtype=np.float32).reshape(1, 32)),
        dvec=np.ascontiguousarray(
            np.asarray(L0_attn_D_vec, dtype=np.float32).reshape(1, DM)),
        idf=np.eye(128, dtype=np.float32),
        idb=np.eye(128, dtype=np.float32).astype(BF),
    )
    if _NC_CACHE is None:
        _NC_CACHE = build_program()
    in_maps = []
    for c in range(NCORE):
        uc = np.ascontiguousarray(u[c * TOK:(c + 1) * TOK, :])
        in_maps.append(dict(u=uc, ut=np.ascontiguousarray(uc.T), **shared))
    res = run_bass_kernel_spmd(_NC_CACHE, in_maps, core_ids=list(range(NCORE)))
    out = np.concatenate(
        [np.asarray(r["out"], dtype=np.float32) for r in res.results], axis=0)
    return out.reshape(4, 4096, DM)


# version 09
# speedup vs baseline: 1.4267x; 1.0323x over previous
import contextlib
import numpy as np
import ml_dtypes
import concourse.bass as bass
import concourse.mybir as mybir
from concourse.bass_utils import run_bass_kernel_spmd

F32 = mybir.dt.float32
BF16 = mybir.dt.bfloat16
BF = ml_dtypes.bfloat16
AF = mybir.ActivationFunctionType
ALU = mybir.AluOpType
AX = mybir.AxisListType

NCORE = 8
TOK = 2048
DM = 2048
HID = 8192
NS = 16
NHG = 16
NK = 16
NJC = 64
NQ = 4
INV_D = 1.0 / DM
EPS = 1e-5
LADK = (128.0, 64.0, 32.0, 16.0, 8.0, 4.0, 2.0, 1.0)

O_COLS = 0
O_DP1 = 128
O_UB = 2176
O_UT = 4224
O_RES = 6272
O_X1 = 8320
O_LC = 10368
O_LY = 12416
O_GS = 14464
O_WL = 18560
O_PQ = 22656
O_BR = 24704
A_W = 25728
B_XI = 0
B_GWA = 32768
B_GWB = 40960
B_SG = 49152
B_OQ = 51200
B_W = 53248

(C_EPS, C_HALF, C_T0, C_T1, C_T2, C_T3, C_SCG, C_SCV, C_SCO,
 C_ISG, C_ISV, C_RA0, C_RA1, C_RA2, C_SA0, C_SA1, C_SA2, C_SB0, C_SB1,
 C_SB2, C_K0) = range(21)
C_MAC = 32
C_SXS = 48
C_SGC = 64
C_SVC = 80
C_S2 = 96
C_GXI = 112
NCOL = 128


class Prog:
    ENGS = ("sync", "tensor", "vector", "scalar", "gpsimd")

    def __init__(self):
        self.ops = {e: [] for e in self.ENGS}
        self.cnt = {}
        self.sems = {}

    def w(self, eng, sname, val):
        if val <= 0:
            return
        self.ops[eng].append(lambda e, s=self.sems[sname], v=val: e.wait_ge(s, v))

    def do(self, eng, fn):
        self.ops[eng].append(fn)

    def inc(self, eng, sname, fn, amt=1):
        self.cnt[sname] = self.cnt.get(sname, 0) + amt
        self.ops[eng].append(
            lambda e, f=fn, s=self.sems[sname], a=amt: f(e).then_inc(s, a))
        return self.cnt[sname]

    def val(self, sname):
        return self.cnt.get(sname, 0)


def build_program(debug=False):
    nc = bass.Bass("TRN2", target_bir_lowering=False)

    u_h = nc.dram_tensor("u", [TOK, DM], F32, kind="ExternalInput").ap()
    ut_h = nc.dram_tensor("ut", [DM, TOK], F32, kind="ExternalInput").ap()
    gwt_h = nc.dram_tensor("gwt", [DM, HID], F32, kind="ExternalInput").ap()
    vwt_h = nc.dram_tensor("vwt", [DM, HID], F32, kind="ExternalInput").ap()
    owt_h = nc.dram_tensor("owt", [HID, DM], F32, kind="ExternalInput").ap()
    gbr_h = nc.dram_tensor("gbr", [1, HID], F32, kind="ExternalInput").ap()
    vbr_h = nc.dram_tensor("vbr", [1, HID], F32, kind="ExternalInput").ap()
    obr_h = nc.dram_tensor("obr", [1, DM], F32, kind="ExternalInput").ap()
    wbr_h = nc.dram_tensor("wbr", [DM, 64], F32, kind="ExternalInput").ap()
    wbs_h = nc.dram_tensor("wbs", [64, 256], F32, kind="ExternalInput").ap()
    wcr_h = nc.dram_tensor("wcr", [256, 64], F32, kind="ExternalInput").ap()
    wcm_h = nc.dram_tensor("wcm", [64, DM], F32, kind="ExternalInput").ap()
    a_h = nc.dram_tensor("afac", [1, 32], F32, kind="ExternalInput").ap()
    d_h = nc.dram_tensor("dvec", [1, DM], F32, kind="ExternalInput").ap()
    idf_h = nc.dram_tensor("idf", [128, 128], F32, kind="ExternalInput").ap()
    idb_h = nc.dram_tensor("idb", [128, 128], BF16, kind="ExternalInput").ap()
    out_h = nc.dram_tensor("out", [TOK, DM], F32, kind="ExternalOutput").ap()
    scr_kind = "ExternalOutput" if debug else "Internal"
    x1_scr = nc.dram_tensor("x1_scr", [NS, 128, DM], F32, kind=scr_kind).ap()
    h_scr = nc.dram_tensor("h_scr", [NS, 128, HID], F32, kind=scr_kind).ap()
    oq_scr = nc.dram_tensor("oq_scr", [NJC, 128, DM], BF16, kind=scr_kind).ap()

    ctx = contextlib.ExitStack()
    A = ctx.enter_context(nc.sbuf_tensor("arena_f32", [128, A_W], F32))
    B = ctx.enter_context(nc.sbuf_tensor("arena_bf16", [128, B_W], BF16))

    def af(off, w, p0=0, p1=128):
        return A[p0:p1, off:off + w]

    def ab(off, w, p0=0, p1=128):
        return B[p0:p1, off:off + w]

    idf = ctx.enter_context(nc.sbuf_tensor("idf_s", [128, 128], F32))
    idb = ctx.enter_context(nc.sbuf_tensor("idb_s", [128, 128], BF16))
    ones1 = ctx.enter_context(nc.sbuf_tensor("ones1", [1, 128], F32))
    onesc = ctx.enter_context(nc.sbuf_tensor("onesc", [128, 1], F32))
    gxs = ctx.enter_context(nc.sbuf_tensor("gxs", [128, NS], F32))
    g11 = ctx.enter_context(nc.sbuf_tensor("g11", [1, 8], F32))

    p_a = ctx.enter_context(nc.psum_tensor("p_a", [128, 2048], F32))
    p_b = ctx.enter_context(nc.psum_tensor("p_b", [128, 1024], F32))
    p_tr = ctx.enter_context(nc.psum_tensor("p_tr", [128, 2048], BF16))

    P = Prog()
    semnames = ["PE", "DV", "AC", "GP", "L0",
                "LU0", "LU1", "LT0", "LT1", "LA0", "LA1", "LW0", "LW1",
                "LQ0", "LQ1", "LH0", "LH1", "LB0", "LB1", "LX0", "LX1",
                "LR0", "LR1", "PT", "PG", "PL",
                "SX0", "SX1", "SH0", "SH1", "SQ0", "SQ1", "SO0", "SO1"]
    for nm in semnames:
        P.sems[nm] = ctx.enter_context(nc.semaphore("s" + nm))

    cols = af(O_COLS, NCOL)

    def col(i):
        return A[:, O_COLS + i:O_COLS + i + 1]

    def ccol(base, i):
        return A[:, O_COLS + base + i:O_COLS + base + i + 1]

    M_ = O_WL
    wbrT = af(M_, 1024)
    wcm_f = af(M_ + 1024, DM, 0, 64)
    G_f = af(M_ + 3072, 64, 0, 64)
    t1b = af(M_ + 3136, 128, 0, 64)
    t2b = af(M_ + 3264, 128, 0, 64)
    wbs_t = af(O_PQ, 256, 0, 64)
    bb0 = af(O_PQ + 256, 256, 0, 64)
    bb1 = af(O_PQ + 512, 256, 0, 64)
    tmp256 = af(O_PQ + 768, 256, 0, 64)
    a_bc = af(O_PQ + 1024, 32, 0, 64)
    a_row = af(O_PQ + 1056, 32, 0, 1)
    btT = af(O_PQ + 1088, 128)
    wcr0 = af(O_PQ + 1216, 64)
    wcr1 = af(O_PQ + 1280, 64)
    d_row = af(O_RES, DM, 0, 1)
    dp1 = af(O_DP1, DM)

    ld0 = [(idf[:, :], idf_h[:, :]), (idb[:, :], idb_h[:, :]),
           (d_row, d_h[:, :]), (wbs_t, wbs_h[:, :]), (a_row, a_h[:, :]),
           (wcm_f, wcm_h[:, :]),
           (wbrT.rearrange("p (k r) -> p k r", r=64),
            wbr_h.rearrange("(k p) r -> p k r", p=128)),
           (wcr0, wcr_h[0:128, :]), (wcr1, wcr_h[128:256, :])]
    for dst, src in ld0:
        P.inc("sync", "L0", lambda e, d=dst, s=src: e.dma_start(d, s), 16)
    ld0_done = P.val("L0")

    P.do("vector", lambda e: e.memset(col(C_EPS), float(EPS)))
    P.do("vector", lambda e: e.memset(col(C_HALF), 0.5))
    for j, kk in enumerate(LADK):
        P.do("vector", lambda e, j=j, kk=kk: e.memset(col(C_K0 + j), -kk / 2.0))
    P.do("vector", lambda e: e.memset(ones1[:1, :], 1.0))
    P.do("vector", lambda e: e.memset(onesc[:, :], 1.0))
    P.do("vector", lambda e: e.memset(cols[:, C_RA0:C_RA0 + 3], 0.0))
    P.w("vector", "L0", ld0_done)
    dv_c0 = P.inc("vector", "DV", lambda e: (e.drain(), e.engine_nop())[1])

    P.w("tensor", "DV", dv_c0)
    pe_abc = P.inc("tensor", "PE", lambda e: e.matmul(
        p_b[0:128, 0:32], ones1[:1, 0:128], a_row, start=True, stop=True))
    P.w("vector", "PE", pe_abc)
    P.do("vector", lambda e: e.tensor_copy(a_bc, p_b[0:64, 0:32]))
    P.do("vector", lambda e: e.drain())
    P.do("vector", lambda e: e.tensor_copy(bb0, wbs_t))
    cur, nxt = bb0, bb1
    for lvl in range(8):
        wwid = 2 ** (7 - lvl)
        vi = cur.rearrange("p (b t w) -> p b t w", t=2, w=wwid)
        vo = nxt.rearrange("p (b t w) -> p b t w", t=2, w=wwid)
        vt = tmp256.rearrange("p (b t w) -> p b t w", t=2, w=wwid)
        for j in range(2):
            a0 = a_bc[:, lvl * 4 + j: lvl * 4 + j + 1]
            a1 = a_bc[:, lvl * 4 + 2 + j: lvl * 4 + 3 + j]
            P.do("vector", lambda e: e.drain())
            P.do("vector", lambda e, o=vt[:, :, j, :], i=vi[:, :, 0, :], s=a0:
                 e.tensor_single_scalar(o, i, s, ALU.mult))
            P.do("vector", lambda e: e.drain())
            P.do("vector", lambda e, o=vo[:, :, j, :], i=vi[:, :, 1, :], s=a1,
                 t=vt[:, :, j, :]:
                 e.scalar_tensor_tensor(o, i, s, t, ALU.mult, ALU.add))
        cur, nxt = nxt, cur
    dv_bfly = P.inc("vector", "DV", lambda e: (e.drain(), e.engine_nop())[1])

    P.w("tensor", "DV", dv_bfly)
    P.do("tensor", lambda e, i=cur: e.transpose(p_b[:, 0:64], i[:, 0:128],
                                                idf[0:64, 0:64]))
    pe_btt = P.inc("tensor", "PE", lambda e, i=cur: e.transpose(
        p_b[:, 64:128], i[:, 128:256], idf[0:64, 0:64]))
    P.w("vector", "PE", pe_btt)
    dv_btt = P.inc("vector", "DV", lambda e: e.tensor_copy(btT, p_b[:, 0:128]))
    P.w("tensor", "DV", dv_btt)
    P.do("tensor", lambda e: e.matmul(p_b[0:64, 256:320], btT[0:128, 0:64],
                                      wcr0, start=True, stop=False))
    pe_g = P.inc("tensor", "PE", lambda e: e.matmul(
        p_b[0:64, 256:320], btT[0:128, 64:128], wcr1, start=False, stop=True))
    P.w("vector", "PE", pe_g)
    dv_g = P.inc("vector", "DV", lambda e: e.tensor_copy(G_f, p_b[0:64, 256:320]))

    P.w("tensor", "DV", dv_g)
    dv_dpe = {}
    for i in range(4):
        if i >= 2:
            P.w("tensor", "DV", dv_dpe[i - 2])
        pe_d = P.inc("tensor", "PE", lambda e, i=i: e.matmul(
            p_a[:, (i % 2) * 512:(i % 2) * 512 + 512], ones1[:1, :],
            d_row[:, i * 512:(i + 1) * 512], start=True, stop=True))
        P.w("vector", "PE", pe_d)
        dv_dpe[i] = P.inc("vector", "DV", lambda e, i=i: e.tensor_scalar(
            dp1[:, i * 512:(i + 1) * 512],
            p_a[:, (i % 2) * 512:(i % 2) * 512 + 512],
            1.0, 1.0, ALU.mult, ALU.add))
    dv_dp1 = P.inc("vector", "DV", lambda e: (e.drain(), e.engine_nop())[1])

    ubuf = af(O_UB, DM)
    utb = af(O_UT, DM)
    res = af(O_RES, DM)
    x1b = af(O_X1, DM)
    lC = af(O_LC, DM)
    lY = af(O_LY, DM)
    sgt = ab(B_SG, DM)
    xt_bf = ab(B_OQ, DM)
    xi_all = B[:, B_XI:B_XI + NS * DM].rearrange("p (k t) -> p k t", t=TOK)
    pA_w = [af(O_GS, DM), af(O_GS + 2048, DM)]

    pa_chunks = []
    for mi, wh in ((0, gwt_h), (1, vwt_h)):
        fl = wh.rearrange("(c p) x -> c p x", p=128)
        for c in range(16):
            for s4 in range(4):
                pa_chunks.append((fl[c, :, s4 * 2048:(s4 + 1) * 2048], mi))
    flo = owt_h.rearrange("(c p) x -> c p x", p=128)
    for c in range(64):
        pa_chunks.append((flo[c, :, :], 2))
    NPA = len(pa_chunks)
    per = (NPA + NS - 1) // NS
    pa_sched = [list(range(s * per, min((s + 1) * per, NPA))) for s in range(NS)]

    la_ac = {}
    gp_add = {}
    abs_scr = [ab(B_GWA, DM), ab(B_GWA + 2048, DM)]

    def emit_passA(idxs):
        for j in idxs:
            src, mi = pa_chunks[j]
            bp = j % 2
            if j >= 2:
                P.w("sync", "AC", la_ac[j - 2])
            la = P.inc("sync", "LA%d" % bp,
                       lambda e, d=pA_w[bp], s=src: e.dma_start(d, s), 16)
            P.w("scalar", "LA%d" % bp, la)
            if j >= 2:
                P.w("scalar", "GP", gp_add[j - 2])
            acc = ccol(0, (C_SA0 if j % 2 == 0 else C_SB0) + mi)
            la_ac[j] = P.inc("scalar", "AC", lambda e, bp=bp, acc=acc, j=j:
                             e.activation(abs_scr[j % 2], pA_w[bp], AF.Abs,
                                          accum_out=acc))
            P.w("gpsimd", "AC", la_ac[j])
            gp_add[j] = P.inc("gpsimd", "GP", lambda e, mi=mi, acc=acc:
                              e.tensor_tensor(ccol(0, C_RA0 + mi),
                                              ccol(0, C_RA0 + mi), acc,
                                              ALU.add))
            P.do("gpsimd", lambda e: e.drain())

    st_x1 = {}
    dv_xi_done = {}
    dv_ures_s = {}
    pe_t1_s = {}
    dv_resadd = {}
    for s in range(NS):
        bp = s % 2
        if s >= 1:
            P.w("sync", "DV", dv_ures_s[s - 1])
        lu = P.inc("sync", "LU%d" % bp, lambda e, s=s: e.dma_start(
            ubuf, u_h[s * 128:(s + 1) * 128, :]), 16)
        if s >= 1:
            P.w("sync", "PE", pe_t1_s[s - 1])
        lt = P.inc("sync", "LT%d" % bp, lambda e, s=s: e.dma_start(
            utb.rearrange("p (k t) -> p k t", t=128),
            ut_h.rearrange("(k p) t -> p k t", p=128)
            [:, :, s * 128:(s + 1) * 128]), 16)
        P.w("tensor", "LT%d" % bp, lt)
        if s == 0:
            P.w("tensor", "DV", dv_dp1)
        for k in range(NK):
            fn = lambda e, k=k: e.matmul(
                p_b[0:64, 0:128], wbrT[:, k * 64:(k + 1) * 64],
                utb[:, k * 128:(k + 1) * 128], start=(k == 0), stop=(k == 15))
            if k == 15:
                pe_t1 = P.inc("tensor", "PE", fn)
            else:
                P.do("tensor", fn)
        pe_t1_s[s] = pe_t1
        P.w("vector", "PE", pe_t1)
        dv_t1 = P.inc("vector", "DV", lambda e: e.tensor_copy(
            t1b, p_b[0:64, 0:128]))
        P.w("tensor", "DV", dv_t1)
        pe_t2 = P.inc("tensor", "PE", lambda e: e.matmul(
            p_b[0:64, 512:640], G_f, t1b[0:64, :], start=True, stop=True))
        P.w("vector", "PE", pe_t2)
        dv_t2 = P.inc("vector", "DV", lambda e: e.tensor_copy(
            t2b, p_b[0:64, 512:640]))
        P.w("tensor", "DV", dv_t2)
        pe_y = {}
        for g in range(4):
            if s >= 1:
                P.w("tensor", "DV", dv_resadd[(s - 1, g)])
            pe_y[g] = P.inc("tensor", "PE", lambda e, g=g: e.matmul(
                p_a[:, g * 512:(g + 1) * 512], t2b[0:64, :],
                wcm_f[:, g * 512:(g + 1) * 512], start=True, stop=True))
        P.w("vector", "LU%d" % bp, lu)
        P.do("vector", lambda e: e.drain())
        P.do("vector", lambda e: e.tensor_tensor(res, ubuf, dp1, ALU.mult))
        dv_ures_s[s] = P.inc("vector", "DV",
                             lambda e: (e.drain(), e.engine_nop())[1])
        for g in range(4):
            P.w("vector", "PE", pe_y[g])
            P.do("vector", lambda e: e.drain())
            dv_resadd[(s, g)] = P.inc("vector", "DV", lambda e, g=g:
                                      e.tensor_tensor(
                                          res[:, g * 512:(g + 1) * 512],
                                          res[:, g * 512:(g + 1) * 512],
                                          p_a[:, g * 512:(g + 1) * 512],
                                          ALU.add))
        mcol, vcol, rcol = col(C_T0), col(C_T1), col(C_T3)
        P.do("vector", lambda e: e.drain())
        P.do("vector", lambda e: e.tensor_reduce(mcol, res, AX.X, ALU.add))
        P.do("vector", lambda e: e.drain())
        P.do("vector", lambda e: e.tensor_single_scalar(mcol, mcol, INV_D,
                                                        ALU.mult))
        P.do("vector", lambda e: e.drain())
        dv_ln = P.inc("vector", "DV", lambda e: e.tensor_single_scalar(
            res, res, mcol, ALU.subtract))
        P.w("scalar", "DV", dv_ln)
        P.do("scalar", lambda e: e.activation(sgt, res, AF.Square,
                                              accum_out=vcol))
        P.do("scalar", lambda e: e.drain())
        P.do("scalar", lambda e: e.activation(col(C_T2), vcol, AF.Ln,
                                              bias=col(C_EPS), scale=INV_D))
        P.do("scalar", lambda e: e.drain())
        ac_ln = P.inc("scalar", "AC", lambda e: e.activation(
            rcol, col(C_T2), AF.Exp, scale=-0.5))
        P.w("vector", "AC", ac_ln)
        if s >= 1:
            P.w("vector", "SX%d" % ((s - 1) % 2), st_x1[s - 1])
        P.do("vector", lambda e: e.drain())
        dv_x1 = P.inc("vector", "DV", lambda e: e.tensor_single_scalar(
            x1b, res, rcol, ALU.mult))
        P.w("sync", "DV", dv_x1)
        st_x1[s] = P.inc("sync", "SX%d" % bp, lambda e, s=s: e.dma_start(
            x1_scr[s, :, :], x1b), 16)
        gxc = gxs[:, s:s + 1]
        sxc = ccol(C_SXS, s)
        P.do("vector", lambda e: e.drain())
        P.do("vector", lambda e, g=gxc: e.tensor_reduce(
            g, x1b, AX.X, ALU.max, apply_absolute_value=True))
        P.do("vector", lambda e: e.drain())
        P.do("vector", lambda e, g=gxc: e.tensor_single_scalar(
            g, g, 1e-5, ALU.max))
        P.do("vector", lambda e: e.drain())
        P.do("vector", lambda e, g=gxc, s=s: e.tensor_single_scalar(
            ccol(C_GXI, s), g, 1.0 / 127.0, ALU.mult))
        P.do("vector", lambda e: e.drain())
        P.do("vector", lambda e, sc=sxc, s=s: e.reciprocal(
            sc, ccol(C_GXI, s)))
        P.do("vector", lambda e: e.drain())
        dv_prev = P.inc("vector", "DV", lambda e, sc=sxc:
                        e.tensor_single_scalar(lY, x1b, sc, ALU.mult))
        for j in range(8):
            src_c = lY if j == 0 else lC
            P.w("scalar", "DV", dv_prev)
            ac_j = P.inc("scalar", "AC", lambda e, src_c=src_c: e.activation(
                sgt, src_c, AF.Sign, bias=col(C_HALF)))
            P.w("vector", "AC", ac_j)
            dv_prev = P.inc("vector", "DV", lambda e, j=j, src_c=src_c:
                            e.scalar_tensor_tensor(lC, sgt, col(C_K0 + j),
                                                   src_c, ALU.mult, ALU.add))
        P.do("vector", lambda e: e.drain())
        P.do("vector", lambda e: e.tensor_tensor(res, lY, lC, ALU.subtract))
        P.do("vector", lambda e: e.drain())
        if s >= 1:
            P.w("vector", "PE", pe_xt_s)
        dv_xt = P.inc("vector", "DV", lambda e: e.tensor_scalar(
            xt_bf, res, 1.0, -0.5, ALU.mult, ALU.add))
        P.w("tensor", "DV", dv_xt)
        if s >= 1:
            P.w("tensor", "DV", dv_xi_done[s - 1])
        for k in range(16):
            fn = lambda e, k=k: e.transpose(
                p_tr[:, k * 128:(k + 1) * 128], xt_bf[:, k * 128:(k + 1) * 128],
                idb[:, :])
            if k == 15:
                pe_xt_s = P.inc("tensor", "PE", fn)
            else:
                P.do("tensor", fn)
        P.w("vector", "PE", pe_xt_s)
        ptr_view = p_tr[:, :].rearrange("p (k t) -> p k t", t=128)
        dv_xi_done[s] = P.inc(
            "vector", "DV",
            lambda e, s=s, o=xi_all[:, :, s * 128:(s + 1) * 128], i=ptr_view:
            e.tensor_copy(o, i))
        emit_passA(pa_sched[s])

    pe_after_p1 = P.val("PE")

    P.w("tensor", "GP", P.val("GP"))
    pe_g3 = {}
    for mi in range(3):
        pe_g3[mi] = P.inc("tensor", "PE", lambda e, mi=mi: e.matmul(
            p_b[0:1, 640 + mi:641 + mi], ccol(0, C_RA0 + mi), onesc[:, :],
            start=True, stop=True))
    P.w("vector", "PE", pe_g3[2])
    P.do("vector", lambda e: e.tensor_scalar(
        g11[:1, 0:3], p_b[0:1, 640:643], 1.0 / 16777216.0, 1e-5,
        ALU.mult, ALU.max))
    dv_g11 = P.inc("vector", "DV", lambda e: (e.drain(), e.engine_nop())[1])
    P.w("tensor", "DV", dv_g11)
    pe_gb = P.inc("tensor", "PE", lambda e: e.matmul(
        p_b[:, 644:647], ones1[:1, :], g11[:1, 0:3], start=True, stop=True))
    P.w("vector", "PE", pe_gb)
    for gi, (cth, csc, cis) in enumerate(
            [(C_SCG, C_SCG, C_ISG), (C_SCV, C_SCV, C_ISV), (C_SCO, C_SCO, -1)]):
        P.do("vector", lambda e, gi=gi, csc=csc: e.tensor_single_scalar(
            col(csc), p_b[:, 644 + gi:645 + gi], 0.5, ALU.mult))
        P.do("vector", lambda e: e.drain())
        if cis >= 0:
            P.do("vector", lambda e, csc=csc, cis=cis: e.reciprocal(
                col(cis), col(csc)))
            P.do("vector", lambda e: e.drain())
    TH = {}
    for mi, csc in ((0, C_SCG), (1, C_SCV), (2, C_SCO)):
        cp = C_SA0 + mi
        cn = C_SB0 + mi
        P.do("vector", lambda e, cp=cp, csc=csc: e.tensor_copy(
            col(cp), col(csc)))
        P.do("vector", lambda e: e.drain())
        P.do("vector", lambda e, cp=cp, cn=cn: e.tensor_single_scalar(
            col(cn), col(cp), -1.0, ALU.mult))
        P.do("vector", lambda e: e.drain())
        TH[mi] = (cp, cn)
    P.do("vector", lambda e: e.tensor_single_scalar(
        cols[:, C_SGC:C_SGC + 16], cols[:, C_GXI:C_GXI + 16], col(C_SCG),
        ALU.mult))
    P.do("vector", lambda e: e.drain())
    P.do("vector", lambda e: e.tensor_single_scalar(
        cols[:, C_SVC:C_SVC + 16], cols[:, C_GXI:C_GXI + 16], col(C_SCV),
        ALU.mult))
    P.do("vector", lambda e: e.drain())
    rows_g = lambda s: af(O_RES + s * 128, 128, 0, 1)
    rows_v = lambda s: af(O_X1 + s * 128, 128, 0, 1)
    P.w("vector", "SX0", st_x1[NS - 2])
    P.w("vector", "SX1", st_x1[NS - 1])
    for mi in range(2):
        base = C_SGC if mi == 0 else C_SVC
        rows = rows_g if mi == 0 else rows_v
        for s in range(NS):
            P.do("vector", lambda e, base=base, s=s: e.reciprocal(
                col(C_T0), ccol(base, s)))
            P.do("vector", lambda e: e.drain())
            dv_iv = P.inc("vector", "DV", lambda e: (e.drain(),
                                                     e.engine_nop())[1])
            P.w("tensor", "DV", dv_iv)
            pe_tv = P.inc("tensor", "PE", lambda e: e.transpose(
                p_b[0:1, 648:776], col(C_T0), idf[:, :]))
            P.w("vector", "PE", pe_tv)
            P.do("vector", lambda e: e.drain())
            P.inc("vector", "DV", lambda e, ro=rows(s): e.tensor_copy(
                ro, p_b[0:1, 648:776]))
    dv_rows = P.inc("vector", "DV", lambda e: (e.drain(), e.engine_nop())[1])

    wlb = [af(O_WL, DM), af(O_WL + 2048, DM)]
    pqb = af(O_PQ, DM)
    gs_slab = af(O_GS, 4096)
    sgh = [ab(B_SG, 1024), ab(B_SG + 1024, 1024)]
    oqb = ab(B_OQ, DM)
    gwq2 = [B[:, B_GWA:B_GWA + 8192], B[:, B_GWB:B_GWB + 8192]]
    hbuf2 = [af(O_UB, 512), af(O_UB + 512, 512)]
    brb = [af(O_BR, 512, 0, 1), af(O_BR + 512, 512, 0, 1)]
    P.do("vector", lambda e: e.memset(cols[:, C_MAC:C_MAC + 16], 0.0))
    dv_mz = P.inc("vector", "DV", lambda e: (e.drain(), e.engine_nop())[1])

    pq_ac = {}
    pq_st = {}
    chain = {"dv": 0}

    def emit_prequant(jcs):
        for jc in jcs:
            if jc >= NJC:
                continue
            if jc >= 1:
                P.w("sync", "AC", pq_ac[jc - 1])
            lq = P.inc("sync", "LQ%d" % (jc % 2), lambda e, jc=jc:
                       e.dma_start(pqb, flo[jc, :, :]), 16)
            P.w("scalar", "LQ%d" % (jc % 2), lq)
            if chain["dv"]:
                P.w("scalar", "DV", chain["dv"])
            P.do("scalar", lambda e: e.activation(
                ab(B_SG, DM), pqb, AF.Sign, bias=col(C_SB0 + 2)))
            if jc >= 1:
                P.w("scalar", "SQ%d" % ((jc - 1) % 2), pq_st[jc - 1])
            acq = P.inc("scalar", "AC", lambda e: e.activation(
                oqb, pqb, AF.Sign, bias=col(C_SA0 + 2)))
            P.w("vector", "AC", acq)
            dvq = P.inc("vector", "DV", lambda e: e.tensor_tensor(
                oqb, ab(B_SG, DM), oqb, ALU.add))
            chain["dv"] = dvq
            pq_ac[jc] = acq
            P.w("sync", "DV", dvq)
            pq_st[jc] = P.inc("sync", "SQ%d" % (jc % 2),
                              lambda e, jc=jc: e.dma_start(oq_scr[jc, :, :],
                                                           oqb), 16)

    P.w("scalar", "DV", dv_rows)
    P.w("sync", "DV", dv_rows)
    P.w("sync", "PE", pe_after_p1)
    gwt_r = gwt_h.rearrange("(k p) x -> p k x", p=128)
    vwt_r = vwt_h.rearrange("(k p) x -> p k x", p=128)
    st_h = {}
    lw_ac = {}
    pe_slab_free = {}
    pe_burst = {}
    dv_quant = {}
    epi_done = {}
    lr_hist = {}
    pq_next = 0
    jq_box = [0]
    tc = 0
    sg0q = ab(B_SG, DM)
    sg1q = ab(B_OQ, DM)

    def quantize(hg, mat, gate_pe):
        wr = gwt_r if mat == 0 else vwt_r
        cp, cn = TH[mat]
        wsl = gwq2[mat]
        first = True
        for c in range(4):
            jq = jq_box[0]
            bp = jq % 2
            if jq >= 2:
                P.w("sync", "AC", lw_ac[jq - 2])
            lw = P.inc("sync", "LW%d" % bp, lambda e, wr=wr, c=c, hg=hg,
                       bp=bp: e.dma_start(
                           wlb[bp].rearrange("p (k x) -> p k x", x=512),
                           wr[:, c * 4:c * 4 + 4,
                              hg * 512:(hg + 1) * 512]), 16)
            P.w("scalar", "LW%d" % bp, lw)
            if chain["dv"]:
                P.w("scalar", "DV", chain["dv"])
            if first:
                P.w("scalar", "SQ0", P.val("SQ0"))
                P.w("scalar", "SQ1", P.val("SQ1"))
            P.do("scalar", lambda e, bp=bp, cn=cn: e.activation(
                sg0q, wlb[bp], AF.Sign, bias=col(cn)))
            acs = P.inc("scalar", "AC", lambda e, bp=bp, cp=cp: e.activation(
                sg1q, wlb[bp], AF.Sign, bias=col(cp)))
            P.w("vector", "AC", acs)
            if first and gate_pe is not None:
                P.w("vector", "PE", gate_pe)
            dvs2 = P.inc("vector", "DV", lambda e, c=c, wsl=wsl:
                         e.tensor_tensor(wsl[:, c * 2048:(c + 1) * 2048],
                                         sg0q, sg1q, ALU.add))
            chain["dv"] = dvs2
            lw_ac[jq] = acs
            jq_box[0] = jq + 1
            first = False
        dv_quant[(hg, mat)] = chain["dv"]

    P.w("scalar", "PE", pe_after_p1)
    quantize(0, 0, None)
    quantize(0, 1, None)
    for hg in range(NHG):
        if hg >= 1:
            P.w("sync", "PE", pe_burst[(hg - 1, 1, 1)])
        lrg = P.inc("sync", "LR0", lambda e, hg=hg: e.dma_start(
            brb[0], gbr_h[:, hg * 512:(hg + 1) * 512]), 16)
        lrv = P.inc("sync", "LR1", lambda e, hg=hg: e.dma_start(
            brb[1], vbr_h[:, hg * 512:(hg + 1) * 512]), 16)
        lr_hist[hg] = (lrg, lrv)
        for half in range(2):
            for mat in range(2):
                wsl = gwq2[mat]
                P.w("tensor", "DV", dv_quant[(hg, mat)])
                if hg == 0 and half == 0 and mat == 0:
                    P.w("tensor", "DV", dv_xi_done[NS - 1])
                    P.w("tensor", "DV", dv_mz)
                P.w("tensor", "LR%d" % mat, lr_hist[hg][mat])
                for i8 in range(8):
                    t = half * 8 + i8
                    bank = tc % 6
                    psl = (p_a[:, bank * 512:(bank + 1) * 512] if bank < 4
                           else p_b[:, (bank - 4) * 512:(bank - 3) * 512])
                    if tc >= 6:
                        kind, val = epi_done[tc - 6]
                        P.w("tensor", kind, val)
                    ro = (rows_g if mat == 0 else rows_v)(t)
                    P.do("tensor", lambda e, psl=psl, ro=ro, b=brb[mat]:
                         e.matmul(psl, ro, b, start=True, stop=False))
                    for k in range(NK):
                        fn = lambda e, k=k, t=t, psl=psl, wsl=wsl: e.matmul(
                            psl,
                            B[:, B_XI + k * 2048 + t * 128:
                               B_XI + k * 2048 + t * 128 + 128],
                            wsl[:, k * 512:(k + 1) * 512],
                            start=False, stop=(k == 15))
                        if k == 15:
                            pe_mm = P.inc("tensor", "PE", fn)
                        else:
                            P.do("tensor", fn)
                    slot = t % 8
                    if mat == 0:
                        P.w("scalar", "PE", pe_mm)
                        if hg > 0 or half > 0:
                            back = 8 if half == 1 else 16
                            P.w("scalar", "DV", epi_done[tc - back][1])
                        ac_e = P.inc(
                            "scalar", "AC", lambda e, slot=slot, psl=psl,
                            t=t: e.activation(
                                gs_slab[:, slot * 512:(slot + 1) * 512],
                                psl, AF.Silu, scale=ccol(C_SGC, t)))
                        epi_done[tc] = ("AC", ac_e)
                    else:
                        P.w("vector", "PE", pe_mm)
                        hp = t % 2
                        nvt = hg * 16 + t
                        if nvt >= 2:
                            P.w("vector", "SH%d" % hp,
                                st_h[((nvt - 2) // 16, (nvt - 2) % 16)])
                        P.w("vector", "AC", epi_done[tc - 8][1])
                        P.do("vector", lambda e: e.drain())
                        P.do("vector", lambda e, slot=slot, psl=psl, hp=hp,
                             t=t: e.scalar_tensor_tensor(
                                 hbuf2[hp], psl, ccol(C_SVC, t),
                                 gs_slab[:, slot * 512:(slot + 1) * 512],
                                 ALU.mult, ALU.mult))
                        P.do("vector", lambda e: e.drain())
                        P.do("vector", lambda e, hp=hp: e.tensor_reduce(
                            col(C_T0), hbuf2[hp], AX.X, ALU.max,
                            apply_absolute_value=True))
                        P.do("vector", lambda e: e.drain())
                        dv_h = P.inc("vector", "DV", lambda e, t=t:
                                     e.tensor_tensor(ccol(C_MAC, t),
                                                     ccol(C_MAC, t),
                                                     col(C_T0), ALU.max))
                        epi_done[tc] = ("DV", dv_h)
                        P.w("sync", "DV", dv_h)
                        st_h[(hg, t)] = P.inc(
                            "sync", "SH%d" % hp,
                            lambda e, t=t, hg=hg, hp=hp: e.dma_start(
                                h_scr[t, :, hg * 512:(hg + 1) * 512],
                                hbuf2[hp]), 16)
                    tc += 1
                pe_burst[(hg, half, mat)] = P.val("PE")
                if half == 1 and mat == 0 and hg + 1 < NHG:
                    quantize(hg + 1, 0, pe_burst[(hg, 1, 0)])
        if hg + 1 < NHG:
            quantize(hg + 1, 1, pe_burst[(hg, 1, 1)])
        emit_prequant(list(range(pq_next, pq_next + 4)))
        pq_next += 4

    P.w("vector", "AC", pq_ac[NJC - 1])
    P.do("vector", lambda e: e.drain())
    P.do("vector", lambda e: e.tensor_scalar(
        cols[:, C_MAC:C_MAC + 16], cols[:, C_MAC:C_MAC + 16], 1.0, 1e-5,
        ALU.mult, ALU.max))
    P.do("vector", lambda e: e.drain())
    P.do("vector", lambda e: e.tensor_scalar(
        cols[:, C_S2:C_S2 + 16], cols[:, C_MAC:C_MAC + 16], 1.0 / 127.0, 0.0,
        ALU.mult, ALU.add))
    P.do("vector", lambda e: e.drain())
    P.do("vector", lambda e: e.reciprocal(
        cols[:, C_MAC:C_MAC + 16], cols[:, C_S2:C_S2 + 16]))
    P.do("vector", lambda e: e.drain())
    P.do("vector", lambda e: e.tensor_single_scalar(
        cols[:, C_S2:C_S2 + 16], cols[:, C_S2:C_S2 + 16], col(C_SCO),
        ALU.mult))
    P.do("vector", lambda e: e.drain())
    rows_o = lambda s: af(O_PQ + s * 128, 128, 0, 1)
    for s in range(NS):
        P.do("vector", lambda e, s=s: e.reciprocal(col(C_T0), ccol(C_S2, s)))
        P.do("vector", lambda e: e.drain())
        dv_iv = P.inc("vector", "DV", lambda e: (e.drain(),
                                                 e.engine_nop())[1])
        P.w("tensor", "DV", dv_iv)
        if s == 0:
            P.w("tensor", "PE", pe_burst[(NHG - 1, 1, 1)])
        pe_tv = P.inc("tensor", "PE", lambda e: e.transpose(
            p_b[0:1, 648:776], col(C_T0), idf[:, :]))
        P.w("vector", "PE", pe_tv)
        P.do("vector", lambda e: e.drain())
        P.inc("vector", "DV", lambda e, ro=rows_o(s): e.tensor_copy(
            ro, p_b[0:1, 648:776]))
    dv_orows = P.inc("vector", "DV", lambda e: (e.drain(), e.engine_nop())[1])

    NE = 8
    hbq = [af(O_UB, DM), af(O_UT, DM)]
    lCc = [af(O_LC, DM), af(O_WL, DM)]
    lYc = [af(O_LY, DM), af(O_WL + 2048, DM)]
    sgc = [ab(B_SG, DM), ab(B_OQ, DM)]
    xtmp = af(O_RES, DM)
    x1r2 = [af(O_GS, DM), af(O_GS + 2048, DM)]
    z2 = [af(O_DP1, DM), af(O_X1, DM)]
    bro = af(O_BR, 1024, 0, 1)
    owqb = [B[:, B_GWA + i * 4096:B_GWA + (i + 1) * 4096] for i in range(2)]
    oq_r = oq_scr.rearrange("j p d -> p j d")
    hitH = [B[:, B_XI + i * 16384:B_XI + (i + 1) * 16384] for i in range(2)]
    hitH_v = [hitH[i].rearrange("p (jc t) -> p jc t", t=256) for i in range(2)]
    sq2 = ab(B_SG, DM)

    dv_lad = {}
    dv_evac = {}
    dv_evac_all = {}
    dv_z_round = {}
    st_out = {}
    dv_ln2 = {}
    lh_vals = {}
    lb_vals = {}
    lr_vals = {}
    lx_vals = {}
    tr_emitters = {}

    def emit_lh(e, p):
        pc = e * 8 + p
        c, ch = p // 2, p % 2
        st = 2 * e + ch
        bp = pc % 2
        if pc >= 2:
            P.w("sync", "PL", pc - 1)
        if pc == 0:
            P.w("sync", "SH0", P.val("SH0"))
            P.w("sync", "SH1", P.val("SH1"))
        lh_vals[pc] = P.inc("sync", "LH%d" % bp, lambda e_, st=st, c=c,
                            bp=bp: e_.dma_start(
                                hbq[bp],
                                h_scr[st, :, c * 2048:(c + 1) * 2048]), 16)

    def emit_p3_sync(e):
        em = e - 1
        if em >= 0:
            nd0 = em * 2
            if nd0 >= 1:
                P.w("sync", "PG", (nd0 - 1) * 16 + 1)
            lr_vals[nd0] = P.inc("sync", "LR%d" % (nd0 % 2), lambda e_:
                                 e_.dma_start(bro, obr_h[:, 0:1024]), 16)
            for t2 in range(2):
                st = 2 * em + t2
                if em > 0:
                    P.w("sync", "DV", dv_ln2[(em - 1, t2)])
                elif t2 == 0:
                    P.w("sync", "SX0", P.val("SX0"))
                    P.w("sync", "SX1", P.val("SX1"))
                lx_vals[(em, t2)] = P.inc(
                    "sync", "LX%d" % t2, lambda e_, st=st, t2=t2:
                    e_.dma_start(x1r2[t2], x1_scr[st, :, :]), 16)
            for g in range(16):
                nload = em * 32 + g
                if nload >= 2:
                    P.w("sync", "PG", nload - 1)
                if em == 0 and g == 0:
                    P.w("sync", "SQ0", P.val("SQ0"))
                    P.w("sync", "SQ1", P.val("SQ1"))
                lb_vals[(em, 0, g)] = P.inc(
                    "sync", "LB%d" % (g % 2), lambda e_, g=g: e_.dma_start(
                        owqb[g % 2].rearrange("p (j d) -> p j d", d=1024),
                        oq_r[:, g * 4:g * 4 + 4, 0:1024]), 16)
                if e < NE and 1 <= g <= 8:
                    emit_lh(e, g - 1)
            nd1 = em * 2 + 1
            P.w("sync", "PG", (nd1 - 1) * 16 + 1)
            lr_vals[nd1] = P.inc("sync", "LR%d" % (nd1 % 2), lambda e_:
                                 e_.dma_start(bro, obr_h[:, 1024:2048]), 16)
            for g in range(16):
                nload = em * 32 + 16 + g
                P.w("sync", "PG", nload - 1)
                lb_vals[(em, 1, g)] = P.inc(
                    "sync", "LB%d" % (g % 2), lambda e_, g=g: e_.dma_start(
                        owqb[g % 2].rearrange("p (j d) -> p j d", d=1024),
                        oq_r[:, g * 4:g * 4 + 4, 1024:2048]), 16)
        else:
            for p in range(8):
                emit_lh(e, p)

    def emit_ladder(e):
        hh = e % 2
        emitters = []
        for c in range(4):
            pcs = [e * 8 + c * 2, e * 8 + c * 2 + 1]
            for ch in range(2):
                pc = pcs[ch]
                st = 2 * e + ch
                bp = pc % 2
                P.w("vector", "LH%d" % bp, lh_vals[pc])
                if pc == 0:
                    P.w("vector", "DV", dv_orows)
                    P.w("vector", "SQ0", P.val("SQ0"))
                    P.w("vector", "SQ1", P.val("SQ1"))
                r2c = ccol(C_MAC, st)
                P.do("vector", lambda e_: e_.drain())
                P.inc("vector", "PL", lambda e_, bp=bp, r2c=r2c, lY=lYc[ch]:
                      e_.tensor_single_scalar(lY, hbq[bp], r2c, ALU.mult))
            dv_prev = [0, 0]
            for j in range(8):
                ac_j = [0, 0]
                for ch in range(2):
                    pc = pcs[ch]
                    src_c = lYc[ch] if j == 0 else lCc[ch]
                    if j == 0:
                        P.w("scalar", "PL", pc + 1)
                        if pc >= 2:
                            P.w("scalar", "PT", pc - 1)
                    else:
                        P.w("scalar", "DV", dv_prev[ch])
                    ac_j[ch] = P.inc("scalar", "AC",
                                     lambda e_, src_c=src_c, sg=sgc[ch]:
                                     e_.activation(sg, src_c, AF.Sign,
                                                   bias=col(C_HALF)))
                for ch in range(2):
                    src_c = lYc[ch] if j == 0 else lCc[ch]
                    P.w("vector", "AC", ac_j[ch])
                    dv_prev[ch] = P.inc(
                        "vector", "DV",
                        lambda e_, j=j, src_c=src_c, sg=sgc[ch], lC=lCc[ch]:
                        e_.scalar_tensor_tensor(lC, sg, col(C_K0 + j), src_c,
                                                ALU.mult, ALU.add))
            for ch in range(2):
                pc = pcs[ch]
                c_, ch_ = c, ch
                P.do("vector", lambda e_: e_.drain())
                P.do("vector", lambda e_, lY=lYc[ch], lC=lCc[ch]:
                     e_.tensor_tensor(xtmp, lY, lC, ALU.subtract))
                P.do("vector", lambda e_: e_.drain())
                dv_lad[pc] = P.inc("vector", "DV", lambda e_, sg=sgc[ch]:
                                   e_.tensor_scalar(sg, xtmp, 1.0, -0.5,
                                                    ALU.mult, ALU.add))

                def tr_emit(pc=pc, sg=sgc[ch]):
                    P.w("tensor", "DV", dv_lad[pc])
                    if pc >= 1:
                        P.w("tensor", "DV", dv_evac[pc - 1])
                    for kk in range(16):
                        fn = lambda e_, kk=kk, sg=sg: e_.transpose(
                            p_tr[:, kk * 128:(kk + 1) * 128],
                            sg[:, kk * 128:(kk + 1) * 128], idb[:, :])
                        if kk == 15:
                            P.inc("tensor", "PT", fn)
                        else:
                            P.do("tensor", fn)
                emitters.append(tr_emit)
                P.w("vector", "PT", pc + 1)
                if e >= 2 and pc == e * 8:
                    P.w("vector", "PG", (e - 1) * 32)
                ptr_view = p_tr[:, :].rearrange("p (k t) -> p k t", t=128)
                dv_evac[pc] = P.inc(
                    "vector", "DV",
                    lambda e_, c=c_, ch=ch_, hh=hh, i=ptr_view:
                    e_.tensor_copy(hitH_v[hh][:, c * 16:(c + 1) * 16,
                                              ch * 128:(ch + 1) * 128], i))
        dv_evac_all[e] = P.val("DV")
        tr_emitters[e] = emitters

    def emit_mm(em, emitters):
        hh = em % 2
        spl = 0
        for r in range(2):
            nd = em * 2 + r
            for grp in range(16):
                gbp = grp % 2
                if r == 0 and grp % 2 == 1 and spl < len(emitters):
                    emitters[spl]()
                    spl += 1
                P.w("tensor", "LB%d" % gbp, lb_vals[(em, r, grp)])
                if grp == 0:
                    P.w("tensor", "LR%d" % (nd % 2), lr_vals[nd])
                    if r == 0:
                        P.w("tensor", "DV", dv_evac_all[em])
                    if r >= 1:
                        P.w("tensor", "DV", dv_z_round[(em, 0)])
                    elif em > 0:
                        P.w("tensor", "DV", dv_z_round[(em - 1, 1)])
                    for t2 in range(2):
                        st = 2 * em + t2
                        for di in range(2):
                            bank = t2 * 2 + di
                            P.do("tensor", lambda e_, bank=bank, st=st,
                                 di=di: e_.matmul(
                                     p_a[:, bank * 512:(bank + 1) * 512],
                                     rows_o(st),
                                     bro[:, di * 512:(di + 1) * 512],
                                     start=True, stop=False))
                for jj in range(4):
                    jc = grp * 4 + jj
                    for t2 in range(2):
                        for di in range(2):
                            bank = t2 * 2 + di
                            fn = lambda e_, jc=jc, t2=t2, di=di, bank=bank, \
                                gbp=gbp, jj=jj, hh=hh: e_.matmul(
                                    p_a[:, bank * 512:(bank + 1) * 512],
                                    hitH[hh][:, jc * 256 + t2 * 128:
                                             jc * 256 + t2 * 128 + 128],
                                    owqb[gbp][:, jj * 1024 + di * 512:
                                              jj * 1024 + di * 512 + 512],
                                    start=False, stop=(jc == NJC - 1))
                            if jj == 3 and t2 == 1 and di == 1:
                                P.inc("tensor", "PG", fn)
                            else:
                                P.do("tensor", fn)
            for t2 in range(2):
                st = 2 * em + t2
                for di in range(2):
                    dmg = r * 2 + di
                    bank = t2 * 2 + di
                    P.w("vector", "PG", nd * 16 + 16)
                    if r == 0 and di == 0:
                        P.w("vector", "LX%d" % t2, lx_vals[(em, t2)])
                        if em > 0:
                            P.w("vector", "SO%d" % ((2 * (em - 1) + t2) % 2),
                                st_out[2 * (em - 1) + t2])
                    s2c = ccol(C_S2, st)
                    P.do("vector", lambda e_: e_.drain())
                    dv_z = P.inc("vector", "DV",
                                 lambda e_, t2=t2, dmg=dmg, s2c=s2c,
                                 bank=bank:
                                 e_.scalar_tensor_tensor(
                                     z2[t2][:, dmg * 512:(dmg + 1) * 512],
                                     p_a[:, bank * 512:(bank + 1) * 512],
                                     s2c,
                                     x1r2[t2][:, dmg * 512:(dmg + 1) * 512],
                                     ALU.mult, ALU.add))
            dv_z_round[(em, r)] = dv_z
        for t2 in range(2):
            st = 2 * em + t2
            P.w("vector", "DV", dv_z_round[(em, 1)])
            zt = z2[t2]
            mcol, vcol, rcol = col(C_T0), col(C_T1), col(C_T3)
            P.do("vector", lambda e_: e_.drain())
            P.do("vector", lambda e_, zt=zt: e_.tensor_reduce(mcol, zt, AX.X,
                                                              ALU.add))
            P.do("vector", lambda e_: e_.drain())
            P.do("vector", lambda e_: e_.tensor_single_scalar(mcol, mcol,
                                                              INV_D,
                                                              ALU.mult))
            P.do("vector", lambda e_: e_.drain())
            dv_l2 = P.inc("vector", "DV", lambda e_, zt=zt:
                          e_.tensor_single_scalar(zt, zt, mcol,
                                                  ALU.subtract))
            P.w("scalar", "DV", dv_l2)
            P.do("scalar", lambda e_, zt=zt: e_.activation(sq2, zt, AF.Square,
                                                           accum_out=vcol))
            P.do("scalar", lambda e_: e_.drain())
            P.do("scalar", lambda e_: e_.activation(col(C_T2), vcol, AF.Ln,
                                                    bias=col(C_EPS),
                                                    scale=INV_D))
            P.do("scalar", lambda e_: e_.drain())
            ac_l2 = P.inc("scalar", "AC", lambda e_: e_.activation(
                rcol, col(C_T2), AF.Exp, scale=-0.5))
            P.w("vector", "AC", ac_l2)
            P.do("vector", lambda e_: e_.drain())
            dv_o = P.inc("vector", "DV", lambda e_, zt=zt:
                         e_.tensor_single_scalar(zt, zt, rcol, ALU.mult))
            dv_ln2[(em, t2)] = dv_o
            P.w("sync", "DV", dv_o)
            st_out[st] = P.inc("sync", "SO%d" % (st % 2),
                               lambda e_, st=st, zt=zt: e_.dma_start(
                                   out_h[st * 128:(st + 1) * 128, :], zt), 16)

    P.w("sync", "PE", pe_burst[(NHG - 1, 1, 1)])
    for e in range(NE + 1):
        emit_p3_sync(e)
        if e < NE:
            emit_ladder(e)
        if e == 0:
            for em_ in tr_emitters[0]:
                em_()
            tr_emitters[0] = []
        if e >= 1:
            emit_mm(e - 1, tr_emitters[e] if e < NE else [])

    P.w("sync", "SO0", P.val("SO0"))
    P.w("sync", "SO1", P.val("SO1"))

    with nc.Block() as block:
        @block.sync
        def _(eng):
            for f in P.ops["sync"]:
                f(eng)

        @block.tensor
        def _(eng):
            for f in P.ops["tensor"]:
                f(eng)

        @block.vector
        def _(eng):
            for f in P.ops["vector"]:
                f(eng)

        @block.scalar
        def _(eng):
            for f in P.ops["scalar"]:
                f(eng)

        @block.gpsimd
        def _(eng):
            for f in P.ops["gpsimd"]:
                f(eng)

    ctx.close()
    return nc


_NC_CACHE = None


def kernel(L0_attn_u, L0_attn_A_factors, L0_attn_W_b_rank, L0_attn_W_b_state,
           L0_attn_W_c_rank, L0_attn_W_c_model, L0_attn_D_vec,
           L0_ffn_gate_w, L0_ffn_gate_b, L0_ffn_val_w, L0_ffn_val_b,
           L0_ffn_out_w, L0_ffn_out_b):
    global _NC_CACHE
    u = np.ascontiguousarray(np.asarray(L0_attn_u, dtype=np.float32).reshape(-1, DM))
    gw = np.asarray(L0_ffn_gate_w, dtype=np.float32)
    vw = np.asarray(L0_ffn_val_w, dtype=np.float32)
    ow = np.asarray(L0_ffn_out_w, dtype=np.float32)
    shared = dict(
        gwt=np.ascontiguousarray(gw.T),
        vwt=np.ascontiguousarray(vw.T),
        owt=np.ascontiguousarray(ow.T),
        gbr=np.ascontiguousarray(np.asarray(L0_ffn_gate_b,
                                            dtype=np.float32).reshape(1, HID)),
        vbr=np.ascontiguousarray(np.asarray(L0_ffn_val_b,
                                            dtype=np.float32).reshape(1, HID)),
        obr=np.ascontiguousarray(np.asarray(L0_ffn_out_b,
                                            dtype=np.float32).reshape(1, DM)),
        wbr=np.ascontiguousarray(np.asarray(L0_attn_W_b_rank, dtype=np.float32)),
        wbs=np.ascontiguousarray(np.asarray(L0_attn_W_b_state, dtype=np.float32)),
        wcr=np.ascontiguousarray(np.asarray(L0_attn_W_c_rank, dtype=np.float32)),
        wcm=np.ascontiguousarray(np.asarray(L0_attn_W_c_model, dtype=np.float32)),
        afac=np.ascontiguousarray(
            np.asarray(L0_attn_A_factors, dtype=np.float32).reshape(1, 32)),
        dvec=np.ascontiguousarray(
            np.asarray(L0_attn_D_vec, dtype=np.float32).reshape(1, DM)),
        idf=np.eye(128, dtype=np.float32),
        idb=np.eye(128, dtype=np.float32).astype(BF),
    )
    if _NC_CACHE is None:
        _NC_CACHE = build_program()
    in_maps = []
    for c in range(NCORE):
        uc = np.ascontiguousarray(u[c * TOK:(c + 1) * TOK, :])
        in_maps.append(dict(u=uc, ut=np.ascontiguousarray(uc.T), **shared))
    res = run_bass_kernel_spmd(_NC_CACHE, in_maps, core_ids=list(range(NCORE)))
    out = np.concatenate(
        [np.asarray(r["out"], dtype=np.float32) for r in res.results], axis=0)
    return out.reshape(4, 4096, DM)


# version 10
# speedup vs baseline: 1.4465x; 1.0139x over previous
import contextlib
import numpy as np
import ml_dtypes
import concourse.bass as bass
import concourse.mybir as mybir
from concourse.bass_utils import run_bass_kernel_spmd

F32 = mybir.dt.float32
BF16 = mybir.dt.bfloat16
BF = ml_dtypes.bfloat16
AF = mybir.ActivationFunctionType
ALU = mybir.AluOpType
AX = mybir.AxisListType

NCORE = 8
TOK = 2048
DM = 2048
HID = 8192
NS = 16
NHG = 16
NK = 16
NJC = 64
NQ = 4
INV_D = 1.0 / DM
EPS = 1e-5
LADK = (128.0, 64.0, 32.0, 16.0, 8.0, 4.0, 2.0, 1.0)

O_COLS = 0
O_DP1 = 128
O_UB = 2176
O_UT = 4224
O_RES = 6272
O_X1 = 8320
O_LC = 10368
O_LY = 12416
O_GS = 14464
O_WL = 18560
O_PQ = 22656
O_BR = 24704
A_W = 25728
B_XI = 0
B_GWA = 32768
B_GWB = 40960
B_SG = 49152
B_OQ = 51200
B_W = 53248

(C_EPS, C_HALF, C_T0, C_T1, C_T2, C_T3, C_SCG, C_SCV, C_SCO,
 C_ISG, C_ISV, C_RA0, C_RA1, C_RA2, C_SA0, C_SA1, C_SA2, C_SB0, C_SB1,
 C_SB2, C_K0) = range(21)
C_MAC = 32
C_SXS = 48
C_SGC = 64
C_SVC = 80
C_S2 = 96
C_GXI = 112
NCOL = 128


class Prog:
    ENGS = ("sync", "tensor", "vector", "scalar", "gpsimd")

    def __init__(self):
        self.ops = {e: [] for e in self.ENGS}
        self.cnt = {}
        self.sems = {}

    def w(self, eng, sname, val):
        if val <= 0:
            return
        self.ops[eng].append(lambda e, s=self.sems[sname], v=val: e.wait_ge(s, v))

    def do(self, eng, fn):
        self.ops[eng].append(fn)

    def inc(self, eng, sname, fn, amt=1):
        self.cnt[sname] = self.cnt.get(sname, 0) + amt
        self.ops[eng].append(
            lambda e, f=fn, s=self.sems[sname], a=amt: f(e).then_inc(s, a))
        return self.cnt[sname]

    def val(self, sname):
        return self.cnt.get(sname, 0)


def build_program(debug=False):
    nc = bass.Bass("TRN2", target_bir_lowering=False)

    u_h = nc.dram_tensor("u", [TOK, DM], F32, kind="ExternalInput").ap()
    ut_h = nc.dram_tensor("ut", [DM, TOK], F32, kind="ExternalInput").ap()
    gwt_h = nc.dram_tensor("gwt", [DM, HID], F32, kind="ExternalInput").ap()
    vwt_h = nc.dram_tensor("vwt", [DM, HID], F32, kind="ExternalInput").ap()
    owt_h = nc.dram_tensor("owt", [HID, DM], F32, kind="ExternalInput").ap()
    gbr_h = nc.dram_tensor("gbr", [1, HID], F32, kind="ExternalInput").ap()
    vbr_h = nc.dram_tensor("vbr", [1, HID], F32, kind="ExternalInput").ap()
    obr_h = nc.dram_tensor("obr", [1, DM], F32, kind="ExternalInput").ap()
    wbr_h = nc.dram_tensor("wbr", [DM, 64], F32, kind="ExternalInput").ap()
    wbs_h = nc.dram_tensor("wbs", [64, 256], F32, kind="ExternalInput").ap()
    wcr_h = nc.dram_tensor("wcr", [256, 64], F32, kind="ExternalInput").ap()
    wcm_h = nc.dram_tensor("wcm", [64, DM], F32, kind="ExternalInput").ap()
    a_h = nc.dram_tensor("afac", [1, 32], F32, kind="ExternalInput").ap()
    d_h = nc.dram_tensor("dvec", [1, DM], F32, kind="ExternalInput").ap()
    idf_h = nc.dram_tensor("idf", [128, 128], F32, kind="ExternalInput").ap()
    idb_h = nc.dram_tensor("idb", [128, 128], BF16, kind="ExternalInput").ap()
    out_h = nc.dram_tensor("out", [TOK, DM], F32, kind="ExternalOutput").ap()
    scr_kind = "ExternalOutput" if debug else "Internal"
    x1_scr = nc.dram_tensor("x1_scr", [NS, 128, DM], F32, kind=scr_kind).ap()
    h_scr = nc.dram_tensor("h_scr", [NS, 128, HID], F32, kind=scr_kind).ap()
    oq_scr = nc.dram_tensor("oq_scr", [NJC, 128, DM], BF16, kind=scr_kind).ap()

    ctx = contextlib.ExitStack()
    A = ctx.enter_context(nc.sbuf_tensor("arena_f32", [128, A_W], F32))
    B = ctx.enter_context(nc.sbuf_tensor("arena_bf16", [128, B_W], BF16))

    def af(off, w, p0=0, p1=128):
        return A[p0:p1, off:off + w]

    def ab(off, w, p0=0, p1=128):
        return B[p0:p1, off:off + w]

    idf = ctx.enter_context(nc.sbuf_tensor("idf_s", [128, 128], F32))
    idb = ctx.enter_context(nc.sbuf_tensor("idb_s", [128, 128], BF16))
    ones1 = ctx.enter_context(nc.sbuf_tensor("ones1", [1, 128], F32))
    onesc = ctx.enter_context(nc.sbuf_tensor("onesc", [128, 1], F32))
    gxs = ctx.enter_context(nc.sbuf_tensor("gxs", [128, NS], F32))
    g11 = ctx.enter_context(nc.sbuf_tensor("g11", [1, 8], F32))

    p_a = ctx.enter_context(nc.psum_tensor("p_a", [128, 2048], F32))
    p_b = ctx.enter_context(nc.psum_tensor("p_b", [128, 1024], F32))
    p_tr = ctx.enter_context(nc.psum_tensor("p_tr", [128, 2048], BF16))

    P = Prog()
    semnames = ["PE", "DV", "AC", "GP", "L0",
                "LU0", "LU1", "LT0", "LT1", "LA0", "LA1", "LW0", "LW1",
                "LQ0", "LQ1", "LH0", "LH1", "LB0", "LB1", "LX0", "LX1",
                "LR0", "LR1", "PT", "PG", "PL",
                "SX0", "SX1", "SH0", "SH1", "SQ0", "SQ1", "SO0", "SO1"]
    for nm in semnames:
        P.sems[nm] = ctx.enter_context(nc.semaphore("s" + nm))

    cols = af(O_COLS, NCOL)

    def col(i):
        return A[:, O_COLS + i:O_COLS + i + 1]

    def ccol(base, i):
        return A[:, O_COLS + base + i:O_COLS + base + i + 1]

    M_ = O_WL
    wbrT = af(M_, 1024)
    wcm_f = af(M_ + 1024, DM, 0, 64)
    G_f = af(M_ + 3072, 64, 0, 64)
    t1b = af(M_ + 3136, 128, 0, 64)
    t2b = af(M_ + 3264, 128, 0, 64)
    wbs_t = af(O_PQ, 256, 0, 64)
    bb0 = af(O_PQ + 256, 256, 0, 64)
    bb1 = af(O_PQ + 512, 256, 0, 64)
    tmp256 = af(O_PQ + 768, 256, 0, 64)
    a_bc = af(O_PQ + 1024, 32, 0, 64)
    a_row = af(O_PQ + 1056, 32, 0, 1)
    btT = af(O_PQ + 1088, 128)
    wcr0 = af(O_PQ + 1216, 64)
    wcr1 = af(O_PQ + 1280, 64)
    d_row = af(O_RES, DM, 0, 1)
    dp1 = af(O_DP1, DM)

    ld0 = [(idf[:, :], idf_h[:, :]), (idb[:, :], idb_h[:, :]),
           (d_row, d_h[:, :]), (wbs_t, wbs_h[:, :]), (a_row, a_h[:, :]),
           (wcm_f, wcm_h[:, :]),
           (wbrT.rearrange("p (k r) -> p k r", r=64),
            wbr_h.rearrange("(k p) r -> p k r", p=128)),
           (wcr0, wcr_h[0:128, :]), (wcr1, wcr_h[128:256, :])]
    for dst, src in ld0:
        P.inc("sync", "L0", lambda e, d=dst, s=src: e.dma_start(d, s), 16)
    ld0_done = P.val("L0")

    P.do("vector", lambda e: e.memset(col(C_EPS), float(EPS)))
    P.do("vector", lambda e: e.memset(col(C_HALF), 0.5))
    for j, kk in enumerate(LADK):
        P.do("vector", lambda e, j=j, kk=kk: e.memset(col(C_K0 + j), -kk / 2.0))
    P.do("vector", lambda e: e.memset(ones1[:1, :], 1.0))
    P.do("vector", lambda e: e.memset(onesc[:, :], 1.0))
    P.do("vector", lambda e: e.memset(cols[:, C_RA0:C_RA0 + 3], 0.0))
    P.w("vector", "L0", ld0_done)
    dv_c0 = P.inc("vector", "DV", lambda e: (e.drain(), e.engine_nop())[1])

    P.w("tensor", "DV", dv_c0)
    pe_abc = P.inc("tensor", "PE", lambda e: e.matmul(
        p_b[0:128, 0:32], ones1[:1, 0:128], a_row, start=True, stop=True))
    P.w("vector", "PE", pe_abc)
    P.do("vector", lambda e: e.tensor_copy(a_bc, p_b[0:64, 0:32]))
    P.do("vector", lambda e: e.drain())
    P.do("vector", lambda e: e.tensor_copy(bb0, wbs_t))
    cur, nxt = bb0, bb1
    for lvl in range(8):
        wwid = 2 ** (7 - lvl)
        vi = cur.rearrange("p (b t w) -> p b t w", t=2, w=wwid)
        vo = nxt.rearrange("p (b t w) -> p b t w", t=2, w=wwid)
        vt = tmp256.rearrange("p (b t w) -> p b t w", t=2, w=wwid)
        for j in range(2):
            a0 = a_bc[:, lvl * 4 + j: lvl * 4 + j + 1]
            a1 = a_bc[:, lvl * 4 + 2 + j: lvl * 4 + 3 + j]
            P.do("vector", lambda e: e.drain())
            P.do("vector", lambda e, o=vt[:, :, j, :], i=vi[:, :, 0, :], s=a0:
                 e.tensor_single_scalar(o, i, s, ALU.mult))
            P.do("vector", lambda e: e.drain())
            P.do("vector", lambda e, o=vo[:, :, j, :], i=vi[:, :, 1, :], s=a1,
                 t=vt[:, :, j, :]:
                 e.scalar_tensor_tensor(o, i, s, t, ALU.mult, ALU.add))
        cur, nxt = nxt, cur
    dv_bfly = P.inc("vector", "DV", lambda e: (e.drain(), e.engine_nop())[1])

    P.w("tensor", "DV", dv_bfly)
    P.do("tensor", lambda e, i=cur: e.transpose(p_b[:, 0:64], i[:, 0:128],
                                                idf[0:64, 0:64]))
    pe_btt = P.inc("tensor", "PE", lambda e, i=cur: e.transpose(
        p_b[:, 64:128], i[:, 128:256], idf[0:64, 0:64]))
    P.w("vector", "PE", pe_btt)
    dv_btt = P.inc("vector", "DV", lambda e: e.tensor_copy(btT, p_b[:, 0:128]))
    P.w("tensor", "DV", dv_btt)
    P.do("tensor", lambda e: e.matmul(p_b[0:64, 256:320], btT[0:128, 0:64],
                                      wcr0, start=True, stop=False))
    pe_g = P.inc("tensor", "PE", lambda e: e.matmul(
        p_b[0:64, 256:320], btT[0:128, 64:128], wcr1, start=False, stop=True))
    P.w("vector", "PE", pe_g)
    dv_g = P.inc("vector", "DV", lambda e: e.tensor_copy(G_f, p_b[0:64, 256:320]))

    P.w("tensor", "DV", dv_g)
    dv_dpe = {}
    for i in range(4):
        if i >= 2:
            P.w("tensor", "DV", dv_dpe[i - 2])
        pe_d = P.inc("tensor", "PE", lambda e, i=i: e.matmul(
            p_a[:, (i % 2) * 512:(i % 2) * 512 + 512], ones1[:1, :],
            d_row[:, i * 512:(i + 1) * 512], start=True, stop=True))
        P.w("vector", "PE", pe_d)
        dv_dpe[i] = P.inc("vector", "DV", lambda e, i=i: e.tensor_scalar(
            dp1[:, i * 512:(i + 1) * 512],
            p_a[:, (i % 2) * 512:(i % 2) * 512 + 512],
            1.0, 1.0, ALU.mult, ALU.add))
    dv_dp1 = P.inc("vector", "DV", lambda e: (e.drain(), e.engine_nop())[1])

    ubuf = af(O_UB, DM)
    utb = af(O_UT, DM)
    res = af(O_RES, DM)
    x1b = af(O_X1, DM)
    lC = af(O_LC, DM)
    lY = af(O_LY, DM)
    sgt = ab(B_SG, DM)
    xt_bf = ab(B_OQ, DM)
    xi_all = B[:, B_XI:B_XI + NS * DM].rearrange("p (k t) -> p k t", t=TOK)
    pA_w = [af(O_GS, DM), af(O_GS + 2048, DM)]

    pa_chunks = []
    for mi, wh in ((0, gwt_h), (1, vwt_h)):
        fl = wh.rearrange("(c p) x -> c p x", p=128)
        for c in range(16):
            for s4 in range(4):
                pa_chunks.append((fl[c, :, s4 * 2048:(s4 + 1) * 2048], mi))
    flo = owt_h.rearrange("(c p) x -> c p x", p=128)
    for c in range(64):
        pa_chunks.append((flo[c, :, :], 2))
    NPA = len(pa_chunks)
    per = (NPA + NS - 1) // NS
    pa_sched = [list(range(s * per, min((s + 1) * per, NPA))) for s in range(NS)]

    la_ac = {}
    gp_add = {}
    abs_scr = [ab(B_GWA, DM), ab(B_GWA + 2048, DM)]

    def emit_passA(idxs):
        for j in idxs:
            src, mi = pa_chunks[j]
            bp = j % 2
            if j >= 2:
                P.w("sync", "AC", la_ac[j - 2])
            la = P.inc("sync", "LA%d" % bp,
                       lambda e, d=pA_w[bp], s=src: e.dma_start(d, s), 16)
            P.w("scalar", "LA%d" % bp, la)
            if j >= 2:
                P.w("scalar", "GP", gp_add[j - 2])
            acc = ccol(0, (C_SA0 if j % 2 == 0 else C_SB0) + mi)
            la_ac[j] = P.inc("scalar", "AC", lambda e, bp=bp, acc=acc, j=j:
                             e.activation(abs_scr[j % 2], pA_w[bp], AF.Abs,
                                          accum_out=acc))
            P.w("gpsimd", "AC", la_ac[j])
            gp_add[j] = P.inc("gpsimd", "GP", lambda e, mi=mi, acc=acc:
                              e.tensor_tensor(ccol(0, C_RA0 + mi),
                                              ccol(0, C_RA0 + mi), acc,
                                              ALU.add))
            P.do("gpsimd", lambda e: e.drain())

    st_x1 = {}
    dv_xi_done = {}
    dv_ures_s = {}
    pe_t1_s = {}
    dv_resadd = {}
    for s in range(NS):
        bp = s % 2
        if s >= 1:
            P.w("sync", "DV", dv_ures_s[s - 1])
        lu = P.inc("sync", "LU%d" % bp, lambda e, s=s: e.dma_start(
            ubuf, u_h[s * 128:(s + 1) * 128, :]), 16)
        if s >= 1:
            P.w("sync", "PE", pe_t1_s[s - 1])
        lt = P.inc("sync", "LT%d" % bp, lambda e, s=s: e.dma_start(
            utb.rearrange("p (k t) -> p k t", t=128),
            ut_h.rearrange("(k p) t -> p k t", p=128)
            [:, :, s * 128:(s + 1) * 128]), 16)
        P.w("tensor", "LT%d" % bp, lt)
        if s == 0:
            P.w("tensor", "DV", dv_dp1)
        for k in range(NK):
            fn = lambda e, k=k: e.matmul(
                p_b[0:64, 0:128], wbrT[:, k * 64:(k + 1) * 64],
                utb[:, k * 128:(k + 1) * 128], start=(k == 0), stop=(k == 15))
            if k == 15:
                pe_t1 = P.inc("tensor", "PE", fn)
            else:
                P.do("tensor", fn)
        pe_t1_s[s] = pe_t1
        P.w("vector", "PE", pe_t1)
        dv_t1 = P.inc("vector", "DV", lambda e: e.tensor_copy(
            t1b, p_b[0:64, 0:128]))
        P.w("tensor", "DV", dv_t1)
        pe_t2 = P.inc("tensor", "PE", lambda e: e.matmul(
            p_b[0:64, 512:640], G_f, t1b[0:64, :], start=True, stop=True))
        P.w("vector", "PE", pe_t2)
        dv_t2 = P.inc("vector", "DV", lambda e: e.tensor_copy(
            t2b, p_b[0:64, 512:640]))
        P.w("tensor", "DV", dv_t2)
        pe_y = {}
        for g in range(4):
            if s >= 1:
                P.w("tensor", "DV", dv_resadd[(s - 1, g)])
            pe_y[g] = P.inc("tensor", "PE", lambda e, g=g: e.matmul(
                p_a[:, g * 512:(g + 1) * 512], t2b[0:64, :],
                wcm_f[:, g * 512:(g + 1) * 512], start=True, stop=True))
        P.w("vector", "LU%d" % bp, lu)
        P.do("vector", lambda e: e.drain())
        P.do("vector", lambda e: e.tensor_tensor(res, ubuf, dp1, ALU.mult))
        dv_ures_s[s] = P.inc("vector", "DV",
                             lambda e: (e.drain(), e.engine_nop())[1])
        for g in range(4):
            P.w("vector", "PE", pe_y[g])
            P.do("vector", lambda e: e.drain())
            dv_resadd[(s, g)] = P.inc("vector", "DV", lambda e, g=g:
                                      e.tensor_tensor(
                                          res[:, g * 512:(g + 1) * 512],
                                          res[:, g * 512:(g + 1) * 512],
                                          p_a[:, g * 512:(g + 1) * 512],
                                          ALU.add))
        mcol, vcol, rcol = col(C_T0), col(C_T1), col(C_T3)
        P.do("vector", lambda e: e.drain())
        P.do("vector", lambda e: e.tensor_reduce(mcol, res, AX.X, ALU.add))
        P.do("vector", lambda e: e.drain())
        P.do("vector", lambda e: e.tensor_single_scalar(mcol, mcol, INV_D,
                                                        ALU.mult))
        P.do("vector", lambda e: e.drain())
        dv_ln = P.inc("vector", "DV", lambda e: e.tensor_single_scalar(
            res, res, mcol, ALU.subtract))
        P.w("scalar", "DV", dv_ln)
        P.do("scalar", lambda e: e.activation(sgt, res, AF.Square,
                                              accum_out=vcol))
        P.do("scalar", lambda e: e.drain())
        P.do("scalar", lambda e: e.activation(col(C_T2), vcol, AF.Ln,
                                              bias=col(C_EPS), scale=INV_D))
        P.do("scalar", lambda e: e.drain())
        ac_ln = P.inc("scalar", "AC", lambda e: e.activation(
            rcol, col(C_T2), AF.Exp, scale=-0.5))
        P.w("vector", "AC", ac_ln)
        if s >= 1:
            P.w("vector", "SX%d" % ((s - 1) % 2), st_x1[s - 1])
        P.do("vector", lambda e: e.drain())
        dv_x1 = P.inc("vector", "DV", lambda e: e.tensor_single_scalar(
            x1b, res, rcol, ALU.mult))
        P.w("sync", "DV", dv_x1)
        st_x1[s] = P.inc("sync", "SX%d" % bp, lambda e, s=s: e.dma_start(
            x1_scr[s, :, :], x1b), 16)
        gxc = gxs[:, s:s + 1]
        sxc = ccol(C_SXS, s)
        P.do("vector", lambda e: e.drain())
        P.do("vector", lambda e, g=gxc: e.tensor_reduce(
            g, x1b, AX.X, ALU.max, apply_absolute_value=True))
        P.do("vector", lambda e: e.drain())
        P.do("vector", lambda e, g=gxc: e.tensor_single_scalar(
            g, g, 1e-5, ALU.max))
        P.do("vector", lambda e: e.drain())
        P.do("vector", lambda e, g=gxc, s=s: e.tensor_single_scalar(
            ccol(C_GXI, s), g, 1.0 / 127.0, ALU.mult))
        P.do("vector", lambda e: e.drain())
        P.do("vector", lambda e, sc=sxc, s=s: e.reciprocal(
            sc, ccol(C_GXI, s)))
        P.do("vector", lambda e: e.drain())
        dv_prev = P.inc("vector", "DV", lambda e, sc=sxc:
                        e.tensor_single_scalar(lY, x1b, sc, ALU.mult))
        for j in range(8):
            src_c = lY if j == 0 else lC
            P.w("scalar", "DV", dv_prev)
            ac_j = P.inc("scalar", "AC", lambda e, src_c=src_c: e.activation(
                sgt, src_c, AF.Sign, bias=col(C_HALF)))
            P.w("vector", "AC", ac_j)
            dv_prev = P.inc("vector", "DV", lambda e, j=j, src_c=src_c:
                            e.scalar_tensor_tensor(lC, sgt, col(C_K0 + j),
                                                   src_c, ALU.mult, ALU.add))
        P.do("vector", lambda e: e.drain())
        P.do("vector", lambda e: e.tensor_tensor(res, lY, lC, ALU.subtract))
        P.do("vector", lambda e: e.drain())
        if s >= 1:
            P.w("vector", "PE", pe_xt_s)
        dv_xt = P.inc("vector", "DV", lambda e: e.tensor_scalar(
            xt_bf, res, 1.0, -0.5, ALU.mult, ALU.add))
        P.w("tensor", "DV", dv_xt)
        if s >= 1:
            P.w("tensor", "DV", dv_xi_done[s - 1])
        for k in range(16):
            fn = lambda e, k=k: e.transpose(
                p_tr[:, k * 128:(k + 1) * 128], xt_bf[:, k * 128:(k + 1) * 128],
                idb[:, :])
            if k == 15:
                pe_xt_s = P.inc("tensor", "PE", fn)
            else:
                P.do("tensor", fn)
        P.w("vector", "PE", pe_xt_s)
        ptr_view = p_tr[:, :].rearrange("p (k t) -> p k t", t=128)
        dv_xi_done[s] = P.inc(
            "vector", "DV",
            lambda e, s=s, o=xi_all[:, :, s * 128:(s + 1) * 128], i=ptr_view:
            e.tensor_copy(o, i))
        emit_passA(pa_sched[s])

    pe_after_p1 = P.val("PE")

    P.w("tensor", "GP", P.val("GP"))
    pe_g3 = {}
    for mi in range(3):
        pe_g3[mi] = P.inc("tensor", "PE", lambda e, mi=mi: e.matmul(
            p_b[0:1, 640 + mi:641 + mi], ccol(0, C_RA0 + mi), onesc[:, :],
            start=True, stop=True))
    P.w("vector", "PE", pe_g3[2])
    P.do("vector", lambda e: e.tensor_scalar(
        g11[:1, 0:3], p_b[0:1, 640:643], 1.0 / 16777216.0, 1e-5,
        ALU.mult, ALU.max))
    dv_g11 = P.inc("vector", "DV", lambda e: (e.drain(), e.engine_nop())[1])
    P.w("tensor", "DV", dv_g11)
    pe_gb = P.inc("tensor", "PE", lambda e: e.matmul(
        p_b[:, 644:647], ones1[:1, :], g11[:1, 0:3], start=True, stop=True))
    P.w("vector", "PE", pe_gb)
    for gi, (cth, csc, cis) in enumerate(
            [(C_SCG, C_SCG, C_ISG), (C_SCV, C_SCV, C_ISV), (C_SCO, C_SCO, -1)]):
        P.do("vector", lambda e, gi=gi, csc=csc: e.tensor_single_scalar(
            col(csc), p_b[:, 644 + gi:645 + gi], 0.5, ALU.mult))
        P.do("vector", lambda e: e.drain())
        if cis >= 0:
            P.do("vector", lambda e, csc=csc, cis=cis: e.reciprocal(
                col(cis), col(csc)))
            P.do("vector", lambda e: e.drain())
    TH = {}
    for mi, csc in ((0, C_SCG), (1, C_SCV), (2, C_SCO)):
        cp = C_SA0 + mi
        cn = C_SB0 + mi
        P.do("vector", lambda e, cp=cp, csc=csc: e.tensor_copy(
            col(cp), col(csc)))
        P.do("vector", lambda e: e.drain())
        P.do("vector", lambda e, cp=cp, cn=cn: e.tensor_single_scalar(
            col(cn), col(cp), -1.0, ALU.mult))
        P.do("vector", lambda e: e.drain())
        TH[mi] = (cp, cn)
    P.do("vector", lambda e: e.tensor_single_scalar(
        cols[:, C_SGC:C_SGC + 16], cols[:, C_GXI:C_GXI + 16], col(C_SCG),
        ALU.mult))
    P.do("vector", lambda e: e.drain())
    P.do("vector", lambda e: e.tensor_single_scalar(
        cols[:, C_SVC:C_SVC + 16], cols[:, C_GXI:C_GXI + 16], col(C_SCV),
        ALU.mult))
    P.do("vector", lambda e: e.drain())
    rows_g = lambda s: af(O_RES + s * 128, 128, 0, 1)
    rows_v = lambda s: af(O_X1 + s * 128, 128, 0, 1)
    P.w("vector", "SX0", st_x1[NS - 2])
    P.w("vector", "SX1", st_x1[NS - 1])
    for mi in range(2):
        base = C_SGC if mi == 0 else C_SVC
        rows = rows_g if mi == 0 else rows_v
        for s in range(NS):
            P.do("vector", lambda e, base=base, s=s: e.reciprocal(
                col(C_T0), ccol(base, s)))
            P.do("vector", lambda e: e.drain())
            dv_iv = P.inc("vector", "DV", lambda e: (e.drain(),
                                                     e.engine_nop())[1])
            P.w("tensor", "DV", dv_iv)
            pe_tv = P.inc("tensor", "PE", lambda e: e.transpose(
                p_b[0:1, 648:776], col(C_T0), idf[:, :]))
            P.w("vector", "PE", pe_tv)
            P.do("vector", lambda e: e.drain())
            P.inc("vector", "DV", lambda e, ro=rows(s): e.tensor_copy(
                ro, p_b[0:1, 648:776]))
    dv_rows = P.inc("vector", "DV", lambda e: (e.drain(), e.engine_nop())[1])

    wlb = [af(O_WL, DM), af(O_WL + 2048, DM)]
    pqb = af(O_PQ, DM)
    gs_slab = af(O_GS, 4096)
    sgh = [ab(B_SG, 1024), ab(B_SG + 1024, 1024)]
    oqb = ab(B_OQ, DM)
    gwq2 = [B[:, B_GWA:B_GWA + 8192], B[:, B_GWB:B_GWB + 8192]]
    hbuf2 = [af(O_UB, 512), af(O_UB + 512, 512)]
    brb = [af(O_BR, 512, 0, 1), af(O_BR + 512, 512, 0, 1)]
    brbD = [af(O_DP1, 512, 0, 1), af(O_DP1 + 512, 512, 0, 1)]
    P.do("vector", lambda e: e.memset(cols[:, C_MAC:C_MAC + 16], 0.0))
    dv_mz = P.inc("vector", "DV", lambda e: (e.drain(), e.engine_nop())[1])

    pq_ac = {}
    pq_st = {}
    chain = {"dv": 0}

    def emit_prequant(jcs):
        for jc in jcs:
            if jc >= NJC:
                continue
            if jc >= 1:
                P.w("sync", "AC", pq_ac[jc - 1])
            lq = P.inc("sync", "LQ%d" % (jc % 2), lambda e, jc=jc:
                       e.dma_start(pqb, flo[jc, :, :]), 16)
            P.w("scalar", "LQ%d" % (jc % 2), lq)
            if chain["dv"]:
                P.w("scalar", "DV", chain["dv"])
            P.do("scalar", lambda e: e.activation(
                ab(B_SG, DM), pqb, AF.Sign, bias=col(C_SB0 + 2)))
            if jc >= 1:
                P.w("scalar", "SQ%d" % ((jc - 1) % 2), pq_st[jc - 1])
            acq = P.inc("scalar", "AC", lambda e: e.activation(
                oqb, pqb, AF.Sign, bias=col(C_SA0 + 2)))
            P.w("vector", "AC", acq)
            dvq = P.inc("vector", "DV", lambda e: e.tensor_tensor(
                oqb, ab(B_SG, DM), oqb, ALU.add))
            chain["dv"] = dvq
            pq_ac[jc] = acq
            P.w("sync", "DV", dvq)
            pq_st[jc] = P.inc("sync", "SQ%d" % (jc % 2),
                              lambda e, jc=jc: e.dma_start(oq_scr[jc, :, :],
                                                           oqb), 16)

    P.w("scalar", "DV", dv_rows)
    P.w("sync", "DV", dv_rows)
    P.w("sync", "PE", pe_after_p1)
    gwt_r = gwt_h.rearrange("(k p) x -> p k x", p=128)
    vwt_r = vwt_h.rearrange("(k p) x -> p k x", p=128)
    st_h = {}
    lw_ac = {}
    pe_slab_free = {}
    pe_burst = {}
    dv_quant = {}
    epi_done = {}
    lr_hist = {}
    pq_next = 0
    jq_box = [0]
    tc = 0
    sg0q = ab(B_SG, DM)
    sg1q = ab(B_OQ, DM)

    def quantize(hg, mat, gate_pe):
        wr = gwt_r if mat == 0 else vwt_r
        cp, cn = TH[mat]
        wsl = gwq2[mat]
        first = True
        for c in range(4):
            jq = jq_box[0]
            bp = jq % 2
            if jq >= 2:
                P.w("sync", "AC", lw_ac[jq - 2])
            lw = P.inc("sync", "LW%d" % bp, lambda e, wr=wr, c=c, hg=hg,
                       bp=bp: e.dma_start(
                           wlb[bp].rearrange("p (k x) -> p k x", x=512),
                           wr[:, c * 4:c * 4 + 4,
                              hg * 512:(hg + 1) * 512]), 16)
            P.w("scalar", "LW%d" % bp, lw)
            if chain["dv"]:
                P.w("scalar", "DV", chain["dv"])
            if first:
                P.w("scalar", "SQ0", P.val("SQ0"))
                P.w("scalar", "SQ1", P.val("SQ1"))
            P.do("scalar", lambda e, bp=bp, cn=cn: e.activation(
                sg0q, wlb[bp], AF.Sign, bias=col(cn)))
            acs = P.inc("scalar", "AC", lambda e, bp=bp, cp=cp: e.activation(
                sg1q, wlb[bp], AF.Sign, bias=col(cp)))
            P.w("vector", "AC", acs)
            if first and gate_pe is not None:
                P.w("vector", "PE", gate_pe)
            dvs2 = P.inc("vector", "DV", lambda e, c=c, wsl=wsl:
                         e.tensor_tensor(wsl[:, c * 2048:(c + 1) * 2048],
                                         sg0q, sg1q, ALU.add))
            chain["dv"] = dvs2
            lw_ac[jq] = acs
            jq_box[0] = jq + 1
            first = False
        dv_quant[(hg, mat)] = chain["dv"]

    P.w("scalar", "PE", pe_after_p1)
    quantize(0, 0, None)
    quantize(0, 1, None)
    for hg in range(NHG):
        bpair = brb if hg % 2 == 0 else brbD
        if hg == 1:
            P.w("sync", "DV", dv_xi_done[NS - 1])
        if hg >= 2:
            P.w("sync", "PE", pe_burst[(hg - 2, 1, 1)])
        lrg = P.inc("sync", "LR0", lambda e, hg=hg, b=bpair[0]: e.dma_start(
            b, gbr_h[:, hg * 512:(hg + 1) * 512]), 16)
        lrv = P.inc("sync", "LR1", lambda e, hg=hg, b=bpair[1]: e.dma_start(
            b, vbr_h[:, hg * 512:(hg + 1) * 512]), 16)
        lr_hist[hg] = (lrg, lrv)
        for half in range(2):
            for mat in range(2):
                wsl = gwq2[mat]
                P.w("tensor", "DV", dv_quant[(hg, mat)])
                if hg == 0 and half == 0 and mat == 0:
                    P.w("tensor", "DV", dv_xi_done[NS - 1])
                    P.w("tensor", "DV", dv_mz)
                P.w("tensor", "LR%d" % mat, lr_hist[hg][mat])
                for i8 in range(8):
                    t = half * 8 + i8
                    bank = tc % 6
                    psl = (p_a[:, bank * 512:(bank + 1) * 512] if bank < 4
                           else p_b[:, (bank - 4) * 512:(bank - 3) * 512])
                    if tc >= 6:
                        kind, val = epi_done[tc - 6]
                        P.w("tensor", kind, val)
                    ro = (rows_g if mat == 0 else rows_v)(t)
                    bpr = (brb if hg % 2 == 0 else brbD)[mat]
                    P.do("tensor", lambda e, psl=psl, ro=ro, b=bpr:
                         e.matmul(psl, ro, b, start=True, stop=False))
                    for k in range(NK):
                        fn = lambda e, k=k, t=t, psl=psl, wsl=wsl: e.matmul(
                            psl,
                            B[:, B_XI + k * 2048 + t * 128:
                               B_XI + k * 2048 + t * 128 + 128],
                            wsl[:, k * 512:(k + 1) * 512],
                            start=False, stop=(k == 15))
                        if k == 15:
                            pe_mm = P.inc("tensor", "PE", fn)
                        else:
                            P.do("tensor", fn)
                    slot = t % 8
                    if mat == 0:
                        P.w("scalar", "PE", pe_mm)
                        if hg > 0 or half > 0:
                            back = 8 if half == 1 else 16
                            P.w("scalar", "DV", epi_done[tc - back][1])
                        ac_e = P.inc(
                            "scalar", "AC", lambda e, slot=slot, psl=psl,
                            t=t: e.activation(
                                gs_slab[:, slot * 512:(slot + 1) * 512],
                                psl, AF.Silu, scale=ccol(C_SGC, t)))
                        epi_done[tc] = ("AC", ac_e)
                    else:
                        P.w("vector", "PE", pe_mm)
                        hp = t % 2
                        nvt = hg * 16 + t
                        if nvt >= 2:
                            P.w("vector", "SH%d" % hp,
                                st_h[((nvt - 2) // 16, (nvt - 2) % 16)])
                        P.w("vector", "AC", epi_done[tc - 8][1])
                        P.do("vector", lambda e: e.drain())
                        P.do("vector", lambda e, slot=slot, psl=psl, hp=hp,
                             t=t: e.scalar_tensor_tensor(
                                 hbuf2[hp], psl, ccol(C_SVC, t),
                                 gs_slab[:, slot * 512:(slot + 1) * 512],
                                 ALU.mult, ALU.mult))
                        P.do("vector", lambda e: e.drain())
                        P.do("vector", lambda e, hp=hp: e.tensor_reduce(
                            col(C_T0), hbuf2[hp], AX.X, ALU.max,
                            apply_absolute_value=True))
                        P.do("vector", lambda e: e.drain())
                        dv_h = P.inc("vector", "DV", lambda e, t=t:
                                     e.tensor_tensor(ccol(C_MAC, t),
                                                     ccol(C_MAC, t),
                                                     col(C_T0), ALU.max))
                        epi_done[tc] = ("DV", dv_h)
                        P.w("sync", "DV", dv_h)
                        st_h[(hg, t)] = P.inc(
                            "sync", "SH%d" % hp,
                            lambda e, t=t, hg=hg, hp=hp: e.dma_start(
                                h_scr[t, :, hg * 512:(hg + 1) * 512],
                                hbuf2[hp]), 16)
                    tc += 1
                pe_burst[(hg, half, mat)] = P.val("PE")
                if half == 1 and mat == 0 and hg + 1 < NHG:
                    quantize(hg + 1, 0, pe_burst[(hg, 1, 0)])
        if hg + 1 < NHG:
            quantize(hg + 1, 1, pe_burst[(hg, 1, 1)])
        emit_prequant(list(range(pq_next, pq_next + 4)))
        pq_next += 4

    P.w("vector", "AC", pq_ac[NJC - 1])
    P.do("vector", lambda e: e.drain())
    P.do("vector", lambda e: e.tensor_scalar(
        cols[:, C_MAC:C_MAC + 16], cols[:, C_MAC:C_MAC + 16], 1.0, 1e-5,
        ALU.mult, ALU.max))
    P.do("vector", lambda e: e.drain())
    P.do("vector", lambda e: e.tensor_scalar(
        cols[:, C_S2:C_S2 + 16], cols[:, C_MAC:C_MAC + 16], 1.0 / 127.0, 0.0,
        ALU.mult, ALU.add))
    P.do("vector", lambda e: e.drain())
    P.do("vector", lambda e: e.reciprocal(
        cols[:, C_MAC:C_MAC + 16], cols[:, C_S2:C_S2 + 16]))
    P.do("vector", lambda e: e.drain())
    P.do("vector", lambda e: e.tensor_single_scalar(
        cols[:, C_S2:C_S2 + 16], cols[:, C_S2:C_S2 + 16], col(C_SCO),
        ALU.mult))
    P.do("vector", lambda e: e.drain())
    rows_o = lambda s: af(O_PQ + s * 128, 128, 0, 1)
    for s in range(NS):
        P.do("vector", lambda e, s=s: e.reciprocal(col(C_T0), ccol(C_S2, s)))
        P.do("vector", lambda e: e.drain())
        dv_iv = P.inc("vector", "DV", lambda e: (e.drain(),
                                                 e.engine_nop())[1])
        P.w("tensor", "DV", dv_iv)
        if s == 0:
            P.w("tensor", "PE", pe_burst[(NHG - 1, 1, 1)])
        pe_tv = P.inc("tensor", "PE", lambda e: e.transpose(
            p_b[0:1, 648:776], col(C_T0), idf[:, :]))
        P.w("vector", "PE", pe_tv)
        P.do("vector", lambda e: e.drain())
        P.inc("vector", "DV", lambda e, ro=rows_o(s): e.tensor_copy(
            ro, p_b[0:1, 648:776]))
    dv_orows = P.inc("vector", "DV", lambda e: (e.drain(), e.engine_nop())[1])

    NE = 8
    hbq = [af(O_UB, DM), af(O_UT, DM)]
    lCc = [af(O_LC, DM), af(O_WL, DM)]
    lYc = [af(O_LY, DM), af(O_WL + 2048, DM)]
    sgc = [ab(B_SG, DM), ab(B_OQ, DM)]
    xtmp = af(O_RES, DM)
    x1r2 = [af(O_GS, DM), af(O_GS + 2048, DM)]
    z2 = [af(O_DP1, DM), af(O_X1, DM)]
    bro = af(O_BR, 1024, 0, 1)
    owqb = [B[:, B_GWA + i * 4096:B_GWA + (i + 1) * 4096] for i in range(2)]
    oq_r = oq_scr.rearrange("j p d -> p j d")
    hitH = [B[:, B_XI + i * 16384:B_XI + (i + 1) * 16384] for i in range(2)]
    hitH_v = [hitH[i].rearrange("p (jc t) -> p jc t", t=256) for i in range(2)]
    sq2 = ab(B_SG, DM)

    dv_lad = {}
    dv_evac = {}
    dv_evac_all = {}
    dv_z_round = {}
    st_out = {}
    dv_ln2 = {}
    lh_vals = {}
    lb_vals = {}
    lr_vals = {}
    lx_vals = {}
    tr_emitters = {}

    def emit_lh(e, p):
        pc = e * 8 + p
        c, ch = p // 2, p % 2
        st = 2 * e + ch
        bp = pc % 2
        if pc >= 2:
            P.w("sync", "PL", pc - 1)
        if pc == 0:
            P.w("sync", "SH0", P.val("SH0"))
            P.w("sync", "SH1", P.val("SH1"))
        lh_vals[pc] = P.inc("sync", "LH%d" % bp, lambda e_, st=st, c=c,
                            bp=bp: e_.dma_start(
                                hbq[bp],
                                h_scr[st, :, c * 2048:(c + 1) * 2048]), 16)

    def emit_p3_sync(e):
        em = e - 1
        if em >= 0:
            nd0 = em * 2
            if nd0 >= 1:
                P.w("sync", "PG", (nd0 - 1) * 16 + 1)
            lr_vals[nd0] = P.inc("sync", "LR%d" % (nd0 % 2), lambda e_:
                                 e_.dma_start(bro, obr_h[:, 0:1024]), 16)
            for t2 in range(2):
                st = 2 * em + t2
                if em > 0:
                    P.w("sync", "DV", dv_ln2[(em - 1, t2)])
                elif t2 == 0:
                    P.w("sync", "SX0", P.val("SX0"))
                    P.w("sync", "SX1", P.val("SX1"))
                lx_vals[(em, t2)] = P.inc(
                    "sync", "LX%d" % t2, lambda e_, st=st, t2=t2:
                    e_.dma_start(x1r2[t2], x1_scr[st, :, :]), 16)
            for g in range(16):
                nload = em * 32 + g
                if nload >= 2:
                    P.w("sync", "PG", nload - 1)
                if em == 0 and g == 0:
                    P.w("sync", "SQ0", P.val("SQ0"))
                    P.w("sync", "SQ1", P.val("SQ1"))
                lb_vals[(em, 0, g)] = P.inc(
                    "sync", "LB%d" % (g % 2), lambda e_, g=g: e_.dma_start(
                        owqb[g % 2].rearrange("p (j d) -> p j d", d=1024),
                        oq_r[:, g * 4:g * 4 + 4, 0:1024]), 16)
                if e < NE and 1 <= g <= 8:
                    emit_lh(e, g - 1)
            nd1 = em * 2 + 1
            P.w("sync", "PG", (nd1 - 1) * 16 + 1)
            lr_vals[nd1] = P.inc("sync", "LR%d" % (nd1 % 2), lambda e_:
                                 e_.dma_start(bro, obr_h[:, 1024:2048]), 16)
            for g in range(16):
                nload = em * 32 + 16 + g
                P.w("sync", "PG", nload - 1)
                lb_vals[(em, 1, g)] = P.inc(
                    "sync", "LB%d" % (g % 2), lambda e_, g=g: e_.dma_start(
                        owqb[g % 2].rearrange("p (j d) -> p j d", d=1024),
                        oq_r[:, g * 4:g * 4 + 4, 1024:2048]), 16)
        else:
            for p in range(8):
                emit_lh(e, p)

    def emit_ladder(e):
        hh = e % 2
        emitters = []
        for c in range(4):
            pcs = [e * 8 + c * 2, e * 8 + c * 2 + 1]
            for ch in range(2):
                pc = pcs[ch]
                st = 2 * e + ch
                bp = pc % 2
                P.w("vector", "LH%d" % bp, lh_vals[pc])
                if pc == 0:
                    P.w("vector", "DV", dv_orows)
                    P.w("vector", "SQ0", P.val("SQ0"))
                    P.w("vector", "SQ1", P.val("SQ1"))
                r2c = ccol(C_MAC, st)
                P.do("vector", lambda e_: e_.drain())
                P.inc("vector", "PL", lambda e_, bp=bp, r2c=r2c, lY=lYc[ch]:
                      e_.tensor_single_scalar(lY, hbq[bp], r2c, ALU.mult))
            dv_prev = [0, 0]
            for j in range(8):
                ac_j = [0, 0]
                for ch in range(2):
                    pc = pcs[ch]
                    src_c = lYc[ch] if j == 0 else lCc[ch]
                    if j == 0:
                        P.w("scalar", "PL", pc + 1)
                        if pc >= 2:
                            P.w("scalar", "PT", pc - 1)
                    else:
                        P.w("scalar", "DV", dv_prev[ch])
                    ac_j[ch] = P.inc("scalar", "AC",
                                     lambda e_, src_c=src_c, sg=sgc[ch]:
                                     e_.activation(sg, src_c, AF.Sign,
                                                   bias=col(C_HALF)))
                for ch in range(2):
                    src_c = lYc[ch] if j == 0 else lCc[ch]
                    P.w("vector", "AC", ac_j[ch])
                    dv_prev[ch] = P.inc(
                        "vector", "DV",
                        lambda e_, j=j, src_c=src_c, sg=sgc[ch], lC=lCc[ch]:
                        e_.scalar_tensor_tensor(lC, sg, col(C_K0 + j), src_c,
                                                ALU.mult, ALU.add))
            for ch in range(2):
                pc = pcs[ch]
                c_, ch_ = c, ch
                P.do("vector", lambda e_: e_.drain())
                P.do("vector", lambda e_, lY=lYc[ch], lC=lCc[ch]:
                     e_.tensor_tensor(xtmp, lY, lC, ALU.subtract))
                P.do("vector", lambda e_: e_.drain())
                dv_lad[pc] = P.inc("vector", "DV", lambda e_, sg=sgc[ch]:
                                   e_.tensor_scalar(sg, xtmp, 1.0, -0.5,
                                                    ALU.mult, ALU.add))

                def tr_emit(pc=pc, sg=sgc[ch]):
                    P.w("tensor", "DV", dv_lad[pc])
                    if pc >= 1:
                        P.w("tensor", "DV", dv_evac[pc - 1])
                    for kk in range(16):
                        fn = lambda e_, kk=kk, sg=sg: e_.transpose(
                            p_tr[:, kk * 128:(kk + 1) * 128],
                            sg[:, kk * 128:(kk + 1) * 128], idb[:, :])
                        if kk == 15:
                            P.inc("tensor", "PT", fn)
                        else:
                            P.do("tensor", fn)
                emitters.append(tr_emit)
                P.w("vector", "PT", pc + 1)
                if e >= 2 and pc == e * 8:
                    P.w("vector", "PG", (e - 1) * 32)
                ptr_view = p_tr[:, :].rearrange("p (k t) -> p k t", t=128)
                dv_evac[pc] = P.inc(
                    "vector", "DV",
                    lambda e_, c=c_, ch=ch_, hh=hh, i=ptr_view:
                    e_.tensor_copy(hitH_v[hh][:, c * 16:(c + 1) * 16,
                                              ch * 128:(ch + 1) * 128], i))
        dv_evac_all[e] = P.val("DV")
        tr_emitters[e] = emitters

    def emit_mm(em, emitters):
        hh = em % 2
        spl = 0
        for r in range(2):
            nd = em * 2 + r
            for grp in range(16):
                gbp = grp % 2
                if r == 0 and grp % 2 == 1 and spl < len(emitters):
                    emitters[spl]()
                    spl += 1
                P.w("tensor", "LB%d" % gbp, lb_vals[(em, r, grp)])
                if grp == 0:
                    P.w("tensor", "LR%d" % (nd % 2), lr_vals[nd])
                    if r == 0:
                        P.w("tensor", "DV", dv_evac_all[em])
                    if r >= 1:
                        P.w("tensor", "DV", dv_z_round[(em, 0)])
                    elif em > 0:
                        P.w("tensor", "DV", dv_z_round[(em - 1, 1)])
                    for t2 in range(2):
                        st = 2 * em + t2
                        for di in range(2):
                            bank = t2 * 2 + di
                            P.do("tensor", lambda e_, bank=bank, st=st,
                                 di=di: e_.matmul(
                                     p_a[:, bank * 512:(bank + 1) * 512],
                                     rows_o(st),
                                     bro[:, di * 512:(di + 1) * 512],
                                     start=True, stop=False))
                for jj in range(4):
                    jc = grp * 4 + jj
                    for t2 in range(2):
                        for di in range(2):
                            bank = t2 * 2 + di
                            fn = lambda e_, jc=jc, t2=t2, di=di, bank=bank, \
                                gbp=gbp, jj=jj, hh=hh: e_.matmul(
                                    p_a[:, bank * 512:(bank + 1) * 512],
                                    hitH[hh][:, jc * 256 + t2 * 128:
                                             jc * 256 + t2 * 128 + 128],
                                    owqb[gbp][:, jj * 1024 + di * 512:
                                              jj * 1024 + di * 512 + 512],
                                    start=False, stop=(jc == NJC - 1))
                            if jj == 3 and t2 == 1 and di == 1:
                                P.inc("tensor", "PG", fn)
                            else:
                                P.do("tensor", fn)
            for t2 in range(2):
                st = 2 * em + t2
                for di in range(2):
                    dmg = r * 2 + di
                    bank = t2 * 2 + di
                    P.w("vector", "PG", nd * 16 + 16)
                    if r == 0 and di == 0:
                        P.w("vector", "LX%d" % t2, lx_vals[(em, t2)])
                        if em > 0:
                            P.w("vector", "SO%d" % ((2 * (em - 1) + t2) % 2),
                                st_out[2 * (em - 1) + t2])
                    s2c = ccol(C_S2, st)
                    P.do("vector", lambda e_: e_.drain())
                    dv_z = P.inc("vector", "DV",
                                 lambda e_, t2=t2, dmg=dmg, s2c=s2c,
                                 bank=bank:
                                 e_.scalar_tensor_tensor(
                                     z2[t2][:, dmg * 512:(dmg + 1) * 512],
                                     p_a[:, bank * 512:(bank + 1) * 512],
                                     s2c,
                                     x1r2[t2][:, dmg * 512:(dmg + 1) * 512],
                                     ALU.mult, ALU.add))
            dv_z_round[(em, r)] = dv_z
        for t2 in range(2):
            st = 2 * em + t2
            P.w("vector", "DV", dv_z_round[(em, 1)])
            zt = z2[t2]
            mcol, vcol, rcol = col(C_T0), col(C_T1), col(C_T3)
            P.do("vector", lambda e_: e_.drain())
            P.do("vector", lambda e_, zt=zt: e_.tensor_reduce(mcol, zt, AX.X,
                                                              ALU.add))
            P.do("vector", lambda e_: e_.drain())
            P.do("vector", lambda e_: e_.tensor_single_scalar(mcol, mcol,
                                                              INV_D,
                                                              ALU.mult))
            P.do("vector", lambda e_: e_.drain())
            dv_l2 = P.inc("vector", "DV", lambda e_, zt=zt:
                          e_.tensor_single_scalar(zt, zt, mcol,
                                                  ALU.subtract))
            P.w("scalar", "DV", dv_l2)
            P.do("scalar", lambda e_, zt=zt: e_.activation(sq2, zt, AF.Square,
                                                           accum_out=vcol))
            P.do("scalar", lambda e_: e_.drain())
            P.do("scalar", lambda e_: e_.activation(col(C_T2), vcol, AF.Ln,
                                                    bias=col(C_EPS),
                                                    scale=INV_D))
            P.do("scalar", lambda e_: e_.drain())
            ac_l2 = P.inc("scalar", "AC", lambda e_: e_.activation(
                rcol, col(C_T2), AF.Exp, scale=-0.5))
            P.w("vector", "AC", ac_l2)
            P.do("vector", lambda e_: e_.drain())
            dv_o = P.inc("vector", "DV", lambda e_, zt=zt:
                         e_.tensor_single_scalar(zt, zt, rcol, ALU.mult))
            dv_ln2[(em, t2)] = dv_o
            P.w("sync", "DV", dv_o)
            st_out[st] = P.inc("sync", "SO%d" % (st % 2),
                               lambda e_, st=st, zt=zt: e_.dma_start(
                                   out_h[st * 128:(st + 1) * 128, :], zt), 16)

    P.w("sync", "PE", pe_burst[(NHG - 1, 1, 1)])
    for e in range(NE + 1):
        emit_p3_sync(e)
        if e < NE:
            emit_ladder(e)
        if e == 0:
            for em_ in tr_emitters[0]:
                em_()
            tr_emitters[0] = []
        if e >= 1:
            emit_mm(e - 1, tr_emitters[e] if e < NE else [])

    P.w("sync", "SO0", P.val("SO0"))
    P.w("sync", "SO1", P.val("SO1"))

    with nc.Block() as block:
        @block.sync
        def _(eng):
            for f in P.ops["sync"]:
                f(eng)

        @block.tensor
        def _(eng):
            for f in P.ops["tensor"]:
                f(eng)

        @block.vector
        def _(eng):
            for f in P.ops["vector"]:
                f(eng)

        @block.scalar
        def _(eng):
            for f in P.ops["scalar"]:
                f(eng)

        @block.gpsimd
        def _(eng):
            for f in P.ops["gpsimd"]:
                f(eng)

    ctx.close()
    return nc


_NC_CACHE = None


def kernel(L0_attn_u, L0_attn_A_factors, L0_attn_W_b_rank, L0_attn_W_b_state,
           L0_attn_W_c_rank, L0_attn_W_c_model, L0_attn_D_vec,
           L0_ffn_gate_w, L0_ffn_gate_b, L0_ffn_val_w, L0_ffn_val_b,
           L0_ffn_out_w, L0_ffn_out_b):
    global _NC_CACHE
    u = np.ascontiguousarray(np.asarray(L0_attn_u, dtype=np.float32).reshape(-1, DM))
    gw = np.asarray(L0_ffn_gate_w, dtype=np.float32)
    vw = np.asarray(L0_ffn_val_w, dtype=np.float32)
    ow = np.asarray(L0_ffn_out_w, dtype=np.float32)
    shared = dict(
        gwt=np.ascontiguousarray(gw.T),
        vwt=np.ascontiguousarray(vw.T),
        owt=np.ascontiguousarray(ow.T),
        gbr=np.ascontiguousarray(np.asarray(L0_ffn_gate_b,
                                            dtype=np.float32).reshape(1, HID)),
        vbr=np.ascontiguousarray(np.asarray(L0_ffn_val_b,
                                            dtype=np.float32).reshape(1, HID)),
        obr=np.ascontiguousarray(np.asarray(L0_ffn_out_b,
                                            dtype=np.float32).reshape(1, DM)),
        wbr=np.ascontiguousarray(np.asarray(L0_attn_W_b_rank, dtype=np.float32)),
        wbs=np.ascontiguousarray(np.asarray(L0_attn_W_b_state, dtype=np.float32)),
        wcr=np.ascontiguousarray(np.asarray(L0_attn_W_c_rank, dtype=np.float32)),
        wcm=np.ascontiguousarray(np.asarray(L0_attn_W_c_model, dtype=np.float32)),
        afac=np.ascontiguousarray(
            np.asarray(L0_attn_A_factors, dtype=np.float32).reshape(1, 32)),
        dvec=np.ascontiguousarray(
            np.asarray(L0_attn_D_vec, dtype=np.float32).reshape(1, DM)),
        idf=np.eye(128, dtype=np.float32),
        idb=np.eye(128, dtype=np.float32).astype(BF),
    )
    if _NC_CACHE is None:
        _NC_CACHE = build_program()
    in_maps = []
    for c in range(NCORE):
        uc = np.ascontiguousarray(u[c * TOK:(c + 1) * TOK, :])
        in_maps.append(dict(u=uc, ut=np.ascontiguousarray(uc.T), **shared))
    res = run_bass_kernel_spmd(_NC_CACHE, in_maps, core_ids=list(range(NCORE)))
    out = np.concatenate(
        [np.asarray(r["out"], dtype=np.float32) for r in res.results], axis=0)
    return out.reshape(4, 4096, DM)


# version 11
# speedup vs baseline: 1.5480x; 1.0702x over previous
import contextlib
import numpy as np
import ml_dtypes
import concourse.bass as bass
import concourse.mybir as mybir
from concourse.bass_utils import run_bass_kernel_spmd

F32 = mybir.dt.float32
BF16 = mybir.dt.bfloat16
BF = ml_dtypes.bfloat16
AF = mybir.ActivationFunctionType
ALU = mybir.AluOpType
AX = mybir.AxisListType

NCORE = 8
TOK = 2048
DM = 2048
HID = 8192
NS = 16
NHG = 16
NK = 16
NJC = 64
NQ = 4
INV_D = 1.0 / DM
EPS = 1e-5
LADK = (128.0, 64.0, 32.0, 16.0, 8.0, 4.0, 2.0, 1.0)

O_COLS = 0
O_DP1 = 128
O_UB = 2176
O_UT = 4224
O_RES = 6272
O_X1 = 8320
O_LC = 10368
O_LY = 12416
O_GS = 14464
O_WL = 18560
O_PQ = 22656
O_BR = 24704
A_W = 25728
B_XI = 0
B_GWA = 32768
B_GWB = 40960
B_SG = 49152
B_OQ = 51200
B_W = 53248

(C_EPS, C_HALF, C_T0, C_T1, C_T2, C_T3, C_SCG, C_SCV, C_SCO,
 C_ISG, C_ISV, C_RA0, C_RA1, C_RA2, C_SA0, C_SA1, C_SA2, C_SB0, C_SB1,
 C_SB2, C_K0) = range(21)
C_MAC = 32
C_SXS = 48
C_SGC = 64
C_SVC = 80
C_S2 = 96
C_GXI = 112
NCOL = 128


class Prog:
    ENGS = ("sync", "tensor", "vector", "scalar", "gpsimd")

    def __init__(self):
        self.ops = {e: [] for e in self.ENGS}
        self.cnt = {}
        self.sems = {}

    def w(self, eng, sname, val):
        if val <= 0:
            return
        self.ops[eng].append(lambda e, s=self.sems[sname], v=val: e.wait_ge(s, v))

    def do(self, eng, fn):
        self.ops[eng].append(fn)

    def inc(self, eng, sname, fn, amt=1):
        self.cnt[sname] = self.cnt.get(sname, 0) + amt
        self.ops[eng].append(
            lambda e, f=fn, s=self.sems[sname], a=amt: f(e).then_inc(s, a))
        return self.cnt[sname]

    def val(self, sname):
        return self.cnt.get(sname, 0)


def build_program(debug=False):
    nc = bass.Bass("TRN2", target_bir_lowering=False)

    u_h = nc.dram_tensor("u", [TOK, DM], F32, kind="ExternalInput").ap()
    ut_h = nc.dram_tensor("ut", [DM, TOK], F32, kind="ExternalInput").ap()
    gwt_h = nc.dram_tensor("gwt", [DM, HID], F32, kind="ExternalInput").ap()
    vwt_h = nc.dram_tensor("vwt", [DM, HID], F32, kind="ExternalInput").ap()
    owt_h = nc.dram_tensor("owt", [HID, DM], F32, kind="ExternalInput").ap()
    gbr_h = nc.dram_tensor("gbr", [1, HID], F32, kind="ExternalInput").ap()
    vbr_h = nc.dram_tensor("vbr", [1, HID], F32, kind="ExternalInput").ap()
    obr_h = nc.dram_tensor("obr", [1, DM], F32, kind="ExternalInput").ap()
    wbr_h = nc.dram_tensor("wbr", [DM, 64], F32, kind="ExternalInput").ap()
    wbs_h = nc.dram_tensor("wbs", [64, 256], F32, kind="ExternalInput").ap()
    wcr_h = nc.dram_tensor("wcr", [256, 64], F32, kind="ExternalInput").ap()
    wcm_h = nc.dram_tensor("wcm", [64, DM], F32, kind="ExternalInput").ap()
    a_h = nc.dram_tensor("afac", [1, 32], F32, kind="ExternalInput").ap()
    d_h = nc.dram_tensor("dvec", [1, DM], F32, kind="ExternalInput").ap()
    idf_h = nc.dram_tensor("idf", [128, 128], F32, kind="ExternalInput").ap()
    idb_h = nc.dram_tensor("idb", [128, 128], BF16, kind="ExternalInput").ap()
    out_h = nc.dram_tensor("out", [TOK, DM], F32, kind="ExternalOutput").ap()
    scr_kind = "ExternalOutput" if debug else "Internal"
    x1_scr = nc.dram_tensor("x1_scr", [NS, 128, DM], F32, kind=scr_kind).ap()
    h_scr = nc.dram_tensor("h_scr", [NS, 128, HID], F32, kind=scr_kind).ap()
    oq_scr = nc.dram_tensor("oq_scr", [NJC, 128, DM], BF16, kind=scr_kind).ap()

    ctx = contextlib.ExitStack()
    A = ctx.enter_context(nc.sbuf_tensor("arena_f32", [128, A_W], F32))
    B = ctx.enter_context(nc.sbuf_tensor("arena_bf16", [128, B_W], BF16))

    def af(off, w, p0=0, p1=128):
        return A[p0:p1, off:off + w]

    def ab(off, w, p0=0, p1=128):
        return B[p0:p1, off:off + w]

    idf = ctx.enter_context(nc.sbuf_tensor("idf_s", [128, 128], F32))
    idb = ctx.enter_context(nc.sbuf_tensor("idb_s", [128, 128], BF16))
    ones1 = ctx.enter_context(nc.sbuf_tensor("ones1", [1, 128], F32))
    onesc = ctx.enter_context(nc.sbuf_tensor("onesc", [128, 1], F32))
    gxs = ctx.enter_context(nc.sbuf_tensor("gxs", [128, NS], F32))
    g11 = ctx.enter_context(nc.sbuf_tensor("g11", [1, 8], F32))

    p_a = ctx.enter_context(nc.psum_tensor("p_a", [128, 2048], F32))
    p_b = ctx.enter_context(nc.psum_tensor("p_b", [128, 1024], F32))
    p_tr = ctx.enter_context(nc.psum_tensor("p_tr", [128, 2048], BF16))

    P = Prog()
    semnames = ["PE", "DV", "AC", "GP", "L0",
                "LU0", "LU1", "LT0", "LT1", "LA0", "LA1", "LW0", "LW1",
                "LQ0", "LQ1", "LH0", "LH1", "LB0", "LB1", "LX0", "LX1",
                "LR0", "LR1", "LR2", "LR3", "PT", "PG", "PL",
                "SX0", "SX1", "SH0", "SH1", "SQ0", "SQ1", "SO0", "SO1"]
    for nm in semnames:
        P.sems[nm] = ctx.enter_context(nc.semaphore("s" + nm))

    cols = af(O_COLS, NCOL)

    def col(i):
        return A[:, O_COLS + i:O_COLS + i + 1]

    def ccol(base, i):
        return A[:, O_COLS + base + i:O_COLS + base + i + 1]

    M_ = O_WL
    wbrT = af(M_, 1024)
    wcm_f = af(M_ + 1024, DM, 0, 64)
    G_f = af(M_ + 3072, 64, 0, 64)
    t1b = af(M_ + 3136, 128, 0, 64)
    t2b = af(M_ + 3264, 128, 0, 64)
    wbs_t = af(O_PQ, 256, 0, 64)
    bb0 = af(O_PQ + 256, 256, 0, 64)
    bb1 = af(O_PQ + 512, 256, 0, 64)
    tmp256 = af(O_PQ + 768, 256, 0, 64)
    a_bc = af(O_PQ + 1024, 32, 0, 64)
    a_row = af(O_PQ + 1056, 32, 0, 1)
    btT = af(O_PQ + 1088, 128)
    wcr0 = af(O_PQ + 1216, 64)
    wcr1 = af(O_PQ + 1280, 64)
    d_row = af(O_RES, DM, 0, 1)
    dp1 = af(O_DP1, DM)

    ld0 = [(idf[:, :], idf_h[:, :]), (idb[:, :], idb_h[:, :]),
           (d_row, d_h[:, :]), (wbs_t, wbs_h[:, :]), (a_row, a_h[:, :]),
           (wcm_f, wcm_h[:, :]),
           (wbrT.rearrange("p (k r) -> p k r", r=64),
            wbr_h.rearrange("(k p) r -> p k r", p=128)),
           (wcr0, wcr_h[0:128, :]), (wcr1, wcr_h[128:256, :])]
    for dst, src in ld0:
        P.inc("sync", "L0", lambda e, d=dst, s=src: e.dma_start(d, s), 16)
    ld0_done = P.val("L0")

    P.do("vector", lambda e: e.memset(col(C_EPS), float(EPS)))
    P.do("vector", lambda e: e.memset(col(C_HALF), 0.5))
    for j, kk in enumerate(LADK):
        P.do("vector", lambda e, j=j, kk=kk: e.memset(col(C_K0 + j), -kk / 2.0))
    P.do("vector", lambda e: e.memset(ones1[:1, :], 1.0))
    P.do("vector", lambda e: e.memset(onesc[:, :], 1.0))
    P.do("vector", lambda e: e.memset(cols[:, C_RA0:C_RA0 + 3], 0.0))
    P.w("vector", "L0", ld0_done)
    dv_c0 = P.inc("vector", "DV", lambda e: (e.drain(), e.engine_nop())[1])

    P.w("tensor", "DV", dv_c0)
    pe_abc = P.inc("tensor", "PE", lambda e: e.matmul(
        p_b[0:128, 0:32], ones1[:1, 0:128], a_row, start=True, stop=True))
    P.w("vector", "PE", pe_abc)
    P.do("vector", lambda e: e.tensor_copy(a_bc, p_b[0:64, 0:32]))
    P.do("vector", lambda e: e.drain())
    P.do("vector", lambda e: e.tensor_copy(bb0, wbs_t))
    cur, nxt = bb0, bb1
    for lvl in range(8):
        wwid = 2 ** (7 - lvl)
        vi = cur.rearrange("p (b t w) -> p b t w", t=2, w=wwid)
        vo = nxt.rearrange("p (b t w) -> p b t w", t=2, w=wwid)
        vt = tmp256.rearrange("p (b t w) -> p b t w", t=2, w=wwid)
        for j in range(2):
            a0 = a_bc[:, lvl * 4 + j: lvl * 4 + j + 1]
            a1 = a_bc[:, lvl * 4 + 2 + j: lvl * 4 + 3 + j]
            P.do("vector", lambda e: e.drain())
            P.do("vector", lambda e, o=vt[:, :, j, :], i=vi[:, :, 0, :], s=a0:
                 e.tensor_single_scalar(o, i, s, ALU.mult))
            P.do("vector", lambda e: e.drain())
            P.do("vector", lambda e, o=vo[:, :, j, :], i=vi[:, :, 1, :], s=a1,
                 t=vt[:, :, j, :]:
                 e.scalar_tensor_tensor(o, i, s, t, ALU.mult, ALU.add))
        cur, nxt = nxt, cur
    dv_bfly = P.inc("vector", "DV", lambda e: (e.drain(), e.engine_nop())[1])

    P.w("tensor", "DV", dv_bfly)
    P.do("tensor", lambda e, i=cur: e.transpose(p_b[:, 0:64], i[:, 0:128],
                                                idf[0:64, 0:64]))
    pe_btt = P.inc("tensor", "PE", lambda e, i=cur: e.transpose(
        p_b[:, 64:128], i[:, 128:256], idf[0:64, 0:64]))
    P.w("vector", "PE", pe_btt)
    dv_btt = P.inc("vector", "DV", lambda e: e.tensor_copy(btT, p_b[:, 0:128]))
    P.w("tensor", "DV", dv_btt)
    P.do("tensor", lambda e: e.matmul(p_b[0:64, 256:320], btT[0:128, 0:64],
                                      wcr0, start=True, stop=False))
    pe_g = P.inc("tensor", "PE", lambda e: e.matmul(
        p_b[0:64, 256:320], btT[0:128, 64:128], wcr1, start=False, stop=True))
    P.w("vector", "PE", pe_g)
    dv_g = P.inc("vector", "DV", lambda e: e.tensor_copy(G_f, p_b[0:64, 256:320]))

    P.w("tensor", "DV", dv_g)
    dv_dpe = {}
    for i in range(4):
        if i >= 2:
            P.w("tensor", "DV", dv_dpe[i - 2])
        pe_d = P.inc("tensor", "PE", lambda e, i=i: e.matmul(
            p_a[:, (i % 2) * 512:(i % 2) * 512 + 512], ones1[:1, :],
            d_row[:, i * 512:(i + 1) * 512], start=True, stop=True))
        P.w("vector", "PE", pe_d)
        dv_dpe[i] = P.inc("vector", "DV", lambda e, i=i: e.tensor_scalar(
            dp1[:, i * 512:(i + 1) * 512],
            p_a[:, (i % 2) * 512:(i % 2) * 512 + 512],
            1.0, 1.0, ALU.mult, ALU.add))
    dv_dp1 = P.inc("vector", "DV", lambda e: (e.drain(), e.engine_nop())[1])

    ubuf = af(O_UB, DM)
    utb = af(O_UT, DM)
    res = af(O_RES, DM)
    x1b = af(O_X1, DM)
    lC = af(O_LC, DM)
    lY = af(O_LY, DM)
    sgt = ab(B_SG, DM)
    xt_bf = ab(B_OQ, DM)
    xi_all = B[:, B_XI:B_XI + NS * DM].rearrange("p (k t) -> p k t", t=TOK)
    pA_w = [af(O_GS, DM), af(O_GS + 2048, DM)]

    pa_chunks = []
    for mi, wh in ((0, gwt_h), (1, vwt_h)):
        fl = wh.rearrange("(c p) x -> c p x", p=128)
        for c in range(16):
            for s4 in range(4):
                pa_chunks.append((fl[c, :, s4 * 2048:(s4 + 1) * 2048], mi))
    flo = owt_h.rearrange("(c p) x -> c p x", p=128)
    for c in range(64):
        pa_chunks.append((flo[c, :, :], 2))
    NPA = len(pa_chunks)
    per = (NPA + NS - 1) // NS
    pa_sched = [list(range(s * per, min((s + 1) * per, NPA))) for s in range(NS)]

    la_ac = {}
    gp_add = {}
    abs_scr = [ab(B_GWA, DM), ab(B_GWA + 2048, DM)]

    def emit_passA(idxs):
        for j in idxs:
            src, mi = pa_chunks[j]
            bp = j % 2
            if j >= 2:
                P.w("sync", "AC", la_ac[j - 2])
            la = P.inc("sync", "LA%d" % bp,
                       lambda e, d=pA_w[bp], s=src: e.dma_start(d, s), 16)
            P.w("scalar", "LA%d" % bp, la)
            if j >= 2:
                P.w("scalar", "GP", gp_add[j - 2])
            acc = ccol(0, (C_SA0 if j % 2 == 0 else C_SB0) + mi)
            la_ac[j] = P.inc("scalar", "AC", lambda e, bp=bp, acc=acc, j=j:
                             e.activation(abs_scr[j % 2], pA_w[bp], AF.Abs,
                                          accum_out=acc))
            P.w("gpsimd", "AC", la_ac[j])
            gp_add[j] = P.inc("gpsimd", "GP", lambda e, mi=mi, acc=acc:
                              e.tensor_tensor(ccol(0, C_RA0 + mi),
                                              ccol(0, C_RA0 + mi), acc,
                                              ALU.add))
            P.do("gpsimd", lambda e: e.drain())

    st_x1 = {}
    dv_xi_done = {}
    dv_ures_s = {}
    pe_t1_s = {}
    dv_resadd = {}
    for s in range(NS):
        bp = s % 2
        if s >= 1:
            P.w("sync", "DV", dv_ures_s[s - 1])
        lu = P.inc("sync", "LU%d" % bp, lambda e, s=s: e.dma_start(
            ubuf, u_h[s * 128:(s + 1) * 128, :]), 16)
        if s >= 1:
            P.w("sync", "PE", pe_t1_s[s - 1])
        lt = P.inc("sync", "LT%d" % bp, lambda e, s=s: e.dma_start(
            utb.rearrange("p (k t) -> p k t", t=128),
            ut_h.rearrange("(k p) t -> p k t", p=128)
            [:, :, s * 128:(s + 1) * 128]), 16)
        P.w("tensor", "LT%d" % bp, lt)
        if s == 0:
            P.w("tensor", "DV", dv_dp1)
        for k in range(NK):
            fn = lambda e, k=k: e.matmul(
                p_b[0:64, 0:128], wbrT[:, k * 64:(k + 1) * 64],
                utb[:, k * 128:(k + 1) * 128], start=(k == 0), stop=(k == 15))
            if k == 15:
                pe_t1 = P.inc("tensor", "PE", fn)
            else:
                P.do("tensor", fn)
        pe_t1_s[s] = pe_t1
        P.w("vector", "PE", pe_t1)
        dv_t1 = P.inc("vector", "DV", lambda e: e.tensor_copy(
            t1b, p_b[0:64, 0:128]))
        P.w("tensor", "DV", dv_t1)
        pe_t2 = P.inc("tensor", "PE", lambda e: e.matmul(
            p_b[0:64, 512:640], G_f, t1b[0:64, :], start=True, stop=True))
        P.w("vector", "PE", pe_t2)
        dv_t2 = P.inc("vector", "DV", lambda e: e.tensor_copy(
            t2b, p_b[0:64, 512:640]))
        P.w("tensor", "DV", dv_t2)
        pe_y = {}
        for g in range(4):
            if s >= 1:
                P.w("tensor", "DV", dv_resadd[(s - 1, g)])
            pe_y[g] = P.inc("tensor", "PE", lambda e, g=g: e.matmul(
                p_a[:, g * 512:(g + 1) * 512], t2b[0:64, :],
                wcm_f[:, g * 512:(g + 1) * 512], start=True, stop=True))
        P.w("vector", "LU%d" % bp, lu)
        P.do("vector", lambda e: e.drain())
        P.do("vector", lambda e: e.tensor_tensor(res, ubuf, dp1, ALU.mult))
        dv_ures_s[s] = P.inc("vector", "DV",
                             lambda e: (e.drain(), e.engine_nop())[1])
        for g in range(4):
            P.w("vector", "PE", pe_y[g])
            P.do("vector", lambda e: e.drain())
            dv_resadd[(s, g)] = P.inc("vector", "DV", lambda e, g=g:
                                      e.tensor_tensor(
                                          res[:, g * 512:(g + 1) * 512],
                                          res[:, g * 512:(g + 1) * 512],
                                          p_a[:, g * 512:(g + 1) * 512],
                                          ALU.add))
        mcol, vcol, rcol = col(C_T0), col(C_T1), col(C_T3)
        P.do("vector", lambda e: e.drain())
        P.do("vector", lambda e: e.tensor_reduce(mcol, res, AX.X, ALU.add))
        P.do("vector", lambda e: e.drain())
        P.do("vector", lambda e: e.tensor_single_scalar(mcol, mcol, INV_D,
                                                        ALU.mult))
        P.do("vector", lambda e: e.drain())
        dv_ln = P.inc("vector", "DV", lambda e: e.tensor_single_scalar(
            res, res, mcol, ALU.subtract))
        P.w("scalar", "DV", dv_ln)
        P.do("scalar", lambda e: e.activation(sgt, res, AF.Square,
                                              accum_out=vcol))
        P.do("scalar", lambda e: e.drain())
        P.do("scalar", lambda e: e.activation(col(C_T2), vcol, AF.Ln,
                                              bias=col(C_EPS), scale=INV_D))
        P.do("scalar", lambda e: e.drain())
        ac_ln = P.inc("scalar", "AC", lambda e: e.activation(
            rcol, col(C_T2), AF.Exp, scale=-0.5))
        P.w("vector", "AC", ac_ln)
        if s >= 1:
            P.w("vector", "SX%d" % ((s - 1) % 2), st_x1[s - 1])
        P.do("vector", lambda e: e.drain())
        dv_x1 = P.inc("vector", "DV", lambda e: e.tensor_single_scalar(
            x1b, res, rcol, ALU.mult))
        P.w("sync", "DV", dv_x1)
        st_x1[s] = P.inc("sync", "SX%d" % bp, lambda e, s=s: e.dma_start(
            x1_scr[s, :, :], x1b), 16)
        gxc = gxs[:, s:s + 1]
        sxc = ccol(C_SXS, s)
        P.do("vector", lambda e: e.drain())
        P.do("vector", lambda e, g=gxc: e.tensor_reduce(
            g, x1b, AX.X, ALU.max, apply_absolute_value=True))
        P.do("vector", lambda e: e.drain())
        P.do("vector", lambda e, g=gxc: e.tensor_single_scalar(
            g, g, 1e-5, ALU.max))
        P.do("vector", lambda e: e.drain())
        P.do("vector", lambda e, g=gxc, s=s: e.tensor_single_scalar(
            ccol(C_GXI, s), g, 1.0 / 127.0, ALU.mult))
        P.do("vector", lambda e: e.drain())
        P.do("vector", lambda e, sc=sxc, s=s: e.reciprocal(
            sc, ccol(C_GXI, s)))
        P.do("vector", lambda e: e.drain())
        dv_prev = [0, 0]
        dv_prev[0] = P.inc("vector", "DV", lambda e, sc=sxc:
                           e.tensor_single_scalar(lY, x1b, sc, ALU.mult))
        dv_prev[1] = dv_prev[0]
        HSL = [slice(0, 1024), slice(1024, 2048)]
        for j in range(8):
            ac_j = [0, 0]
            for hh in range(2):
                src_c = (lY if j == 0 else lC)[:, HSL[hh]]
                P.w("scalar", "DV", dv_prev[hh])
                ac_j[hh] = P.inc("scalar", "AC", lambda e, src_c=src_c,
                                 hh=hh: e.activation(
                                     sgt[:, HSL[hh]], src_c, AF.Sign,
                                     bias=col(C_HALF)))
            for hh in range(2):
                src_c = (lY if j == 0 else lC)[:, HSL[hh]]
                P.w("vector", "AC", ac_j[hh])
                dv_prev[hh] = P.inc(
                    "vector", "DV", lambda e, j=j, src_c=src_c, hh=hh:
                    e.scalar_tensor_tensor(lC[:, HSL[hh]], sgt[:, HSL[hh]],
                                           col(C_K0 + j), src_c,
                                           ALU.mult, ALU.add))
        P.do("vector", lambda e: e.drain())
        P.do("vector", lambda e: e.tensor_tensor(res, lY, lC, ALU.subtract))
        P.do("vector", lambda e: e.drain())
        if s >= 1:
            P.w("vector", "PE", pe_xt_s)
        dv_xt = P.inc("vector", "DV", lambda e: e.tensor_scalar(
            xt_bf, res, 1.0, -0.5, ALU.mult, ALU.add))
        P.w("tensor", "DV", dv_xt)
        if s >= 1:
            P.w("tensor", "DV", dv_xi_done[s - 1])
        for k in range(16):
            fn = lambda e, k=k: e.transpose(
                p_tr[:, k * 128:(k + 1) * 128], xt_bf[:, k * 128:(k + 1) * 128],
                idb[:, :])
            if k == 15:
                pe_xt_s = P.inc("tensor", "PE", fn)
            else:
                P.do("tensor", fn)
        P.w("vector", "PE", pe_xt_s)
        ptr_view = p_tr[:, :].rearrange("p (k t) -> p k t", t=128)
        dv_xi_done[s] = P.inc(
            "vector", "DV",
            lambda e, s=s, o=xi_all[:, :, s * 128:(s + 1) * 128], i=ptr_view:
            e.tensor_copy(o, i))
        emit_passA(pa_sched[s])

    pe_after_p1 = P.val("PE")

    P.w("tensor", "GP", P.val("GP"))
    pe_g3 = {}
    for mi in range(3):
        pe_g3[mi] = P.inc("tensor", "PE", lambda e, mi=mi: e.matmul(
            p_b[0:1, 640 + mi:641 + mi], ccol(0, C_RA0 + mi), onesc[:, :],
            start=True, stop=True))
    P.w("vector", "PE", pe_g3[2])
    P.do("vector", lambda e: e.tensor_scalar(
        g11[:1, 0:3], p_b[0:1, 640:643], 1.0 / 16777216.0, 1e-5,
        ALU.mult, ALU.max))
    dv_g11 = P.inc("vector", "DV", lambda e: (e.drain(), e.engine_nop())[1])
    P.w("tensor", "DV", dv_g11)
    pe_gb = P.inc("tensor", "PE", lambda e: e.matmul(
        p_b[:, 644:647], ones1[:1, :], g11[:1, 0:3], start=True, stop=True))
    P.w("vector", "PE", pe_gb)
    for gi, (cth, csc, cis) in enumerate(
            [(C_SCG, C_SCG, C_ISG), (C_SCV, C_SCV, C_ISV), (C_SCO, C_SCO, -1)]):
        P.do("vector", lambda e, gi=gi, csc=csc: e.tensor_single_scalar(
            col(csc), p_b[:, 644 + gi:645 + gi], 0.5, ALU.mult))
        P.do("vector", lambda e: e.drain())
        if cis >= 0:
            P.do("vector", lambda e, csc=csc, cis=cis: e.reciprocal(
                col(cis), col(csc)))
            P.do("vector", lambda e: e.drain())
    TH = {}
    for mi, csc in ((0, C_SCG), (1, C_SCV), (2, C_SCO)):
        cp = C_SA0 + mi
        cn = C_SB0 + mi
        P.do("vector", lambda e, cp=cp, csc=csc: e.tensor_copy(
            col(cp), col(csc)))
        P.do("vector", lambda e: e.drain())
        P.do("vector", lambda e, cp=cp, cn=cn: e.tensor_single_scalar(
            col(cn), col(cp), -1.0, ALU.mult))
        P.do("vector", lambda e: e.drain())
        TH[mi] = (cp, cn)
    P.do("vector", lambda e: e.tensor_single_scalar(
        cols[:, C_SGC:C_SGC + 16], cols[:, C_GXI:C_GXI + 16], col(C_SCG),
        ALU.mult))
    P.do("vector", lambda e: e.drain())
    P.do("vector", lambda e: e.tensor_single_scalar(
        cols[:, C_SVC:C_SVC + 16], cols[:, C_GXI:C_GXI + 16], col(C_SCV),
        ALU.mult))
    P.do("vector", lambda e: e.drain())
    rows_g = lambda s: af(O_RES + s * 128, 128, 0, 1)
    rows_v = lambda s: af(O_X1 + s * 128, 128, 0, 1)
    P.w("vector", "SX0", st_x1[NS - 2])
    P.w("vector", "SX1", st_x1[NS - 1])
    for mi in range(2):
        base = C_SGC if mi == 0 else C_SVC
        rows = rows_g if mi == 0 else rows_v
        for s in range(NS):
            P.do("vector", lambda e, base=base, s=s: e.reciprocal(
                col(C_T0), ccol(base, s)))
            P.do("vector", lambda e: e.drain())
            dv_iv = P.inc("vector", "DV", lambda e: (e.drain(),
                                                     e.engine_nop())[1])
            P.w("tensor", "DV", dv_iv)
            pe_tv = P.inc("tensor", "PE", lambda e: e.transpose(
                p_b[0:1, 648:776], col(C_T0), idf[:, :]))
            P.w("vector", "PE", pe_tv)
            P.do("vector", lambda e: e.drain())
            P.inc("vector", "DV", lambda e, ro=rows(s): e.tensor_copy(
                ro, p_b[0:1, 648:776]))
    dv_rows = P.inc("vector", "DV", lambda e: (e.drain(), e.engine_nop())[1])

    wlb = [af(O_WL, DM), af(O_WL + 2048, DM)]
    pqb = af(O_PQ, DM)
    gs_slab = af(O_GS, 4096)
    sgh = [ab(B_SG, 1024), ab(B_SG + 1024, 1024)]
    oqb = ab(B_OQ, DM)
    gwq2 = [B[:, B_GWA:B_GWA + 8192], B[:, B_GWB:B_GWB + 8192]]
    hbuf2 = [af(O_UB, 512), af(O_UB + 512, 512)]
    brb = [af(O_BR, 512, 0, 1), af(O_BR + 512, 512, 0, 1)]
    brbD = [af(O_DP1, 512, 0, 1), af(O_DP1 + 512, 512, 0, 1)]
    P.do("vector", lambda e: e.memset(cols[:, C_MAC:C_MAC + 16], 0.0))
    dv_mz = P.inc("vector", "DV", lambda e: (e.drain(), e.engine_nop())[1])

    pq_ac = {}
    pq_st = {}
    chain = {"dv": 0}

    def emit_prequant(jcs):
        for jc in jcs:
            if jc >= NJC:
                continue
            if jc >= 1:
                P.w("sync", "AC", pq_ac[jc - 1])
            lq = P.inc("sync", "LQ%d" % (jc % 2), lambda e, jc=jc:
                       e.dma_start(pqb, flo[jc, :, :]), 16)
            P.w("scalar", "LQ%d" % (jc % 2), lq)
            if chain["dv"]:
                P.w("scalar", "DV", chain["dv"])
            P.do("scalar", lambda e: e.activation(
                ab(B_SG, DM), pqb, AF.Sign, bias=col(C_SB0 + 2)))
            if jc >= 1:
                P.w("scalar", "SQ%d" % ((jc - 1) % 2), pq_st[jc - 1])
            acq = P.inc("scalar", "AC", lambda e: e.activation(
                oqb, pqb, AF.Sign, bias=col(C_SA0 + 2)))
            P.w("vector", "AC", acq)
            dvq = P.inc("vector", "DV", lambda e: e.tensor_tensor(
                oqb, ab(B_SG, DM), oqb, ALU.add))
            chain["dv"] = dvq
            pq_ac[jc] = acq
            P.w("sync", "DV", dvq)
            pq_st[jc] = P.inc("sync", "SQ%d" % (jc % 2),
                              lambda e, jc=jc: e.dma_start(oq_scr[jc, :, :],
                                                           oqb), 16)

    P.w("scalar", "DV", dv_rows)
    P.w("sync", "DV", dv_rows)
    P.w("sync", "PE", pe_after_p1)
    gwt_r = gwt_h.rearrange("(k p) x -> p k x", p=128)
    vwt_r = vwt_h.rearrange("(k p) x -> p k x", p=128)
    st_h = {}
    lw_ac = {}
    pe_slab_free = {}
    pe_burst = {}
    dv_quant = {}
    epi_done = {}
    lr_hist = {}
    pq_next = 0
    jq_box = [0]
    tc = 0
    sg0q = ab(B_SG, DM)
    sg1q = ab(B_OQ, DM)

    def quantize(hg, mat, gate_pe):
        wr = gwt_r if mat == 0 else vwt_r
        cp, cn = TH[mat]
        wsl = gwq2[mat]
        first = True
        for c in range(4):
            jq = jq_box[0]
            bp = jq % 2
            if jq >= 2:
                P.w("sync", "AC", lw_ac[jq - 2])
            lw = P.inc("sync", "LW%d" % bp, lambda e, wr=wr, c=c, hg=hg,
                       bp=bp: e.dma_start(
                           wlb[bp].rearrange("p (k x) -> p k x", x=512),
                           wr[:, c * 4:c * 4 + 4,
                              hg * 512:(hg + 1) * 512]), 16)
            P.w("scalar", "LW%d" % bp, lw)
            if chain["dv"]:
                P.w("scalar", "DV", chain["dv"])
            if first:
                P.w("scalar", "SQ0", P.val("SQ0"))
                P.w("scalar", "SQ1", P.val("SQ1"))
            P.do("scalar", lambda e, bp=bp, cn=cn: e.activation(
                sg0q, wlb[bp], AF.Sign, bias=col(cn)))
            acs = P.inc("scalar", "AC", lambda e, bp=bp, cp=cp: e.activation(
                sg1q, wlb[bp], AF.Sign, bias=col(cp)))
            P.w("vector", "AC", acs)
            if first and gate_pe is not None:
                P.w("vector", "PE", gate_pe)
            dvs2 = P.inc("vector", "DV", lambda e, c=c, wsl=wsl:
                         e.tensor_tensor(wsl[:, c * 2048:(c + 1) * 2048],
                                         sg0q, sg1q, ALU.add))
            chain["dv"] = dvs2
            lw_ac[jq] = acs
            jq_box[0] = jq + 1
            first = False
        dv_quant[(hg, mat)] = chain["dv"]

    P.w("scalar", "PE", pe_after_p1)
    quantize(0, 0, None)
    quantize(0, 1, None)
    for hg in range(NHG):
        if hg == 0:
            for h2 in (0, 1):
                bpair = brb if h2 % 2 == 0 else brbD
                if h2 == 1:
                    P.w("sync", "DV", dv_xi_done[NS - 1])
                lrg = P.inc("sync", "LR%d" % (2 * (h2 % 2)),
                            lambda e, h2=h2, b=bpair[0]:
                            e.dma_start(b, gbr_h[:, h2 * 512:
                                                  (h2 + 1) * 512]), 16)
                lrv = P.inc("sync", "LR%d" % (1 + 2 * (h2 % 2)),
                            lambda e, h2=h2, b=bpair[1]:
                            e.dma_start(b, vbr_h[:, h2 * 512:
                                                  (h2 + 1) * 512]), 16)
                lr_hist[h2] = (lrg, lrv)
        for half in range(2):
            for mat in range(2):
                wsl = gwq2[mat]
                P.w("tensor", "DV", dv_quant[(hg, mat)])
                if hg == 0 and half == 0 and mat == 0:
                    P.w("tensor", "DV", dv_xi_done[NS - 1])
                    P.w("tensor", "DV", dv_mz)
                P.w("tensor", "LR%d" % (mat + 2 * (hg % 2)),
                    lr_hist[hg][mat])
                for i8 in range(8):
                    t = half * 8 + i8
                    bank = tc % 6
                    psl = (p_a[:, bank * 512:(bank + 1) * 512] if bank < 4
                           else p_b[:, (bank - 4) * 512:(bank - 3) * 512])
                    if tc >= 6:
                        kind, val = epi_done[tc - 6]
                        P.w("tensor", kind, val)
                    ro = (rows_g if mat == 0 else rows_v)(t)
                    bpr = (brb if hg % 2 == 0 else brbD)[mat]
                    P.do("tensor", lambda e, psl=psl, ro=ro, b=bpr:
                         e.matmul(psl, ro, b, start=True, stop=False))
                    for k in range(NK):
                        fn = lambda e, k=k, t=t, psl=psl, wsl=wsl: e.matmul(
                            psl,
                            B[:, B_XI + k * 2048 + t * 128:
                               B_XI + k * 2048 + t * 128 + 128],
                            wsl[:, k * 512:(k + 1) * 512],
                            start=False, stop=(k == 15))
                        if k == 15:
                            pe_mm = P.inc("tensor", "PE", fn)
                        else:
                            P.do("tensor", fn)
                    slot = t % 8
                    if mat == 0:
                        P.w("scalar", "PE", pe_mm)
                        if hg > 0 or half > 0:
                            back = 8 if half == 1 else 16
                            P.w("scalar", "DV", epi_done[tc - back][1])
                        ac_e = P.inc(
                            "scalar", "AC", lambda e, slot=slot, psl=psl,
                            t=t: e.activation(
                                gs_slab[:, slot * 512:(slot + 1) * 512],
                                psl, AF.Silu, scale=ccol(C_SGC, t)))
                        epi_done[tc] = ("AC", ac_e)
                    else:
                        P.w("vector", "PE", pe_mm)
                        hp = t % 2
                        nvt = hg * 16 + t
                        if nvt >= 2:
                            P.w("vector", "SH%d" % hp,
                                st_h[((nvt - 2) // 16, (nvt - 2) % 16)])
                        P.w("vector", "AC", epi_done[tc - 8][1])
                        P.do("vector", lambda e: e.drain())
                        P.do("vector", lambda e, slot=slot, psl=psl, hp=hp,
                             t=t: e.scalar_tensor_tensor(
                                 hbuf2[hp], psl, ccol(C_SVC, t),
                                 gs_slab[:, slot * 512:(slot + 1) * 512],
                                 ALU.mult, ALU.mult))
                        P.do("vector", lambda e: e.drain())
                        P.do("vector", lambda e, hp=hp: e.tensor_reduce(
                            col(C_T0), hbuf2[hp], AX.X, ALU.max,
                            apply_absolute_value=True))
                        P.do("vector", lambda e: e.drain())
                        dv_h = P.inc("vector", "DV", lambda e, t=t:
                                     e.tensor_tensor(ccol(C_MAC, t),
                                                     ccol(C_MAC, t),
                                                     col(C_T0), ALU.max))
                        epi_done[tc] = ("DV", dv_h)
                        P.w("sync", "DV", dv_h)
                        st_h[(hg, t)] = P.inc(
                            "sync", "SH%d" % hp,
                            lambda e, t=t, hg=hg, hp=hp: e.dma_start(
                                h_scr[t, :, hg * 512:(hg + 1) * 512],
                                hbuf2[hp]), 16)
                    tc += 1
                pe_burst[(hg, half, mat)] = P.val("PE")
                if half == 1 and mat == 0 and hg + 1 < NHG:
                    quantize(hg + 1, 0, pe_burst[(hg, 1, 0)])
        if hg + 1 < NHG:
            quantize(hg + 1, 1, pe_burst[(hg, 1, 1)])
        if hg + 2 < NHG:
            h2 = hg + 2
            bpair = brb if h2 % 2 == 0 else brbD
            P.w("sync", "PE", pe_burst[(hg, 1, 1)])
            lrg = P.inc("sync", "LR%d" % (2 * (h2 % 2)),
                        lambda e, h2=h2, b=bpair[0]:
                        e.dma_start(b, gbr_h[:, h2 * 512:(h2 + 1) * 512]), 16)
            lrv = P.inc("sync", "LR%d" % (1 + 2 * (h2 % 2)),
                        lambda e, h2=h2, b=bpair[1]:
                        e.dma_start(b, vbr_h[:, h2 * 512:(h2 + 1) * 512]), 16)
            lr_hist[h2] = (lrg, lrv)
        emit_prequant(list(range(pq_next, pq_next + 4)))
        pq_next += 4

    P.w("vector", "AC", pq_ac[NJC - 1])
    P.do("vector", lambda e: e.drain())
    P.do("vector", lambda e: e.tensor_scalar(
        cols[:, C_MAC:C_MAC + 16], cols[:, C_MAC:C_MAC + 16], 1.0, 1e-5,
        ALU.mult, ALU.max))
    P.do("vector", lambda e: e.drain())
    P.do("vector", lambda e: e.tensor_scalar(
        cols[:, C_S2:C_S2 + 16], cols[:, C_MAC:C_MAC + 16], 1.0 / 127.0, 0.0,
        ALU.mult, ALU.add))
    P.do("vector", lambda e: e.drain())
    P.do("vector", lambda e: e.reciprocal(
        cols[:, C_MAC:C_MAC + 16], cols[:, C_S2:C_S2 + 16]))
    P.do("vector", lambda e: e.drain())
    P.do("vector", lambda e: e.tensor_single_scalar(
        cols[:, C_S2:C_S2 + 16], cols[:, C_S2:C_S2 + 16], col(C_SCO),
        ALU.mult))
    P.do("vector", lambda e: e.drain())
    rows_o = lambda s: af(O_PQ + s * 128, 128, 0, 1)
    for s in range(NS):
        P.do("vector", lambda e, s=s: e.reciprocal(col(C_T0), ccol(C_S2, s)))
        P.do("vector", lambda e: e.drain())
        dv_iv = P.inc("vector", "DV", lambda e: (e.drain(),
                                                 e.engine_nop())[1])
        P.w("tensor", "DV", dv_iv)
        if s == 0:
            P.w("tensor", "PE", pe_burst[(NHG - 1, 1, 1)])
        pe_tv = P.inc("tensor", "PE", lambda e: e.transpose(
            p_b[0:1, 648:776], col(C_T0), idf[:, :]))
        P.w("vector", "PE", pe_tv)
        P.do("vector", lambda e: e.drain())
        P.inc("vector", "DV", lambda e, ro=rows_o(s): e.tensor_copy(
            ro, p_b[0:1, 648:776]))
    dv_orows = P.inc("vector", "DV", lambda e: (e.drain(), e.engine_nop())[1])

    NE = 8
    hbq = [af(O_UB, DM), af(O_UT, DM)]
    lCc = [af(O_LC, DM), af(O_WL, DM)]
    lYc = [af(O_LY, DM), af(O_WL + 2048, DM)]
    sgc = [ab(B_SG, DM), ab(B_OQ, DM)]
    xtmp = af(O_RES, DM)
    x1r2 = [af(O_GS, DM), af(O_GS + 2048, DM)]
    z2 = [af(O_DP1, DM), af(O_X1, DM)]
    bro = af(O_BR, 1024, 0, 1)
    owqb = [B[:, B_GWA + i * 4096:B_GWA + (i + 1) * 4096] for i in range(2)]
    oq_r = oq_scr.rearrange("j p d -> p j d")
    hitH = [B[:, B_XI + i * 16384:B_XI + (i + 1) * 16384] for i in range(2)]
    hitH_v = [hitH[i].rearrange("p (jc t) -> p jc t", t=256) for i in range(2)]
    sq2 = ab(B_SG, DM)

    dv_lad = {}
    dv_evac = {}
    dv_evac_all = {}
    dv_z_round = {}
    st_out = {}
    dv_ln2 = {}
    lh_vals = {}
    lb_vals = {}
    lr_vals = {}
    lx_vals = {}
    tr_emitters = {}

    def emit_lh(e, p):
        pc = e * 8 + p
        c, ch = p // 2, p % 2
        st = 2 * e + ch
        bp = pc % 2
        if pc >= 2:
            P.w("sync", "PL", pc - 1)
        if pc == 0:
            P.w("sync", "SH0", P.val("SH0"))
            P.w("sync", "SH1", P.val("SH1"))
        lh_vals[pc] = P.inc("sync", "LH%d" % bp, lambda e_, st=st, c=c,
                            bp=bp: e_.dma_start(
                                hbq[bp],
                                h_scr[st, :, c * 2048:(c + 1) * 2048]), 16)

    def emit_p3_sync(e):
        em = e - 1
        if em >= 0:
            nd0 = em * 2
            if nd0 >= 1:
                P.w("sync", "PG", (nd0 - 1) * 16 + 1)
            lr_vals[nd0] = P.inc("sync", "LR%d" % (nd0 % 2), lambda e_:
                                 e_.dma_start(bro, obr_h[:, 0:1024]), 16)
            for t2 in range(2):
                st = 2 * em + t2
                if em > 0:
                    P.w("sync", "DV", dv_ln2[(em - 1, t2)])
                elif t2 == 0:
                    P.w("sync", "SX0", P.val("SX0"))
                    P.w("sync", "SX1", P.val("SX1"))
                lx_vals[(em, t2)] = P.inc(
                    "sync", "LX%d" % t2, lambda e_, st=st, t2=t2:
                    e_.dma_start(x1r2[t2], x1_scr[st, :, :]), 16)
            for g in range(16):
                nload = em * 32 + g
                if nload >= 2:
                    P.w("sync", "PG", nload - 1)
                if em == 0 and g == 0:
                    P.w("sync", "SQ0", P.val("SQ0"))
                    P.w("sync", "SQ1", P.val("SQ1"))
                lb_vals[(em, 0, g)] = P.inc(
                    "sync", "LB%d" % (g % 2), lambda e_, g=g: e_.dma_start(
                        owqb[g % 2].rearrange("p (j d) -> p j d", d=1024),
                        oq_r[:, g * 4:g * 4 + 4, 0:1024]), 16)
                if e < NE and 1 <= g <= 8:
                    emit_lh(e, g - 1)
            nd1 = em * 2 + 1
            P.w("sync", "PG", (nd1 - 1) * 16 + 1)
            lr_vals[nd1] = P.inc("sync", "LR%d" % (nd1 % 2), lambda e_:
                                 e_.dma_start(bro, obr_h[:, 1024:2048]), 16)
            for g in range(16):
                nload = em * 32 + 16 + g
                P.w("sync", "PG", nload - 1)
                lb_vals[(em, 1, g)] = P.inc(
                    "sync", "LB%d" % (g % 2), lambda e_, g=g: e_.dma_start(
                        owqb[g % 2].rearrange("p (j d) -> p j d", d=1024),
                        oq_r[:, g * 4:g * 4 + 4, 1024:2048]), 16)
        else:
            for p in range(8):
                emit_lh(e, p)

    def emit_ladder(e):
        hhE = e % 2
        emitters = []
        for c in range(4):
            pcs = [e * 8 + c * 2, e * 8 + c * 2 + 1]
            for ch in range(2):
                pc = pcs[ch]
                st = 2 * e + ch
                bp = pc % 2
                P.w("vector", "LH%d" % bp, lh_vals[pc])
                if pc == 0:
                    P.w("vector", "DV", dv_orows)
                    P.w("vector", "SQ0", P.val("SQ0"))
                    P.w("vector", "SQ1", P.val("SQ1"))
                r2c = ccol(C_MAC, st)
                P.do("vector", lambda e_: e_.drain())
                P.inc("vector", "PL", lambda e_, bp=bp, r2c=r2c, lY=lYc[ch]:
                      e_.tensor_single_scalar(lY, hbq[bp], r2c, ALU.mult))
            dv_prev = [[0, 0], [0, 0]]
            HSL = [slice(0, 1024), slice(1024, 2048)]
            for j in range(8):
                ac_j = [[0, 0], [0, 0]]
                for ch in range(2):
                    pc = pcs[ch]
                    for hh in range(2):
                        src_c = (lYc[ch] if j == 0 else lCc[ch])[:, HSL[hh]]
                        if j == 0 and hh == 0:
                            P.w("scalar", "PL", pc + 1)
                            if pc >= 2:
                                P.w("scalar", "PT", pc - 1)
                        elif j > 0:
                            P.w("scalar", "DV", dv_prev[ch][hh])
                        ac_j[ch][hh] = P.inc(
                            "scalar", "AC",
                            lambda e_, src_c=src_c, sg=sgc[ch], hh=hh:
                            e_.activation(sg[:, HSL[hh]], src_c, AF.Sign,
                                          bias=col(C_HALF)))
                for ch in range(2):
                    for hh in range(2):
                        src_c = (lYc[ch] if j == 0 else lCc[ch])[:, HSL[hh]]
                        P.w("vector", "AC", ac_j[ch][hh])
                        dv_prev[ch][hh] = P.inc(
                            "vector", "DV",
                            lambda e_, j=j, src_c=src_c, sg=sgc[ch],
                            lC=lCc[ch], hh=hh:
                            e_.scalar_tensor_tensor(
                                lC[:, HSL[hh]], sg[:, HSL[hh]],
                                col(C_K0 + j), src_c, ALU.mult, ALU.add))
            for ch in range(2):
                pc = pcs[ch]
                c_, ch_ = c, ch
                P.do("vector", lambda e_: e_.drain())
                P.do("vector", lambda e_, lY=lYc[ch], lC=lCc[ch]:
                     e_.tensor_tensor(xtmp, lY, lC, ALU.subtract))
                P.do("vector", lambda e_: e_.drain())
                dv_lad[pc] = P.inc("vector", "DV", lambda e_, sg=sgc[ch]:
                                   e_.tensor_scalar(sg, xtmp, 1.0, -0.5,
                                                    ALU.mult, ALU.add))

                def tr_emit(pc=pc, sg=sgc[ch]):
                    P.w("tensor", "DV", dv_lad[pc])
                    if pc >= 1:
                        P.w("tensor", "DV", dv_evac[pc - 1])
                    for kk in range(16):
                        fn = lambda e_, kk=kk, sg=sg: e_.transpose(
                            p_tr[:, kk * 128:(kk + 1) * 128],
                            sg[:, kk * 128:(kk + 1) * 128], idb[:, :])
                        if kk == 15:
                            P.inc("tensor", "PT", fn)
                        else:
                            P.do("tensor", fn)
                emitters.append(tr_emit)
                P.w("vector", "PT", pc + 1)
                if e >= 2 and pc == e * 8:
                    P.w("vector", "PG", (e - 1) * 32)
                ptr_view = p_tr[:, :].rearrange("p (k t) -> p k t", t=128)
                dv_evac[pc] = P.inc(
                    "vector", "DV",
                    lambda e_, c=c_, ch=ch_, hh=hhE, i=ptr_view:
                    e_.tensor_copy(hitH_v[hh][:, c * 16:(c + 1) * 16,
                                              ch * 128:(ch + 1) * 128], i))
        dv_evac_all[e] = P.val("DV")
        tr_emitters[e] = emitters

    def emit_mm(em, emitters):
        hh = em % 2
        spl = 0
        for r in range(2):
            nd = em * 2 + r
            for grp in range(16):
                gbp = grp % 2
                if r == 0 and grp % 2 == 1 and spl < len(emitters):
                    emitters[spl]()
                    spl += 1
                P.w("tensor", "LB%d" % gbp, lb_vals[(em, r, grp)])
                if grp == 0:
                    P.w("tensor", "LR%d" % (nd % 2), lr_vals[nd])
                    if r == 0:
                        P.w("tensor", "DV", dv_evac_all[em])
                    if r >= 1:
                        P.w("tensor", "DV", dv_z_round[(em, 0)])
                    elif em > 0:
                        P.w("tensor", "DV", dv_z_round[(em - 1, 1)])
                    for t2 in range(2):
                        st = 2 * em + t2
                        for di in range(2):
                            bank = t2 * 2 + di
                            P.do("tensor", lambda e_, bank=bank, st=st,
                                 di=di: e_.matmul(
                                     p_a[:, bank * 512:(bank + 1) * 512],
                                     rows_o(st),
                                     bro[:, di * 512:(di + 1) * 512],
                                     start=True, stop=False))
                for jj in range(4):
                    jc = grp * 4 + jj
                    for t2 in range(2):
                        for di in range(2):
                            bank = t2 * 2 + di
                            fn = lambda e_, jc=jc, t2=t2, di=di, bank=bank, \
                                gbp=gbp, jj=jj, hh=hh: e_.matmul(
                                    p_a[:, bank * 512:(bank + 1) * 512],
                                    hitH[hh][:, jc * 256 + t2 * 128:
                                             jc * 256 + t2 * 128 + 128],
                                    owqb[gbp][:, jj * 1024 + di * 512:
                                              jj * 1024 + di * 512 + 512],
                                    start=False, stop=(jc == NJC - 1))
                            if jj == 3 and t2 == 1 and di == 1:
                                P.inc("tensor", "PG", fn)
                            else:
                                P.do("tensor", fn)
            for t2 in range(2):
                st = 2 * em + t2
                for di in range(2):
                    dmg = r * 2 + di
                    bank = t2 * 2 + di
                    P.w("vector", "PG", nd * 16 + 16)
                    if r == 0 and di == 0:
                        P.w("vector", "LX%d" % t2, lx_vals[(em, t2)])
                        if em > 0:
                            P.w("vector", "SO%d" % ((2 * (em - 1) + t2) % 2),
                                st_out[2 * (em - 1) + t2])
                    s2c = ccol(C_S2, st)
                    P.do("vector", lambda e_: e_.drain())
                    dv_z = P.inc("vector", "DV",
                                 lambda e_, t2=t2, dmg=dmg, s2c=s2c,
                                 bank=bank:
                                 e_.scalar_tensor_tensor(
                                     z2[t2][:, dmg * 512:(dmg + 1) * 512],
                                     p_a[:, bank * 512:(bank + 1) * 512],
                                     s2c,
                                     x1r2[t2][:, dmg * 512:(dmg + 1) * 512],
                                     ALU.mult, ALU.add))
            dv_z_round[(em, r)] = dv_z
        for t2 in range(2):
            st = 2 * em + t2
            P.w("vector", "DV", dv_z_round[(em, 1)])
            zt = z2[t2]
            mcol, vcol, rcol = col(C_T0), col(C_T1), col(C_T3)
            P.do("vector", lambda e_: e_.drain())
            P.do("vector", lambda e_, zt=zt: e_.tensor_reduce(mcol, zt, AX.X,
                                                              ALU.add))
            P.do("vector", lambda e_: e_.drain())
            P.do("vector", lambda e_: e_.tensor_single_scalar(mcol, mcol,
                                                              INV_D,
                                                              ALU.mult))
            P.do("vector", lambda e_: e_.drain())
            dv_l2 = P.inc("vector", "DV", lambda e_, zt=zt:
                          e_.tensor_single_scalar(zt, zt, mcol,
                                                  ALU.subtract))
            P.w("scalar", "DV", dv_l2)
            P.do("scalar", lambda e_, zt=zt: e_.activation(sq2, zt, AF.Square,
                                                           accum_out=vcol))
            P.do("scalar", lambda e_: e_.drain())
            P.do("scalar", lambda e_: e_.activation(col(C_T2), vcol, AF.Ln,
                                                    bias=col(C_EPS),
                                                    scale=INV_D))
            P.do("scalar", lambda e_: e_.drain())
            ac_l2 = P.inc("scalar", "AC", lambda e_: e_.activation(
                rcol, col(C_T2), AF.Exp, scale=-0.5))
            P.w("vector", "AC", ac_l2)
            P.do("vector", lambda e_: e_.drain())
            dv_o = P.inc("vector", "DV", lambda e_, zt=zt:
                         e_.tensor_single_scalar(zt, zt, rcol, ALU.mult))
            dv_ln2[(em, t2)] = dv_o
            P.w("sync", "DV", dv_o)
            st_out[st] = P.inc("sync", "SO%d" % (st % 2),
                               lambda e_, st=st, zt=zt: e_.dma_start(
                                   out_h[st * 128:(st + 1) * 128, :], zt), 16)

    P.w("sync", "PE", pe_burst[(NHG - 1, 1, 1)])
    for e in range(NE + 1):
        emit_p3_sync(e)
        if e < NE:
            emit_ladder(e)
        if e == 0:
            for em_ in tr_emitters[0]:
                em_()
            tr_emitters[0] = []
        if e >= 1:
            emit_mm(e - 1, tr_emitters[e] if e < NE else [])

    P.w("sync", "SO0", P.val("SO0"))
    P.w("sync", "SO1", P.val("SO1"))

    with nc.Block() as block:
        @block.sync
        def _(eng):
            for f in P.ops["sync"]:
                f(eng)

        @block.tensor
        def _(eng):
            for f in P.ops["tensor"]:
                f(eng)

        @block.vector
        def _(eng):
            for f in P.ops["vector"]:
                f(eng)

        @block.scalar
        def _(eng):
            for f in P.ops["scalar"]:
                f(eng)

        @block.gpsimd
        def _(eng):
            for f in P.ops["gpsimd"]:
                f(eng)

    ctx.close()
    return nc


_NC_CACHE = None


def kernel(L0_attn_u, L0_attn_A_factors, L0_attn_W_b_rank, L0_attn_W_b_state,
           L0_attn_W_c_rank, L0_attn_W_c_model, L0_attn_D_vec,
           L0_ffn_gate_w, L0_ffn_gate_b, L0_ffn_val_w, L0_ffn_val_b,
           L0_ffn_out_w, L0_ffn_out_b):
    global _NC_CACHE
    u = np.ascontiguousarray(np.asarray(L0_attn_u, dtype=np.float32).reshape(-1, DM))
    gw = np.asarray(L0_ffn_gate_w, dtype=np.float32)
    vw = np.asarray(L0_ffn_val_w, dtype=np.float32)
    ow = np.asarray(L0_ffn_out_w, dtype=np.float32)
    shared = dict(
        gwt=np.ascontiguousarray(gw.T),
        vwt=np.ascontiguousarray(vw.T),
        owt=np.ascontiguousarray(ow.T),
        gbr=np.ascontiguousarray(np.asarray(L0_ffn_gate_b,
                                            dtype=np.float32).reshape(1, HID)),
        vbr=np.ascontiguousarray(np.asarray(L0_ffn_val_b,
                                            dtype=np.float32).reshape(1, HID)),
        obr=np.ascontiguousarray(np.asarray(L0_ffn_out_b,
                                            dtype=np.float32).reshape(1, DM)),
        wbr=np.ascontiguousarray(np.asarray(L0_attn_W_b_rank, dtype=np.float32)),
        wbs=np.ascontiguousarray(np.asarray(L0_attn_W_b_state, dtype=np.float32)),
        wcr=np.ascontiguousarray(np.asarray(L0_attn_W_c_rank, dtype=np.float32)),
        wcm=np.ascontiguousarray(np.asarray(L0_attn_W_c_model, dtype=np.float32)),
        afac=np.ascontiguousarray(
            np.asarray(L0_attn_A_factors, dtype=np.float32).reshape(1, 32)),
        dvec=np.ascontiguousarray(
            np.asarray(L0_attn_D_vec, dtype=np.float32).reshape(1, DM)),
        idf=np.eye(128, dtype=np.float32),
        idb=np.eye(128, dtype=np.float32).astype(BF),
    )
    if _NC_CACHE is None:
        _NC_CACHE = build_program()
    in_maps = []
    for c in range(NCORE):
        uc = np.ascontiguousarray(u[c * TOK:(c + 1) * TOK, :])
        in_maps.append(dict(u=uc, ut=np.ascontiguousarray(uc.T), **shared))
    res = run_bass_kernel_spmd(_NC_CACHE, in_maps, core_ids=list(range(NCORE)))
    out = np.concatenate(
        [np.asarray(r["out"], dtype=np.float32) for r in res.results], axis=0)
    return out.reshape(4, 4096, DM)


# version 13 (final)
# speedup vs baseline: 1.5700x; 1.0142x over previous
import contextlib
import numpy as np
import ml_dtypes
import concourse.bass as bass
import concourse.mybir as mybir
from concourse.bass_utils import run_bass_kernel_spmd

F32 = mybir.dt.float32
BF16 = mybir.dt.bfloat16
BF = ml_dtypes.bfloat16
AF = mybir.ActivationFunctionType
ALU = mybir.AluOpType
AX = mybir.AxisListType

NCORE = 8
TOK = 2048
DM = 2048
HID = 8192
NS = 16
NHG = 16
NK = 16
NJC = 64
NQ = 4
INV_D = 1.0 / DM
EPS = 1e-5
LADK = (128.0, 64.0, 32.0, 16.0, 8.0, 4.0, 2.0, 1.0)

O_COLS = 0
O_DP1 = 128
O_UB = 2176
O_UT = 4224
O_RES = 6272
O_X1 = 8320
O_LC = 10368
O_LY = 12416
O_GS = 14464
O_WL = 18560
O_PQ = 22656
O_BR = 24704
A_W = 25728
B_XI = 0
B_GWA = 32768
B_GWB = 40960
B_SG = 49152
B_OQ = 51200
B_W = 53248

(C_EPS, C_HALF, C_T0, C_T1, C_T2, C_T3, C_SCG, C_SCV, C_SCO,
 C_ISG, C_ISV, C_RA0, C_RA1, C_RA2, C_SA0, C_SA1, C_SA2, C_SB0, C_SB1,
 C_SB2, C_K0) = range(21)
C_MAC = 32
C_SXS = 48
C_SGC = 64
C_SVC = 80
C_S2 = 96
C_GXI = 112
NCOL = 128


class Prog:
    ENGS = ("sync", "tensor", "vector", "scalar", "gpsimd")

    def __init__(self):
        self.ops = {e: [] for e in self.ENGS}
        self.cnt = {}
        self.sems = {}

    def w(self, eng, sname, val):
        if val <= 0:
            return
        self.ops[eng].append(lambda e, s=self.sems[sname], v=val: e.wait_ge(s, v))

    def do(self, eng, fn):
        self.ops[eng].append(fn)

    def inc(self, eng, sname, fn, amt=1):
        self.cnt[sname] = self.cnt.get(sname, 0) + amt
        self.ops[eng].append(
            lambda e, f=fn, s=self.sems[sname], a=amt: f(e).then_inc(s, a))
        return self.cnt[sname]

    def val(self, sname):
        return self.cnt.get(sname, 0)


def build_program(debug=False):
    nc = bass.Bass("TRN2", target_bir_lowering=False)

    u_h = nc.dram_tensor("u", [TOK, DM], F32, kind="ExternalInput").ap()
    ut_h = nc.dram_tensor("ut", [DM, TOK], F32, kind="ExternalInput").ap()
    gwt_h = nc.dram_tensor("gwt", [DM, HID], F32, kind="ExternalInput").ap()
    vwt_h = nc.dram_tensor("vwt", [DM, HID], F32, kind="ExternalInput").ap()
    owt_h = nc.dram_tensor("owt", [HID, DM], F32, kind="ExternalInput").ap()
    gbr_h = nc.dram_tensor("gbr", [1, HID], F32, kind="ExternalInput").ap()
    vbr_h = nc.dram_tensor("vbr", [1, HID], F32, kind="ExternalInput").ap()
    obr_h = nc.dram_tensor("obr", [1, DM], F32, kind="ExternalInput").ap()
    wbr_h = nc.dram_tensor("wbr", [DM, 64], F32, kind="ExternalInput").ap()
    wbs_h = nc.dram_tensor("wbs", [64, 256], F32, kind="ExternalInput").ap()
    wcr_h = nc.dram_tensor("wcr", [256, 64], F32, kind="ExternalInput").ap()
    wcm_h = nc.dram_tensor("wcm", [64, DM], F32, kind="ExternalInput").ap()
    a_h = nc.dram_tensor("afac", [1, 32], F32, kind="ExternalInput").ap()
    d_h = nc.dram_tensor("dvec", [1, DM], F32, kind="ExternalInput").ap()
    idf_h = nc.dram_tensor("idf", [128, 128], F32, kind="ExternalInput").ap()
    idb_h = nc.dram_tensor("idb", [128, 128], BF16, kind="ExternalInput").ap()
    out_h = nc.dram_tensor("out", [TOK, DM], F32, kind="ExternalOutput").ap()
    scr_kind = "ExternalOutput" if debug else "Internal"
    x1_scr = nc.dram_tensor("x1_scr", [NS, 128, DM], F32, kind=scr_kind).ap()
    h_scr = nc.dram_tensor("h_scr", [NS, 128, HID], F32, kind=scr_kind).ap()
    oq_scr = nc.dram_tensor("oq_scr", [NJC, 128, DM], BF16, kind=scr_kind).ap()

    ctx = contextlib.ExitStack()
    A = ctx.enter_context(nc.sbuf_tensor("arena_f32", [128, A_W], F32))
    B = ctx.enter_context(nc.sbuf_tensor("arena_bf16", [128, B_W], BF16))

    def af(off, w, p0=0, p1=128):
        return A[p0:p1, off:off + w]

    def ab(off, w, p0=0, p1=128):
        return B[p0:p1, off:off + w]

    idf = ctx.enter_context(nc.sbuf_tensor("idf_s", [128, 128], F32))
    idb = ctx.enter_context(nc.sbuf_tensor("idb_s", [128, 128], BF16))
    ones1 = ctx.enter_context(nc.sbuf_tensor("ones1", [1, 128], F32))
    onesc = ctx.enter_context(nc.sbuf_tensor("onesc", [128, 1], F32))
    gxs = ctx.enter_context(nc.sbuf_tensor("gxs", [128, NS], F32))
    g11 = ctx.enter_context(nc.sbuf_tensor("g11", [1, 8], F32))

    p_a = ctx.enter_context(nc.psum_tensor("p_a", [128, 2048], F32))
    p_b = ctx.enter_context(nc.psum_tensor("p_b", [128, 1024], F32))
    p_tr = ctx.enter_context(nc.psum_tensor("p_tr", [128, 2048], BF16))

    P = Prog()
    semnames = ["PE", "DV", "AC", "GP", "L0",
                "LU0", "LU1", "LT0", "LT1", "LA0", "LA1", "LW0", "LW1",
                "LQ0", "LQ1", "LH0", "LH1", "LB0", "LB1", "LX0", "LX1",
                "LR0", "LR1", "LR2", "LR3", "PT", "PG", "PL",
                "SX0", "SX1", "SH0", "SH1", "SQ0", "SQ1", "SO0", "SO1"]
    for nm in semnames:
        P.sems[nm] = ctx.enter_context(nc.semaphore("s" + nm))

    cols = af(O_COLS, NCOL)

    def col(i):
        return A[:, O_COLS + i:O_COLS + i + 1]

    def ccol(base, i):
        return A[:, O_COLS + base + i:O_COLS + base + i + 1]

    M_ = O_WL
    wbrT = af(M_, 1024)
    wcm_f = af(M_ + 1024, DM, 0, 64)
    G_f = af(M_ + 3072, 64, 0, 64)
    t1b = af(M_ + 3136, 128, 0, 64)
    t2b = af(M_ + 3264, 128, 0, 64)
    wbs_t = af(O_PQ, 256, 0, 64)
    bb0 = af(O_PQ + 256, 256, 0, 64)
    bb1 = af(O_PQ + 512, 256, 0, 64)
    tmp256 = af(O_PQ + 768, 256, 0, 64)
    a_bc = af(O_PQ + 1024, 32, 0, 64)
    a_row = af(O_PQ + 1056, 32, 0, 1)
    btT = af(O_PQ + 1088, 128)
    wcr0 = af(O_PQ + 1216, 64)
    wcr1 = af(O_PQ + 1280, 64)
    d_row = af(O_RES, DM, 0, 1)
    dp1 = af(O_DP1, DM)

    ld0 = [(idf[:, :], idf_h[:, :]), (idb[:, :], idb_h[:, :]),
           (d_row, d_h[:, :]), (wbs_t, wbs_h[:, :]), (a_row, a_h[:, :]),
           (wcm_f, wcm_h[:, :]),
           (wbrT.rearrange("p (k r) -> p k r", r=64),
            wbr_h.rearrange("(k p) r -> p k r", p=128)),
           (wcr0, wcr_h[0:128, :]), (wcr1, wcr_h[128:256, :])]
    for dst, src in ld0:
        P.inc("sync", "L0", lambda e, d=dst, s=src: e.dma_start(d, s), 16)
    ld0_done = P.val("L0")

    P.do("vector", lambda e: e.memset(col(C_EPS), float(EPS)))
    P.do("vector", lambda e: e.memset(col(C_HALF), 0.5))
    for j, kk in enumerate(LADK):
        P.do("vector", lambda e, j=j, kk=kk: e.memset(col(C_K0 + j), -kk / 2.0))
    P.do("vector", lambda e: e.memset(ones1[:1, :], 1.0))
    P.do("vector", lambda e: e.memset(onesc[:, :], 1.0))
    P.do("vector", lambda e: e.memset(cols[:, C_RA0:C_RA0 + 3], 0.0))
    P.w("vector", "L0", ld0_done)
    dv_c0 = P.inc("vector", "DV", lambda e: (e.drain(), e.engine_nop())[1])

    P.w("tensor", "DV", dv_c0)
    pe_abc = P.inc("tensor", "PE", lambda e: e.matmul(
        p_b[0:128, 0:32], ones1[:1, 0:128], a_row, start=True, stop=True))
    P.w("vector", "PE", pe_abc)
    P.do("vector", lambda e: e.tensor_copy(a_bc, p_b[0:64, 0:32]))
    P.do("vector", lambda e: e.drain())
    P.do("vector", lambda e: e.tensor_copy(bb0, wbs_t))
    cur, nxt = bb0, bb1
    for lvl in range(8):
        wwid = 2 ** (7 - lvl)
        vi = cur.rearrange("p (b t w) -> p b t w", t=2, w=wwid)
        vo = nxt.rearrange("p (b t w) -> p b t w", t=2, w=wwid)
        vt = tmp256.rearrange("p (b t w) -> p b t w", t=2, w=wwid)
        for j in range(2):
            a0 = a_bc[:, lvl * 4 + j: lvl * 4 + j + 1]
            a1 = a_bc[:, lvl * 4 + 2 + j: lvl * 4 + 3 + j]
            P.do("vector", lambda e: e.drain())
            P.do("vector", lambda e, o=vt[:, :, j, :], i=vi[:, :, 0, :], s=a0:
                 e.tensor_single_scalar(o, i, s, ALU.mult))
            P.do("vector", lambda e: e.drain())
            P.do("vector", lambda e, o=vo[:, :, j, :], i=vi[:, :, 1, :], s=a1,
                 t=vt[:, :, j, :]:
                 e.scalar_tensor_tensor(o, i, s, t, ALU.mult, ALU.add))
        cur, nxt = nxt, cur
    dv_bfly = P.inc("vector", "DV", lambda e: (e.drain(), e.engine_nop())[1])

    P.w("tensor", "DV", dv_bfly)
    P.do("tensor", lambda e, i=cur: e.transpose(p_b[:, 0:64], i[:, 0:128],
                                                idf[0:64, 0:64]))
    pe_btt = P.inc("tensor", "PE", lambda e, i=cur: e.transpose(
        p_b[:, 64:128], i[:, 128:256], idf[0:64, 0:64]))
    P.w("vector", "PE", pe_btt)
    dv_btt = P.inc("vector", "DV", lambda e: e.tensor_copy(btT, p_b[:, 0:128]))
    P.w("tensor", "DV", dv_btt)
    P.do("tensor", lambda e: e.matmul(p_b[0:64, 256:320], btT[0:128, 0:64],
                                      wcr0, start=True, stop=False))
    pe_g = P.inc("tensor", "PE", lambda e: e.matmul(
        p_b[0:64, 256:320], btT[0:128, 64:128], wcr1, start=False, stop=True))
    P.w("vector", "PE", pe_g)
    dv_g = P.inc("vector", "DV", lambda e: e.tensor_copy(G_f, p_b[0:64, 256:320]))

    P.w("tensor", "DV", dv_g)
    dv_dpe = {}
    for i in range(4):
        if i >= 2:
            P.w("tensor", "DV", dv_dpe[i - 2])
        pe_d = P.inc("tensor", "PE", lambda e, i=i: e.matmul(
            p_a[:, (i % 2) * 512:(i % 2) * 512 + 512], ones1[:1, :],
            d_row[:, i * 512:(i + 1) * 512], start=True, stop=True))
        P.w("vector", "PE", pe_d)
        dv_dpe[i] = P.inc("vector", "DV", lambda e, i=i: e.tensor_scalar(
            dp1[:, i * 512:(i + 1) * 512],
            p_a[:, (i % 2) * 512:(i % 2) * 512 + 512],
            1.0, 1.0, ALU.mult, ALU.add))
    dv_dp1 = P.inc("vector", "DV", lambda e: (e.drain(), e.engine_nop())[1])

    ubuf = af(O_UB, DM)
    utb = af(O_UT, DM)
    res = af(O_RES, DM)
    x1b = af(O_X1, DM)
    lC = af(O_LC, DM)
    lY = af(O_LY, DM)
    sgt = ab(B_SG, DM)
    xt_bf = ab(B_OQ, DM)
    xi_all = B[:, B_XI:B_XI + NS * DM].rearrange("p (k t) -> p k t", t=TOK)
    pA_w = [af(O_GS, DM), af(O_GS + 2048, DM)]

    pa_chunks = []
    for mi, wh in ((0, gwt_h), (1, vwt_h)):
        fl = wh.rearrange("(c p) x -> c p x", p=128)
        for c in range(16):
            for s4 in range(4):
                pa_chunks.append((fl[c, :, s4 * 2048:(s4 + 1) * 2048], mi))
    flo = owt_h.rearrange("(c p) x -> c p x", p=128)
    for c in range(64):
        pa_chunks.append((flo[c, :, :], 2))
    NPA = len(pa_chunks)
    per = (NPA + NS - 1) // NS
    pa_sched = [list(range(s * per, min((s + 1) * per, NPA))) for s in range(NS)]

    la_ac = {}
    gp_add = {}
    abs_scr = [ab(B_GWA, DM), ab(B_GWA + 2048, DM)]

    def emit_passA(idxs):
        for j in idxs:
            src, mi = pa_chunks[j]
            bp = j % 2
            if j >= 2:
                P.w("sync", "AC", la_ac[j - 2])
            la = P.inc("sync", "LA%d" % bp,
                       lambda e, d=pA_w[bp], s=src: e.dma_start(d, s), 16)
            P.w("scalar", "LA%d" % bp, la)
            if j >= 2:
                P.w("scalar", "GP", gp_add[j - 2])
            acc = ccol(0, (C_SA0 if j % 2 == 0 else C_SB0) + mi)
            la_ac[j] = P.inc("scalar", "AC", lambda e, bp=bp, acc=acc, j=j:
                             e.activation(abs_scr[j % 2], pA_w[bp], AF.Abs,
                                          accum_out=acc))
            P.w("gpsimd", "AC", la_ac[j])
            gp_add[j] = P.inc("gpsimd", "GP", lambda e, mi=mi, acc=acc:
                              e.tensor_tensor(ccol(0, C_RA0 + mi),
                                              ccol(0, C_RA0 + mi), acc,
                                              ALU.add))
            P.do("gpsimd", lambda e: e.drain())

    st_x1 = {}
    dv_xi_done = {}
    dv_ures_s = {}
    pe_t1_s = {}
    dv_resadd = {}
    for s in range(NS):
        bp = s % 2
        if s >= 1:
            P.w("sync", "DV", dv_ures_s[s - 1])
        lu = P.inc("sync", "LU%d" % bp, lambda e, s=s: e.dma_start(
            ubuf, u_h[s * 128:(s + 1) * 128, :]), 16)
        if s >= 1:
            P.w("sync", "PE", pe_t1_s[s - 1])
        lt = P.inc("sync", "LT%d" % bp, lambda e, s=s: e.dma_start(
            utb.rearrange("p (k t) -> p k t", t=128),
            ut_h.rearrange("(k p) t -> p k t", p=128)
            [:, :, s * 128:(s + 1) * 128]), 16)
        P.w("tensor", "LT%d" % bp, lt)
        if s == 0:
            P.w("tensor", "DV", dv_dp1)
        for k in range(NK):
            fn = lambda e, k=k: e.matmul(
                p_b[0:64, 0:128], wbrT[:, k * 64:(k + 1) * 64],
                utb[:, k * 128:(k + 1) * 128], start=(k == 0), stop=(k == 15))
            if k == 15:
                pe_t1 = P.inc("tensor", "PE", fn)
            else:
                P.do("tensor", fn)
        pe_t1_s[s] = pe_t1
        P.w("vector", "PE", pe_t1)
        dv_t1 = P.inc("vector", "DV", lambda e: e.tensor_copy(
            t1b, p_b[0:64, 0:128]))
        P.w("tensor", "DV", dv_t1)
        pe_t2 = P.inc("tensor", "PE", lambda e: e.matmul(
            p_b[0:64, 512:640], G_f, t1b[0:64, :], start=True, stop=True))
        P.w("vector", "PE", pe_t2)
        dv_t2 = P.inc("vector", "DV", lambda e: e.tensor_copy(
            t2b, p_b[0:64, 512:640]))
        P.w("tensor", "DV", dv_t2)
        pe_y = {}
        for g in range(4):
            if s >= 1:
                P.w("tensor", "DV", dv_resadd[(s - 1, g)])
            pe_y[g] = P.inc("tensor", "PE", lambda e, g=g: e.matmul(
                p_a[:, g * 512:(g + 1) * 512], t2b[0:64, :],
                wcm_f[:, g * 512:(g + 1) * 512], start=True, stop=True))
        P.w("vector", "LU%d" % bp, lu)
        P.do("vector", lambda e: e.drain())
        P.do("vector", lambda e: e.tensor_tensor(res, ubuf, dp1, ALU.mult))
        dv_ures_s[s] = P.inc("vector", "DV",
                             lambda e: (e.drain(), e.engine_nop())[1])
        for g in range(4):
            P.w("vector", "PE", pe_y[g])
            P.do("vector", lambda e: e.drain())
            dv_resadd[(s, g)] = P.inc("vector", "DV", lambda e, g=g:
                                      e.tensor_tensor(
                                          res[:, g * 512:(g + 1) * 512],
                                          res[:, g * 512:(g + 1) * 512],
                                          p_a[:, g * 512:(g + 1) * 512],
                                          ALU.add))
        mcol, vcol, rcol = col(C_T0), col(C_T1), col(C_T3)
        P.do("vector", lambda e: e.drain())
        P.do("vector", lambda e: e.tensor_reduce(mcol, res, AX.X, ALU.add))
        P.do("vector", lambda e: e.drain())
        P.do("vector", lambda e: e.tensor_single_scalar(mcol, mcol, INV_D,
                                                        ALU.mult))
        P.do("vector", lambda e: e.drain())
        dv_ln = P.inc("vector", "DV", lambda e: e.tensor_single_scalar(
            res, res, mcol, ALU.subtract))
        P.w("scalar", "DV", dv_ln)
        P.do("scalar", lambda e: e.activation(sgt, res, AF.Square,
                                              accum_out=vcol))
        P.do("scalar", lambda e: e.drain())
        P.do("scalar", lambda e: e.activation(col(C_T2), vcol, AF.Ln,
                                              bias=col(C_EPS), scale=INV_D))
        P.do("scalar", lambda e: e.drain())
        ac_ln = P.inc("scalar", "AC", lambda e: e.activation(
            rcol, col(C_T2), AF.Exp, scale=-0.5))
        P.w("vector", "AC", ac_ln)
        if s >= 1:
            P.w("vector", "SX%d" % ((s - 1) % 2), st_x1[s - 1])
        P.do("vector", lambda e: e.drain())
        dv_x1 = P.inc("vector", "DV", lambda e: e.tensor_single_scalar(
            x1b, res, rcol, ALU.mult))
        P.w("sync", "DV", dv_x1)
        st_x1[s] = P.inc("sync", "SX%d" % bp, lambda e, s=s: e.dma_start(
            x1_scr[s, :, :], x1b), 16)
        gxc = gxs[:, s:s + 1]
        sxc = ccol(C_SXS, s)
        P.do("vector", lambda e: e.drain())
        P.do("vector", lambda e, g=gxc: e.tensor_reduce(
            g, x1b, AX.X, ALU.max, apply_absolute_value=True))
        P.do("vector", lambda e: e.drain())
        P.do("vector", lambda e, g=gxc: e.tensor_single_scalar(
            g, g, 1e-5, ALU.max))
        P.do("vector", lambda e: e.drain())
        P.do("vector", lambda e, g=gxc, s=s: e.tensor_single_scalar(
            ccol(C_GXI, s), g, 1.0 / 127.0, ALU.mult))
        P.do("vector", lambda e: e.drain())
        P.do("vector", lambda e, sc=sxc, s=s: e.reciprocal(
            sc, ccol(C_GXI, s)))
        P.do("vector", lambda e: e.drain())
        dv_prev = [0, 0]
        dv_prev[0] = P.inc("vector", "DV", lambda e, sc=sxc:
                           e.tensor_single_scalar(lY, x1b, sc, ALU.mult))
        dv_prev[1] = dv_prev[0]
        HSL = [slice(0, 1024), slice(1024, 2048)]
        for j in range(8):
            ac_j = [0, 0]
            for hh in range(2):
                src_c = (lY if j == 0 else lC)[:, HSL[hh]]
                P.w("scalar", "DV", dv_prev[hh])
                ac_j[hh] = P.inc("scalar", "AC", lambda e, src_c=src_c,
                                 hh=hh: e.activation(
                                     sgt[:, HSL[hh]], src_c, AF.Sign,
                                     bias=col(C_HALF)))
            for hh in range(2):
                src_c = (lY if j == 0 else lC)[:, HSL[hh]]
                P.w("vector", "AC", ac_j[hh])
                dv_prev[hh] = P.inc(
                    "vector", "DV", lambda e, j=j, src_c=src_c, hh=hh:
                    e.scalar_tensor_tensor(lC[:, HSL[hh]], sgt[:, HSL[hh]],
                                           col(C_K0 + j), src_c,
                                           ALU.mult, ALU.add))
        P.do("vector", lambda e: e.drain())
        P.do("vector", lambda e: e.tensor_tensor(res, lY, lC, ALU.subtract))
        P.do("vector", lambda e: e.drain())
        if s >= 1:
            P.w("vector", "PE", pe_xt_s)
        dv_xt = P.inc("vector", "DV", lambda e: e.tensor_scalar(
            xt_bf, res, 1.0, -0.5, ALU.mult, ALU.add))
        P.w("tensor", "DV", dv_xt)
        if s >= 1:
            P.w("tensor", "DV", dv_xi_done[s - 1])
        for k in range(16):
            fn = lambda e, k=k: e.transpose(
                p_tr[:, k * 128:(k + 1) * 128], xt_bf[:, k * 128:(k + 1) * 128],
                idb[:, :])
            if k == 15:
                pe_xt_s = P.inc("tensor", "PE", fn)
            else:
                P.do("tensor", fn)
        P.w("vector", "PE", pe_xt_s)
        ptr_view = p_tr[:, :].rearrange("p (k t) -> p k t", t=128)
        dv_xi_done[s] = P.inc(
            "vector", "DV",
            lambda e, s=s, o=xi_all[:, :, s * 128:(s + 1) * 128], i=ptr_view:
            e.tensor_copy(o, i))
        emit_passA(pa_sched[s])

    pe_after_p1 = P.val("PE")

    P.w("tensor", "GP", P.val("GP"))
    pe_g3 = {}
    for mi in range(3):
        pe_g3[mi] = P.inc("tensor", "PE", lambda e, mi=mi: e.matmul(
            p_b[0:1, 640 + mi:641 + mi], ccol(0, C_RA0 + mi), onesc[:, :],
            start=True, stop=True))
    P.w("vector", "PE", pe_g3[2])
    P.do("vector", lambda e: e.tensor_scalar(
        g11[:1, 0:3], p_b[0:1, 640:643], 1.0 / 16777216.0, 1e-5,
        ALU.mult, ALU.max))
    dv_g11 = P.inc("vector", "DV", lambda e: (e.drain(), e.engine_nop())[1])
    P.w("tensor", "DV", dv_g11)
    pe_gb = P.inc("tensor", "PE", lambda e: e.matmul(
        p_b[:, 644:647], ones1[:1, :], g11[:1, 0:3], start=True, stop=True))
    P.w("vector", "PE", pe_gb)
    for gi, (cth, csc, cis) in enumerate(
            [(C_SCG, C_SCG, C_ISG), (C_SCV, C_SCV, C_ISV), (C_SCO, C_SCO, -1)]):
        P.do("vector", lambda e, gi=gi, csc=csc: e.tensor_single_scalar(
            col(csc), p_b[:, 644 + gi:645 + gi], 0.5, ALU.mult))
        P.do("vector", lambda e: e.drain())
        if cis >= 0:
            P.do("vector", lambda e, csc=csc, cis=cis: e.reciprocal(
                col(cis), col(csc)))
            P.do("vector", lambda e: e.drain())
    TH = {}
    for mi, csc in ((0, C_SCG), (1, C_SCV), (2, C_SCO)):
        cp = C_SA0 + mi
        cn = C_SB0 + mi
        P.do("vector", lambda e, cp=cp, csc=csc: e.tensor_copy(
            col(cp), col(csc)))
        P.do("vector", lambda e: e.drain())
        P.do("vector", lambda e, cp=cp, cn=cn: e.tensor_single_scalar(
            col(cn), col(cp), -1.0, ALU.mult))
        P.do("vector", lambda e: e.drain())
        TH[mi] = (cp, cn)
    P.do("vector", lambda e: e.tensor_single_scalar(
        cols[:, C_SGC:C_SGC + 16], cols[:, C_GXI:C_GXI + 16], col(C_SCG),
        ALU.mult))
    P.do("vector", lambda e: e.drain())
    P.do("vector", lambda e: e.tensor_single_scalar(
        cols[:, C_SVC:C_SVC + 16], cols[:, C_GXI:C_GXI + 16], col(C_SCV),
        ALU.mult))
    P.do("vector", lambda e: e.drain())
    rows_g = lambda s: af(O_RES + s * 128, 128, 0, 1)
    rows_v = lambda s: af(O_X1 + s * 128, 128, 0, 1)
    P.w("vector", "SX0", st_x1[NS - 2])
    P.w("vector", "SX1", st_x1[NS - 1])
    for mi in range(2):
        base = C_SGC if mi == 0 else C_SVC
        rows = rows_g if mi == 0 else rows_v
        for s in range(NS):
            P.do("vector", lambda e, base=base, s=s: e.reciprocal(
                col(C_T0), ccol(base, s)))
            P.do("vector", lambda e: e.drain())
            dv_iv = P.inc("vector", "DV", lambda e: (e.drain(),
                                                     e.engine_nop())[1])
            P.w("tensor", "DV", dv_iv)
            pe_tv = P.inc("tensor", "PE", lambda e: e.transpose(
                p_b[0:1, 648:776], col(C_T0), idf[:, :]))
            P.w("vector", "PE", pe_tv)
            P.do("vector", lambda e: e.drain())
            P.inc("vector", "DV", lambda e, ro=rows(s): e.tensor_copy(
                ro, p_b[0:1, 648:776]))
    dv_rows = P.inc("vector", "DV", lambda e: (e.drain(), e.engine_nop())[1])

    wlb = [af(O_WL, DM), af(O_WL + 2048, DM)]
    pqb = af(O_PQ, DM)
    gs_slab = af(O_GS, 4096)
    sgh = [ab(B_SG, 1024), ab(B_SG + 1024, 1024)]
    oqb = ab(B_OQ, DM)
    gwq2 = [B[:, B_GWA:B_GWA + 8192], B[:, B_GWB:B_GWB + 8192]]
    hbuf2 = [af(O_UB, 512), af(O_UB + 512, 512)]
    brb = [af(O_BR, 512, 0, 1), af(O_BR + 512, 512, 0, 1)]
    brbD = [af(O_DP1, 512, 0, 1), af(O_DP1 + 512, 512, 0, 1)]
    P.do("vector", lambda e: e.memset(cols[:, C_MAC:C_MAC + 16], 0.0))
    dv_mz = P.inc("vector", "DV", lambda e: (e.drain(), e.engine_nop())[1])

    pq_ac = {}
    pq_st = {}
    chain = {"dv": 0}

    def emit_prequant(jcs):
        for jc in jcs:
            if jc >= NJC:
                continue
            if jc >= 1:
                P.w("sync", "AC", pq_ac[jc - 1])
            lq = P.inc("sync", "LQ%d" % (jc % 2), lambda e, jc=jc:
                       e.dma_start(pqb, flo[jc, :, :]), 16)
            P.w("scalar", "LQ%d" % (jc % 2), lq)
            if chain["dv"]:
                P.w("scalar", "DV", chain["dv"])
            P.do("scalar", lambda e: e.activation(
                ab(B_SG, DM), pqb, AF.Sign, bias=col(C_SB0 + 2)))
            if jc >= 1:
                P.w("scalar", "SQ%d" % ((jc - 1) % 2), pq_st[jc - 1])
            acq = P.inc("scalar", "AC", lambda e: e.activation(
                oqb, pqb, AF.Sign, bias=col(C_SA0 + 2)))
            P.w("vector", "AC", acq)
            dvq = P.inc("vector", "DV", lambda e: e.tensor_tensor(
                oqb, ab(B_SG, DM), oqb, ALU.add))
            chain["dv"] = dvq
            pq_ac[jc] = acq
            P.w("sync", "DV", dvq)
            pq_st[jc] = P.inc("sync", "SQ%d" % (jc % 2),
                              lambda e, jc=jc: e.dma_start(oq_scr[jc, :, :],
                                                           oqb), 16)

    P.w("scalar", "DV", dv_rows)
    P.w("sync", "DV", dv_rows)
    P.w("sync", "PE", pe_after_p1)
    gwt_r = gwt_h.rearrange("(k p) x -> p k x", p=128)
    vwt_r = vwt_h.rearrange("(k p) x -> p k x", p=128)
    st_h = {}
    lw_ac = {}
    pe_slab_free = {}
    pe_burst = {}
    dv_quant = {}
    epi_done = {}
    lr_hist = {}
    pq_next = 0
    jq_box = [0]
    tc = 0
    sg0q = ab(B_SG, DM)
    sg1q = ab(B_OQ, DM)

    def quantize(hg, mat, gate_pe):
        wr = gwt_r if mat == 0 else vwt_r
        cp, cn = TH[mat]
        wsl = gwq2[mat]
        first = True
        for c in range(4):
            jq = jq_box[0]
            bp = jq % 2
            if jq >= 2:
                P.w("sync", "AC", lw_ac[jq - 2])
            lw = P.inc("sync", "LW%d" % bp, lambda e, wr=wr, c=c, hg=hg,
                       bp=bp: e.dma_start(
                           wlb[bp].rearrange("p (k x) -> p k x", x=512),
                           wr[:, c * 4:c * 4 + 4,
                              hg * 512:(hg + 1) * 512]), 16)
            P.w("scalar", "LW%d" % bp, lw)
            if chain["dv"]:
                P.w("scalar", "DV", chain["dv"])
            if first:
                P.w("scalar", "SQ0", P.val("SQ0"))
                P.w("scalar", "SQ1", P.val("SQ1"))
            P.do("scalar", lambda e, bp=bp, cn=cn: e.activation(
                sg0q, wlb[bp], AF.Sign, bias=col(cn)))
            acs = P.inc("scalar", "AC", lambda e, bp=bp, cp=cp: e.activation(
                sg1q, wlb[bp], AF.Sign, bias=col(cp)))
            P.w("vector", "AC", acs)
            if first and gate_pe is not None:
                P.w("vector", "PE", gate_pe)
            dvs2 = P.inc("vector", "DV", lambda e, c=c, wsl=wsl:
                         e.tensor_tensor(wsl[:, c * 2048:(c + 1) * 2048],
                                         sg0q, sg1q, ALU.add))
            chain["dv"] = dvs2
            lw_ac[jq] = acs
            jq_box[0] = jq + 1
            first = False
        dv_quant[(hg, mat)] = chain["dv"]

    P.w("scalar", "PE", pe_after_p1)
    quantize(0, 0, None)
    quantize(0, 1, None)
    for hg in range(NHG):
        if hg == 0:
            for h2 in (0, 1):
                bpair = brb if h2 % 2 == 0 else brbD
                if h2 == 1:
                    P.w("sync", "DV", dv_xi_done[NS - 1])
                lrg = P.inc("sync", "LR%d" % (2 * (h2 % 2)),
                            lambda e, h2=h2, b=bpair[0]:
                            e.dma_start(b, gbr_h[:, h2 * 512:
                                                  (h2 + 1) * 512]), 16)
                lrv = P.inc("sync", "LR%d" % (1 + 2 * (h2 % 2)),
                            lambda e, h2=h2, b=bpair[1]:
                            e.dma_start(b, vbr_h[:, h2 * 512:
                                                  (h2 + 1) * 512]), 16)
                lr_hist[h2] = (lrg, lrv)
        for half in range(2):
            for mat in range(2):
                wsl = gwq2[mat]
                P.w("tensor", "DV", dv_quant[(hg, mat)])
                if hg == 0 and half == 0 and mat == 0:
                    P.w("tensor", "DV", dv_xi_done[NS - 1])
                    P.w("tensor", "DV", dv_mz)
                P.w("tensor", "LR%d" % (mat + 2 * (hg % 2)),
                    lr_hist[hg][mat])
                for i8 in range(8):
                    t = half * 8 + i8
                    bank = tc % 6
                    psl = (p_a[:, bank * 512:(bank + 1) * 512] if bank < 4
                           else p_b[:, (bank - 4) * 512:(bank - 3) * 512])
                    if tc >= 6:
                        kind, val = epi_done[tc - 6]
                        P.w("tensor", kind, val)
                    ro = (rows_g if mat == 0 else rows_v)(t)
                    bpr = (brb if hg % 2 == 0 else brbD)[mat]
                    P.do("tensor", lambda e, psl=psl, ro=ro, b=bpr:
                         e.matmul(psl, ro, b, start=True, stop=False))
                    for k in range(NK):
                        fn = lambda e, k=k, t=t, psl=psl, wsl=wsl: e.matmul(
                            psl,
                            B[:, B_XI + k * 2048 + t * 128:
                               B_XI + k * 2048 + t * 128 + 128],
                            wsl[:, k * 512:(k + 1) * 512],
                            start=False, stop=(k == 15))
                        if k == 15:
                            pe_mm = P.inc("tensor", "PE", fn)
                        else:
                            P.do("tensor", fn)
                    slot = t % 8
                    if mat == 0:
                        P.w("scalar", "PE", pe_mm)
                        if hg > 0 or half > 0:
                            back = 8 if half == 1 else 16
                            P.w("scalar", "DV", epi_done[tc - back][1])
                        ac_e = P.inc(
                            "scalar", "AC", lambda e, slot=slot, psl=psl,
                            t=t: e.activation(
                                gs_slab[:, slot * 512:(slot + 1) * 512],
                                psl, AF.Silu, scale=ccol(C_SGC, t)))
                        epi_done[tc] = ("AC", ac_e)
                    else:
                        P.w("vector", "PE", pe_mm)
                        hp = t % 2
                        nvt = hg * 16 + t
                        if nvt >= 2:
                            P.w("vector", "SH%d" % hp,
                                st_h[((nvt - 2) // 16, (nvt - 2) % 16)])
                        P.w("vector", "AC", epi_done[tc - 8][1])
                        P.do("vector", lambda e: e.drain())
                        P.do("vector", lambda e, slot=slot, psl=psl, hp=hp,
                             t=t: e.scalar_tensor_tensor(
                                 hbuf2[hp], psl, ccol(C_SVC, t),
                                 gs_slab[:, slot * 512:(slot + 1) * 512],
                                 ALU.mult, ALU.mult))
                        P.do("vector", lambda e: e.drain())
                        P.do("vector", lambda e, hp=hp: e.tensor_reduce(
                            col(C_T0), hbuf2[hp], AX.X, ALU.max,
                            apply_absolute_value=True))
                        P.do("vector", lambda e: e.drain())
                        dv_h = P.inc("vector", "DV", lambda e, t=t:
                                     e.tensor_tensor(ccol(C_MAC, t),
                                                     ccol(C_MAC, t),
                                                     col(C_T0), ALU.max))
                        epi_done[tc] = ("DV", dv_h)
                        P.w("gpsimd", "DV", dv_h)
                        st_h[(hg, t)] = P.inc(
                            "gpsimd", "SH%d" % hp,
                            lambda e, t=t, hg=hg, hp=hp: e.dma_start(
                                h_scr[t, :, hg * 512:(hg + 1) * 512],
                                hbuf2[hp]), 16)
                    tc += 1
                pe_burst[(hg, half, mat)] = P.val("PE")
                if half == 1 and mat == 0 and hg + 1 < NHG:
                    quantize(hg + 1, 0, pe_burst[(hg, 1, 0)])
        if hg + 1 < NHG:
            quantize(hg + 1, 1, pe_burst[(hg, 1, 1)])
        if hg + 2 < NHG:
            h2 = hg + 2
            bpair = brb if h2 % 2 == 0 else brbD
            P.w("sync", "PE", pe_burst[(hg, 1, 1)])
            lrg = P.inc("sync", "LR%d" % (2 * (h2 % 2)),
                        lambda e, h2=h2, b=bpair[0]:
                        e.dma_start(b, gbr_h[:, h2 * 512:(h2 + 1) * 512]), 16)
            lrv = P.inc("sync", "LR%d" % (1 + 2 * (h2 % 2)),
                        lambda e, h2=h2, b=bpair[1]:
                        e.dma_start(b, vbr_h[:, h2 * 512:(h2 + 1) * 512]), 16)
            lr_hist[h2] = (lrg, lrv)
        emit_prequant(list(range(pq_next, pq_next + 4)))
        pq_next += 4

    P.w("vector", "AC", pq_ac[NJC - 1])
    P.do("vector", lambda e: e.drain())
    P.do("vector", lambda e: e.tensor_scalar(
        cols[:, C_MAC:C_MAC + 16], cols[:, C_MAC:C_MAC + 16], 1.0, 1e-5,
        ALU.mult, ALU.max))
    P.do("vector", lambda e: e.drain())
    P.do("vector", lambda e: e.tensor_scalar(
        cols[:, C_S2:C_S2 + 16], cols[:, C_MAC:C_MAC + 16], 1.0 / 127.0, 0.0,
        ALU.mult, ALU.add))
    P.do("vector", lambda e: e.drain())
    P.do("vector", lambda e: e.reciprocal(
        cols[:, C_MAC:C_MAC + 16], cols[:, C_S2:C_S2 + 16]))
    P.do("vector", lambda e: e.drain())
    P.do("vector", lambda e: e.tensor_single_scalar(
        cols[:, C_S2:C_S2 + 16], cols[:, C_S2:C_S2 + 16], col(C_SCO),
        ALU.mult))
    P.do("vector", lambda e: e.drain())
    rows_o = lambda s: af(O_PQ + s * 128, 128, 0, 1)
    for s in range(NS):
        P.do("vector", lambda e, s=s: e.reciprocal(col(C_T0), ccol(C_S2, s)))
        P.do("vector", lambda e: e.drain())
        dv_iv = P.inc("vector", "DV", lambda e: (e.drain(),
                                                 e.engine_nop())[1])
        P.w("tensor", "DV", dv_iv)
        if s == 0:
            P.w("tensor", "PE", pe_burst[(NHG - 1, 1, 1)])
        pe_tv = P.inc("tensor", "PE", lambda e: e.transpose(
            p_b[0:1, 648:776], col(C_T0), idf[:, :]))
        P.w("vector", "PE", pe_tv)
        P.do("vector", lambda e: e.drain())
        P.inc("vector", "DV", lambda e, ro=rows_o(s): e.tensor_copy(
            ro, p_b[0:1, 648:776]))
    dv_orows = P.inc("vector", "DV", lambda e: (e.drain(), e.engine_nop())[1])

    NE = 8
    hbq = [af(O_UB, DM), af(O_UT, DM)]
    lCc = [af(O_LC, DM), af(O_WL, DM)]
    lYc = [af(O_LY, DM), af(O_WL + 2048, DM)]
    sgc = [ab(B_SG, DM), ab(B_OQ, DM)]
    xtmp = af(O_RES, DM)
    x1r2 = [af(O_GS, DM), af(O_GS + 2048, DM)]
    z2 = [af(O_DP1, DM), af(O_X1, DM)]
    bro = af(O_BR, 1024, 0, 1)
    owqb = [B[:, B_GWA + i * 4096:B_GWA + (i + 1) * 4096] for i in range(2)]
    oq_r = oq_scr.rearrange("j p d -> p j d")
    hitH = [B[:, B_XI + i * 16384:B_XI + (i + 1) * 16384] for i in range(2)]
    hitH_v = [hitH[i].rearrange("p (jc t) -> p jc t", t=256) for i in range(2)]
    sq2 = ab(B_SG, DM)

    dv_lad = {}
    dv_evac = {}
    dv_evac_all = {}
    dv_z_round = {}
    st_out = {}
    dv_ln2 = {}
    lh_vals = {}
    lb_vals = {}
    lr_vals = {}
    lx_vals = {}
    tr_emitters = {}

    def emit_lh(e, p):
        pc = e * 8 + p
        c, ch = p // 2, p % 2
        st = 2 * e + ch
        bp = pc % 2
        if pc >= 2:
            P.w("sync", "PL", pc - 1)
        if pc == 0:
            P.w("sync", "SH0", P.val("SH0"))
            P.w("sync", "SH1", P.val("SH1"))
        lh_vals[pc] = P.inc("sync", "LH%d" % bp, lambda e_, st=st, c=c,
                            bp=bp: e_.dma_start(
                                hbq[bp],
                                h_scr[st, :, c * 2048:(c + 1) * 2048]), 16)

    def emit_p3_sync(e):
        em = e - 1
        if em >= 0:
            nd0 = em * 2
            if nd0 >= 1:
                P.w("sync", "PG", (nd0 - 1) * 16 + 1)
            lr_vals[nd0] = P.inc("sync", "LR%d" % (nd0 % 2), lambda e_:
                                 e_.dma_start(bro, obr_h[:, 0:1024]), 16)
            for t2 in range(2):
                st = 2 * em + t2
                if em > 0:
                    P.w("sync", "DV", dv_ln2[(em - 1, t2)])
                elif t2 == 0:
                    P.w("sync", "SX0", P.val("SX0"))
                    P.w("sync", "SX1", P.val("SX1"))
                lx_vals[(em, t2)] = P.inc(
                    "sync", "LX%d" % t2, lambda e_, st=st, t2=t2:
                    e_.dma_start(x1r2[t2], x1_scr[st, :, :]), 16)
            for g in range(16):
                nload = em * 32 + g
                if nload >= 2:
                    P.w("sync", "PG", nload - 1)
                if em == 0 and g == 0:
                    P.w("sync", "SQ0", P.val("SQ0"))
                    P.w("sync", "SQ1", P.val("SQ1"))
                lb_vals[(em, 0, g)] = P.inc(
                    "sync", "LB%d" % (g % 2), lambda e_, g=g: e_.dma_start(
                        owqb[g % 2].rearrange("p (j d) -> p j d", d=1024),
                        oq_r[:, g * 4:g * 4 + 4, 0:1024]), 16)
                if e < NE and 1 <= g <= 8:
                    emit_lh(e, g - 1)
            nd1 = em * 2 + 1
            P.w("sync", "PG", (nd1 - 1) * 16 + 1)
            lr_vals[nd1] = P.inc("sync", "LR%d" % (nd1 % 2), lambda e_:
                                 e_.dma_start(bro, obr_h[:, 1024:2048]), 16)
            for g in range(16):
                nload = em * 32 + 16 + g
                P.w("sync", "PG", nload - 1)
                lb_vals[(em, 1, g)] = P.inc(
                    "sync", "LB%d" % (g % 2), lambda e_, g=g: e_.dma_start(
                        owqb[g % 2].rearrange("p (j d) -> p j d", d=1024),
                        oq_r[:, g * 4:g * 4 + 4, 1024:2048]), 16)
        else:
            for p in range(8):
                emit_lh(e, p)

    def emit_ladder(e):
        hhE = e % 2
        emitters = []
        for c in range(4):
            pcs = [e * 8 + c * 2, e * 8 + c * 2 + 1]
            for ch in range(2):
                pc = pcs[ch]
                st = 2 * e + ch
                bp = pc % 2
                P.w("vector", "LH%d" % bp, lh_vals[pc])
                if pc == 0:
                    P.w("vector", "DV", dv_orows)
                    P.w("vector", "SQ0", P.val("SQ0"))
                    P.w("vector", "SQ1", P.val("SQ1"))
                r2c = ccol(C_MAC, st)
                P.do("vector", lambda e_: e_.drain())
                P.inc("vector", "PL", lambda e_, bp=bp, r2c=r2c, lY=lYc[ch]:
                      e_.tensor_single_scalar(lY, hbq[bp], r2c, ALU.mult))
            dv_prev = [[0, 0], [0, 0]]
            HSL = [slice(0, 1024), slice(1024, 2048)]
            for j in range(8):
                ac_j = [[0, 0], [0, 0]]
                for ch in range(2):
                    pc = pcs[ch]
                    for hh in range(2):
                        src_c = (lYc[ch] if j == 0 else lCc[ch])[:, HSL[hh]]
                        if j == 0 and hh == 0:
                            P.w("scalar", "PL", pc + 1)
                            if pc >= 2:
                                P.w("scalar", "PT", pc - 1)
                        elif j > 0:
                            P.w("scalar", "DV", dv_prev[ch][hh])
                        ac_j[ch][hh] = P.inc(
                            "scalar", "AC",
                            lambda e_, src_c=src_c, sg=sgc[ch], hh=hh:
                            e_.activation(sg[:, HSL[hh]], src_c, AF.Sign,
                                          bias=col(C_HALF)))
                for ch in range(2):
                    for hh in range(2):
                        src_c = (lYc[ch] if j == 0 else lCc[ch])[:, HSL[hh]]
                        P.w("vector", "AC", ac_j[ch][hh])
                        dv_prev[ch][hh] = P.inc(
                            "vector", "DV",
                            lambda e_, j=j, src_c=src_c, sg=sgc[ch],
                            lC=lCc[ch], hh=hh:
                            e_.scalar_tensor_tensor(
                                lC[:, HSL[hh]], sg[:, HSL[hh]],
                                col(C_K0 + j), src_c, ALU.mult, ALU.add))
            for ch in range(2):
                pc = pcs[ch]
                c_, ch_ = c, ch
                P.do("vector", lambda e_: e_.drain())
                P.do("vector", lambda e_, lY=lYc[ch], lC=lCc[ch]:
                     e_.tensor_tensor(xtmp, lY, lC, ALU.subtract))
                P.do("vector", lambda e_: e_.drain())
                dv_lad[pc] = P.inc("vector", "DV", lambda e_, sg=sgc[ch]:
                                   e_.tensor_scalar(sg, xtmp, 1.0, -0.5,
                                                    ALU.mult, ALU.add))

                def tr_emit(pc=pc, sg=sgc[ch]):
                    P.w("tensor", "DV", dv_lad[pc])
                    if pc >= 1:
                        P.w("tensor", "DV", dv_evac[pc - 1])
                    for kk in range(16):
                        fn = lambda e_, kk=kk, sg=sg: e_.transpose(
                            p_tr[:, kk * 128:(kk + 1) * 128],
                            sg[:, kk * 128:(kk + 1) * 128], idb[:, :])
                        if kk == 15:
                            P.inc("tensor", "PT", fn)
                        else:
                            P.do("tensor", fn)
                emitters.append(tr_emit)
                P.w("vector", "PT", pc + 1)
                if e >= 2 and pc == e * 8:
                    P.w("vector", "PG", (e - 1) * 32)
                ptr_view = p_tr[:, :].rearrange("p (k t) -> p k t", t=128)
                dv_evac[pc] = P.inc(
                    "vector", "DV",
                    lambda e_, c=c_, ch=ch_, hh=hhE, i=ptr_view:
                    e_.tensor_copy(hitH_v[hh][:, c * 16:(c + 1) * 16,
                                              ch * 128:(ch + 1) * 128], i))
        dv_evac_all[e] = P.val("DV")
        tr_emitters[e] = emitters

    def emit_mm(em, emitters):
        hh = em % 2
        spl = 0
        for r in range(2):
            nd = em * 2 + r
            for grp in range(16):
                gbp = grp % 2
                if r == 0 and grp % 2 == 1 and spl < len(emitters):
                    emitters[spl]()
                    spl += 1
                P.w("tensor", "LB%d" % gbp, lb_vals[(em, r, grp)])
                if grp == 0:
                    P.w("tensor", "LR%d" % (nd % 2), lr_vals[nd])
                    if r == 0:
                        P.w("tensor", "DV", dv_evac_all[em])
                    if r >= 1:
                        P.w("tensor", "DV", dv_z_round[(em, 0)])
                    elif em > 0:
                        P.w("tensor", "DV", dv_z_round[(em - 1, 1)])
                    for t2 in range(2):
                        st = 2 * em + t2
                        for di in range(2):
                            bank = t2 * 2 + di
                            P.do("tensor", lambda e_, bank=bank, st=st,
                                 di=di: e_.matmul(
                                     p_a[:, bank * 512:(bank + 1) * 512],
                                     rows_o(st),
                                     bro[:, di * 512:(di + 1) * 512],
                                     start=True, stop=False))
                for jj in range(4):
                    jc = grp * 4 + jj
                    for t2 in range(2):
                        for di in range(2):
                            bank = t2 * 2 + di
                            fn = lambda e_, jc=jc, t2=t2, di=di, bank=bank, \
                                gbp=gbp, jj=jj, hh=hh: e_.matmul(
                                    p_a[:, bank * 512:(bank + 1) * 512],
                                    hitH[hh][:, jc * 256 + t2 * 128:
                                             jc * 256 + t2 * 128 + 128],
                                    owqb[gbp][:, jj * 1024 + di * 512:
                                              jj * 1024 + di * 512 + 512],
                                    start=False, stop=(jc == NJC - 1))
                            if jj == 3 and t2 == 1 and di == 1:
                                P.inc("tensor", "PG", fn)
                            else:
                                P.do("tensor", fn)
            for t2 in range(2):
                st = 2 * em + t2
                for di in range(2):
                    dmg = r * 2 + di
                    bank = t2 * 2 + di
                    P.w("vector", "PG", nd * 16 + 16)
                    if r == 0 and di == 0:
                        P.w("vector", "LX%d" % t2, lx_vals[(em, t2)])
                        if em > 0:
                            P.w("vector", "SO%d" % ((2 * (em - 1) + t2) % 2),
                                st_out[2 * (em - 1) + t2])
                    s2c = ccol(C_S2, st)
                    P.do("vector", lambda e_: e_.drain())
                    dv_z = P.inc("vector", "DV",
                                 lambda e_, t2=t2, dmg=dmg, s2c=s2c,
                                 bank=bank:
                                 e_.scalar_tensor_tensor(
                                     z2[t2][:, dmg * 512:(dmg + 1) * 512],
                                     p_a[:, bank * 512:(bank + 1) * 512],
                                     s2c,
                                     x1r2[t2][:, dmg * 512:(dmg + 1) * 512],
                                     ALU.mult, ALU.add))
            dv_z_round[(em, r)] = dv_z
        for t2 in range(2):
            st = 2 * em + t2
            P.w("vector", "DV", dv_z_round[(em, 1)])
            zt = z2[t2]
            mcol, vcol, rcol = col(C_T0), col(C_T1), col(C_T3)
            P.do("vector", lambda e_: e_.drain())
            P.do("vector", lambda e_, zt=zt: e_.tensor_reduce(mcol, zt, AX.X,
                                                              ALU.add))
            P.do("vector", lambda e_: e_.drain())
            P.do("vector", lambda e_: e_.tensor_single_scalar(mcol, mcol,
                                                              INV_D,
                                                              ALU.mult))
            P.do("vector", lambda e_: e_.drain())
            dv_l2 = P.inc("vector", "DV", lambda e_, zt=zt:
                          e_.tensor_single_scalar(zt, zt, mcol,
                                                  ALU.subtract))
            P.w("scalar", "DV", dv_l2)
            P.do("scalar", lambda e_, zt=zt: e_.activation(sq2, zt, AF.Square,
                                                           accum_out=vcol))
            P.do("scalar", lambda e_: e_.drain())
            P.do("scalar", lambda e_: e_.activation(col(C_T2), vcol, AF.Ln,
                                                    bias=col(C_EPS),
                                                    scale=INV_D))
            P.do("scalar", lambda e_: e_.drain())
            ac_l2 = P.inc("scalar", "AC", lambda e_: e_.activation(
                rcol, col(C_T2), AF.Exp, scale=-0.5))
            P.w("vector", "AC", ac_l2)
            P.do("vector", lambda e_: e_.drain())
            dv_o = P.inc("vector", "DV", lambda e_, zt=zt:
                         e_.tensor_single_scalar(zt, zt, rcol, ALU.mult))
            dv_ln2[(em, t2)] = dv_o
            P.w("sync", "DV", dv_o)
            st_out[st] = P.inc("sync", "SO%d" % (st % 2),
                               lambda e_, st=st, zt=zt: e_.dma_start(
                                   out_h[st * 128:(st + 1) * 128, :], zt), 16)

    P.w("sync", "PE", pe_burst[(NHG - 1, 1, 1)])
    for e in range(NE + 1):
        emit_p3_sync(e)
        if e < NE:
            emit_ladder(e)
        if e == 0:
            for em_ in tr_emitters[0]:
                em_()
            tr_emitters[0] = []
        if e >= 1:
            emit_mm(e - 1, tr_emitters[e] if e < NE else [])

    P.w("sync", "SO0", P.val("SO0"))
    P.w("sync", "SO1", P.val("SO1"))

    with nc.Block() as block:
        @block.sync
        def _(eng):
            for f in P.ops["sync"]:
                f(eng)

        @block.tensor
        def _(eng):
            for f in P.ops["tensor"]:
                f(eng)

        @block.vector
        def _(eng):
            for f in P.ops["vector"]:
                f(eng)

        @block.scalar
        def _(eng):
            for f in P.ops["scalar"]:
                f(eng)

        @block.gpsimd
        def _(eng):
            for f in P.ops["gpsimd"]:
                f(eng)

    ctx.close()
    return nc


_NC_CACHE = None


def kernel(L0_attn_u, L0_attn_A_factors, L0_attn_W_b_rank, L0_attn_W_b_state,
           L0_attn_W_c_rank, L0_attn_W_c_model, L0_attn_D_vec,
           L0_ffn_gate_w, L0_ffn_gate_b, L0_ffn_val_w, L0_ffn_val_b,
           L0_ffn_out_w, L0_ffn_out_b):
    global _NC_CACHE
    u = np.ascontiguousarray(np.asarray(L0_attn_u, dtype=np.float32).reshape(-1, DM))
    gw = np.asarray(L0_ffn_gate_w, dtype=np.float32)
    vw = np.asarray(L0_ffn_val_w, dtype=np.float32)
    ow = np.asarray(L0_ffn_out_w, dtype=np.float32)
    shared = dict(
        gwt=np.ascontiguousarray(gw.T),
        vwt=np.ascontiguousarray(vw.T),
        owt=np.ascontiguousarray(ow.T),
        gbr=np.ascontiguousarray(np.asarray(L0_ffn_gate_b,
                                            dtype=np.float32).reshape(1, HID)),
        vbr=np.ascontiguousarray(np.asarray(L0_ffn_val_b,
                                            dtype=np.float32).reshape(1, HID)),
        obr=np.ascontiguousarray(np.asarray(L0_ffn_out_b,
                                            dtype=np.float32).reshape(1, DM)),
        wbr=np.ascontiguousarray(np.asarray(L0_attn_W_b_rank, dtype=np.float32)),
        wbs=np.ascontiguousarray(np.asarray(L0_attn_W_b_state, dtype=np.float32)),
        wcr=np.ascontiguousarray(np.asarray(L0_attn_W_c_rank, dtype=np.float32)),
        wcm=np.ascontiguousarray(np.asarray(L0_attn_W_c_model, dtype=np.float32)),
        afac=np.ascontiguousarray(
            np.asarray(L0_attn_A_factors, dtype=np.float32).reshape(1, 32)),
        dvec=np.ascontiguousarray(
            np.asarray(L0_attn_D_vec, dtype=np.float32).reshape(1, DM)),
        idf=np.eye(128, dtype=np.float32),
        idb=np.eye(128, dtype=np.float32).astype(BF),
    )
    if _NC_CACHE is None:
        _NC_CACHE = build_program()
    in_maps = []
    for c in range(NCORE):
        uc = np.ascontiguousarray(u[c * TOK:(c + 1) * TOK, :])
        in_maps.append(dict(u=uc, ut=np.ascontiguousarray(uc.T), **shared))
    res = run_bass_kernel_spmd(_NC_CACHE, in_maps, core_ids=list(range(NCORE)))
    out = np.concatenate(
        [np.asarray(r["out"], dtype=np.float32) for r in res.results], axis=0)
    return out.reshape(4, 4096, DM)
